# version 17
# baseline (speedup 1.0000x reference)
import numpy as np
import ml_dtypes
from contextlib import ExitStack

import concourse.bass as bass
import concourse.mybir as mybir
from concourse.bass_utils import run_bass_kernel_spmd

F32 = mybir.dt.float32
BF16 = mybir.dt.bfloat16
AF = mybir.ActivationFunctionType
ALU = mybir.AluOpType
AX = mybir.AxisListType
BF = ml_dtypes.bfloat16

HID = 4096
NH = 32
NE = 28
RK = 8
NKC = 32
NOG = 8
TH_RAW = float(np.float32(0.001) * np.float32(np.sqrt(np.float32(128.0))))
INV_SQRT_D = float(np.float32(1.0) / np.float32(np.sqrt(np.float32(128.0))))
EPS = 1e-5
BIGNEG = -1.0e30


class Sem:
    def __init__(self, es, nc, name):
        self.h = es.enter_context(nc.semaphore(name))
        self.n = 0

    def inc(self, instr, k=1):
        instr.then_inc(self.h, k)
        self.n += k
        return self.n


def build_program(ntok, T, dbg=False):
    nc = bass.Bass("TRN2", target_bir_lowering=False)
    NT = ntok // T
    NM = T // 128
    NB = T // 16

    x_d = nc.dram_tensor("x", [ntok, HID], F32, kind="ExternalInput").ap()
    y_d = nc.dram_tensor("y", [ntok, HID], F32, kind="ExternalOutput").ap()
    wq_d = nc.dram_tensor("wq", [NKC, 128, NKC, 128], F32, kind="ExternalInput").ap()
    wk_d = nc.dram_tensor("wk", [NKC, 128, NKC, 128], F32, kind="ExternalInput").ap()
    wv_d = nc.dram_tensor("wv", [NKC, 128, NKC, 128], F32, kind="ExternalInput").ap()
    wo_d = nc.dram_tensor("wo", [NOG, 128, NKC, 512], F32, kind="ExternalInput").ap()
    wb_d = nc.dram_tensor("wb", [NOG, 128, NKC, 512], F32, kind="ExternalInput").ap()
    a2_d = nc.dram_tensor("a2", [NKC, 128, 224], F32, kind="ExternalInput").ap()
    b20_d = nc.dram_tensor("b20", [NOG, 128, 512], F32, kind="ExternalInput").ap()
    b21_d = nc.dram_tensor("b21", [NOG, 96, 512], F32, kind="ExternalInput").ap()
    gw_d = nc.dram_tensor("gw", [128, NKC, NE], F32, kind="ExternalInput").ap()
    bq_d = nc.dram_tensor("bq", [128, NKC], F32, kind="ExternalInput").ap()
    bk_d = nc.dram_tensor("bk", [128, NKC], F32, kind="ExternalInput").ap()
    bv_d = nc.dram_tensor("bv", [128, NKC], F32, kind="ExternalInput").ap()
    bo_d = nc.dram_tensor("bo", [1, HID], F32, kind="ExternalInput").ap()
    idq_d = nc.dram_tensor("idq", [128, 128], BF16, kind="ExternalInput").ap()
    ones_d = nc.dram_tensor("onesc", [128, 128], BF16, kind="ExternalInput").ap()
    e2_d = nc.dram_tensor("e2", [NE, 224], BF16, kind="ExternalInput").ap()
    thb_d = nc.dram_tensor("thb", [128, 512], BF16, kind="ExternalInput").ap()
    qkv_w_d = [wq_d, wk_d, wv_d]
    dbg_d = {}
    if dbg:
        for nm, shp in [("xn", [128, NKC * T]), ("q", [128, T * 32]),
                        ("k", [128, T * 32]), ("v", [128, T * 32]),
                        ("ao", [128, T * 32]), ("gf", [128, 4 * NE]),
                        ("hs0", [128, T]), ("hs1", [96, T]), ("me", [128, 512]),
                        ("lnd", [128, 8]), ("mrow", [1, 128]),
                        ("lg", [128, 4 * NE]), ("t2d", [128, 8]),
                        ("xn2", [128, NKC * T]), ("gw0", [128, 2 * NE])]:
            dbg_d[nm] = nc.dram_tensor("d_" + nm, shp, F32, kind="ExternalOutput").ap()

    es = ExitStack()

    def sbt(name, shape, dt):
        return es.enter_context(nc.sbuf_tensor(name, shape, dt))

    Xsb = sbt("Xsb", [128, NKC, T], BF16)
    Qsb = sbt("Qsb", [128, T * 32], BF16)
    Ksb = sbt("Ksb", [128, T * 32], BF16)
    VAsb = sbt("VAsb", [128, T * 32], BF16)
    xbf = [sbt(f"xbf{i}", [128, HID], BF16) for i in range(2)]
    WF_R, OB_R = 3, 3
    wbb = [sbt(f"wbb{i}", [128, 2048], BF16) for i in range(WF_R)]
    obf = [sbt(f"obf{i}", [128, 4, 512], BF16) for i in range(OB_R)]
    a2b = [sbt(f"a2b{i}", [128, 224], BF16) for i in range(6)]
    b2b0 = sbt("b2b0", [128, 512], BF16)
    b2b1 = sbt("b2b1", [96, 512], BF16)
    gwr = sbt("gwr", [128, NKC, NE], BF16)
    bo_bf = [sbt(f"bo_bf{i}", [1, 512], BF16) for i in range(2)]
    identb = sbt("identb", [128, 128], BF16)
    onesb = sbt("onesb", [128, 128], BF16)
    e2b = sbt("e2b", [NE, 224], BF16)
    thb = sbt("thbs", [128, 512], BF16)
    biasv_s = [sbt(f"bias{i}", [128, NKC], F32) for i in range(3)]
    stats = sbt("stats", [128, 48], F32)
    mv = sbt("mv", [128, 2], F32)
    veps = sbt("veps", [128, 1], F32)
    sqv = sbt("sqv", [128, 1], F32)
    rsig = sbt("rsig", [128, 1], F32)
    tmp1 = sbt("tmp1", [128, 1], F32)
    tmpb = sbt("tmpb", [128, 1], BF16)
    Dm = [sbt(f"Dm{i}", [128, 128], BF16) for i in range(2)]
    mrowS = [sbt(f"mrowS{i}", [1, 128], BF16) for i in range(2)]
    e_t = sbt("e_t", [128, 512], BF16)
    ind_t = sbt("ind_t", [128, 512], BF16)
    me_t = [sbt(f"me{i}", [128, 512], BF16) for i in range(2)]
    vqs = [sbt(f"vqs{i}", [128, 512], BF16) for i in range(2)]
    avus = [sbt(f"avus{i}", [128, 512], BF16) for i in range(2)]
    drec = [sbt(f"drec{i}", [128, 512], BF16) for i in range(2)]
    rec_t = [sbt(f"rec{i}", [128, 4], F32) for i in range(2)]
    Lsb = sbt("Lsb", [128, 4 * NE], F32)
    m1p = sbt("m1p", [128, 1], F32)
    nm1 = sbt("nm1", [128, 1], F32)
    m2p = sbt("m2p", [128, 1], F32)
    nm2 = sbt("nm2", [128, 1], F32)
    t1s = sbt("t1s", [128, NE], F32)
    c1s = sbt("c1s", [128, NE], F32)
    maskd = sbt("maskd", [128, NE], F32)
    d21 = sbt("d21", [128, 1], F32)
    em2 = sbt("em2", [128, 1], F32)
    den = sbt("den", [128, 1], F32)
    rec2 = sbt("rec2", [128, 1], F32)
    els = sbt("els", [128, NE], F32)
    c2s = sbt("c2s", [128, NE], F32)
    g1s = sbt("g1s", [128, NE], F32)
    gf = sbt("gf", [128, 4 * NE], F32)
    gfb = sbt("gfb", [128, 4 * NE], BF16)
    gfTs = sbt("gfTs", [NE, T], BF16)
    ge0 = sbt("ge0", [128, T], BF16)
    ge1 = sbt("ge1", [96, T], BF16)
    hs0 = sbt("hs0", [128, T], BF16)
    hs1 = sbt("hs1", [96, T], BF16)
    if dbg:
        lnds = sbt("lnds", [128, 8], F32)
        mrowf = sbt("mrowf", [1, 128], F32)
        t2ds = sbt("t2ds", [128, 8], F32)
    xr = [sbt(f"xr{i}", [128, 512], F32) for i in range(4)]
    P = [es.enter_context(nc.psum_tensor(f"P{i}", [128, 512], F32)) for i in range(8)]

    sx = Sem(es, nc, "sx")
    sxa = Sem(es, nc, "sxa")
    sxs = Sem(es, nc, "sxs")
    sw = Sem(es, nc, "sw")
    st = Sem(es, nc, "st")
    sv = Sem(es, nc, "sv")
    ss = Sem(es, nc, "ss")
    so = Sem(es, nc, "so")
    sd = Sem(es, nc, "sd")
    s_xbf = [Sem(es, nc, f"s_xbf{i}") for i in range(2)]
    s_wbb = [Sem(es, nc, f"s_wbb{i}") for i in range(4)]
    s_obf = [Sem(es, nc, f"s_obf{i}") for i in range(8)]
    s_a2 = [Sem(es, nc, f"s_a2{i}") for i in range(6)]
    s_gw = Sem(es, nc, "s_gw")
    s_bo = [Sem(es, nc, f"s_bo{i}") for i in range(2)]
    s_b2 = Sem(es, nc, "s_b2")
    s_ost = [Sem(es, nc, f"s_ost{i}") for i in range(4)]
    swc = Sem(es, nc, "swc")
    s_xr = [Sem(es, nc, f"s_xr{i}") for i in range(4)]

    rings = {}

    def ring_wait(eng, key):
        v = rings.get(key)
        if v is not None:
            eng.wait_ge(v[0].h, v[1])

    def ring_set(key, sem):
        rings[key] = (sem, sem.n)

    block = es.enter_context(nc.Block())

    with es:
        @block.sync
        def _(sy):
            swc.inc(sy.dma_start(out=identb[:], in_=idq_d), 16)
            swc.inc(sy.dma_start(out=onesb[:], in_=ones_d), 16)
            swc.inc(sy.dma_start(out=e2b[:], in_=e2_d), 16)
            swc.inc(sy.dma_start(out=thb[:], in_=thb_d), 16)
            swc.inc(sy.dma_start(out=biasv_s[0][:], in_=bq_d), 16)
            swc.inc(sy.dma_start(out=biasv_s[1][:], in_=bk_d), 16)
            swc.inc(sy.dma_start(out=biasv_s[2][:], in_=bv_d), 16)
        const_w = swc.n

        for it in range(NT):
            t_base = it * T
            first_tile = (it == 0)

            for m in range(NM):
                slot = m % 2

                @block.gpsimd
                def _(gp, m=m, slot=slot):
                    ring_wait(gp, ("xbf", slot))
                    s_xbf[slot].inc(gp.dma_start(
                        out=xbf[slot][:],
                        in_=x_d[t_base + m * 128: t_base + (m + 1) * 128, :]), 16)
                x_cnt = s_xbf[slot].n

                @block.vector
                def _(v, slot=slot, x_cnt=x_cnt):
                    v.wait_ge(s_xbf[slot].h, x_cnt)
                    for j in range(8):
                        ii = v.bn_stats(stats[:, j * 6:(j + 1) * 6],
                                        xbf[slot][:, j * 512:(j + 1) * 512])
                    sv.inc(ii)
                    v.wait_ge(sv.h, sv.n)
                    sv.inc(v.bn_aggr(mv[:], stats[:]))
                    v.wait_ge(sv.h, sv.n)
                    sv.inc(v.tensor_scalar(veps[:], mv[:, 1:2], EPS, None, ALU.add))
                v_veps = sv.n

                @block.scalar
                def _(s, v_veps=v_veps):
                    s.wait_ge(sv.h, v_veps)
                    ss.inc(s.activation(sqv[:], veps[:], AF.Ln))
                    s.wait_ge(ss.h, ss.n)
                    ss.inc(s.activation(rsig[:], sqv[:], AF.Exp, scale=-0.5))
                s_sq = ss.n

                @block.vector
                def _(v, slot=slot, s_sq=s_sq, it=it, m=m):
                    v.wait_ge(ss.h, s_sq)
                    if it == 0 and m == 0:
                        v.wait_ge(swc.h, const_w)
                    sv.inc(v.tensor_scalar(Dm[slot][:], identb[:], rsig[:, 0:1],
                                           None, ALU.mult))
                    sv.inc(v.tensor_scalar(tmp1[:], mv[:, 0:1], rsig[:, 0:1], -1.0,
                                           ALU.mult, ALU.mult))
                    v.wait_ge(sv.h, sv.n)
                    sv.inc(v.tensor_copy(tmpb[:], tmp1[:]))
                v_dm = sv.n

                @block.tensor
                def _(t, slot=slot, v_dm=v_dm, m=m, it=it):
                    if it == 0 and m == 0:
                        t.wait_ge(swc.h, const_w)
                    if it > 0 and m == 0:
                        t.wait_ge(sv.h, rings[("tile_ve",)][1])
                        t.wait_ge(ss.h, rings[("tile_se",)][1])
                    t.wait_ge(sv.h, v_dm)
                    st.inc(t.matmul(P[2][0:1, 0:128], tmpb[:], identb[:],
                                    start=True, stop=True))
                t_row = st.n

                @block.vector
                def _(v, slot=slot, t_row=t_row):
                    v.wait_ge(st.h, t_row)
                    sv.inc(v.tensor_copy(mrowS[slot][:], P[2][0:1, 0:128]))
                v_mrow = sv.n

                if dbg and it == 0 and m == 0:
                    @block.vector
                    def _(v):
                        v.wait_ge(sv.h, v_mrow)
                        v.tensor_copy(lnds[:, 0:2], mv[:])
                        v.tensor_copy(lnds[:, 2:3], veps[:])
                        v.tensor_copy(lnds[:, 3:4], sqv[:])
                        v.tensor_copy(lnds[:, 4:5], rsig[:])
                        v.tensor_copy(lnds[:, 5:6], tmp1[:])
                        sv.inc(v.tensor_copy(mrowf[:], mrowS[0][:]))

                    @block.gpsimd
                    def _(gp):
                        gp.wait_ge(sv.h, sv.n)
                        sd.inc(gp.dma_start(out=dbg_d["lnd"], in_=lnds[:]), 16)
                        sd.inc(gp.dma_start(out=dbg_d["mrow"], in_=mrowf[:]), 16)

                for cg in range(8):
                    bank = cg % 2

                    @block.tensor
                    def _(t, cg=cg, bank=bank, slot=slot, v_mrow=v_mrow, m=m):
                        t.wait_ge(sv.h, v_mrow)
                        ring_wait(t, ("pln", bank))
                        for cs in range(4):
                            c = cg * 4 + cs
                            t.matmul(P[bank][:, cs * 128:(cs + 1) * 128],
                                     xbf[slot][:, c * 128:(c + 1) * 128],
                                     Dm[slot][:], start=(cs == 0), stop=False)
                            ii = t.matmul(P[bank][:, cs * 128:(cs + 1) * 128],
                                          onesb[0:1, :], mrowS[slot][:],
                                          start=False, stop=(cs == 3))
                        st.inc(ii)
                    t_dmm = st.n

                    @block.vector
                    def _(v, cg=cg, bank=bank, t_dmm=t_dmm, m=m):
                        v.wait_ge(st.h, t_dmm)
                        dst = Xsb.ap()[:, cg * 4:(cg + 1) * 4, m * 128:(m + 1) * 128]
                        sv.inc(v.tensor_copy(dst, P[bank][:].rearrange("p (c t) -> p c t", c=4)))
                    ring_set(("pln", bank), sv)
                ring_set(("xbf", slot), st)
            ln_ve_last = sv.n

            flat = 0
            for w in range(3):
                dstt = [Qsb, Ksb, VAsb][w]
                for ko in range(NKC):
                    bank = 3 + (ko % 2)
                    for subq in range(2):
                        fslot = flat % WF_R

                        @block.gpsimd
                        def _(gp, w=w, ko=ko, subq=subq, fslot=fslot):
                            ring_wait(gp, ("wbb", fslot))
                            s_wbb[fslot].inc(gp.dma_start(
                                out=wbb[fslot].ap().rearrange("p (a c) -> p a c", a=16),
                                in_=qkv_w_d[w][ko][:, subq * 16:(subq + 1) * 16, :]), 16)
                        w_cnt = s_wbb[fslot].n

                        @block.tensor
                        def _(t, w=w, ko=ko, subq=subq, bank=bank, fslot=fslot,
                              w_cnt=w_cnt):
                            t.wait_ge(s_wbb[fslot].h, w_cnt)
                            if w == 0 and ko == 0 and subq == 0:
                                t.wait_ge(sv.h, ln_ve_last)
                            if subq == 0:
                                ring_wait(t, ("pqkv", bank))
                            for kl in range(16):
                                ki = subq * 16 + kl
                                ii = t.matmul(P[bank][:],
                                              wbb[fslot][:, kl * 128:(kl + 1) * 128],
                                              Xsb.ap()[:, ki, :],
                                              start=(ki == 0), stop=(ki == NKC - 1))
                            st.inc(ii)
                        ring_set(("wbb", fslot), st)
                        flat += 1
                    t_proj = st.n

                    @block.vector
                    def _(v, w=w, ko=ko, bank=bank, t_proj=t_proj, dstt=dstt, it=it):
                        v.wait_ge(st.h, t_proj)
                        if it == 0 and w == 0 and ko == 0:
                            v.wait_ge(swc.h, const_w)
                        dst = dstt.ap().rearrange("p (t h) -> p t h", h=32)[:, :, ko]
                        sv.inc(v.tensor_scalar(dst, P[bank][:],
                                               biasv_s[w][:, ko:ko + 1], None, ALU.add))
                    ring_set(("pqkv", bank), sv)
            qkv_ve = sv.n

            if dbg and first_tile:
                @block.gpsimd
                def _(gp):
                    gp.wait_ge(sv.h, qkv_ve)
                    sd.inc(gp.dma_start(out=dbg_d["q"], in_=Qsb[:]), 16)
                    sd.inc(gp.dma_start(out=dbg_d["k"], in_=Ksb[:]), 16)
                    sd.inc(gp.dma_start(out=dbg_d["v"], in_=VAsb[:]), 16)
                    sd.inc(gp.dma_start(
                        out=dbg_d["xn"].rearrange("p (c t) -> p c t", c=NKC),
                        in_=Xsb[:]), 16)

            TS = [0] * NB; TZ = [0] * NB; TVT = [0] * NB; TAV = [0] * NB
            TAVT = [0] * NB
            VEind = [0] * NB; VEme = [0] * NB; VErec = [0] * NB
            VEdrec = [0] * NB; VEao = [0] * NB
            SEexp = [0] * NB; SEvq = [0] * NB; SEau = [0] * NB

            for b in range(NB):
                sp = b % 2

                @block.tensor
                def _(t, b=b, sp=sp):
                    if b >= 2:
                        t.wait_ge(ss.h, SEexp[b - 2])
                        t.wait_ge(sv.h, VEind[b - 2])
                    else:
                        t.wait_ge(sv.h, qkv_ve)
                    for q in range(4):
                        t0 = b * 16 + q * 4
                        ii = t.matmul(P[sp][:, q * 128:(q + 1) * 128],
                                      Ksb[:, t0 * 32:(t0 + 4) * 32],
                                      Qsb[:, t0 * 32:(t0 + 4) * 32],
                                      start=(q == 0), stop=(q == 3))
                    st.inc(ii)
                TS[b] = st.n

                @block.scalar
                def _(s, b=b, sp=sp):
                    s.wait_ge(st.h, TS[b])
                    if b >= 1:
                        s.wait_ge(sv.h, VEme[b - 1])
                    ss.inc(s.activation(e_t[:], P[sp][:], AF.Exp, scale=INV_SQRT_D))
                SEexp[b] = ss.n

                @block.vector
                def _(v, b=b, sp=sp, it=it):
                    v.wait_ge(st.h, TS[b])
                    if it == 0 and b == 0:
                        v.wait_ge(swc.h, const_w)
                    sv.inc(v.tensor_tensor(ind_t[:], P[sp][:], thb[:], ALU.is_gt))
                VEind[b] = sv.n

                @block.vector
                def _(v, b=b, sp=sp):
                    v.wait_ge(ss.h, SEexp[b])
                    if b >= 2:
                        v.wait_ge(st.h, TAV[b - 2])
                    sv.inc(v.tensor_tensor(me_t[sp][:], e_t[:], ind_t[:], ALU.mult))
                VEme[b] = sv.n

                if dbg and first_tile and b == 0:
                    @block.gpsimd
                    def _(gp):
                        gp.wait_ge(sv.h, VEme[0])
                        sd.inc(gp.dma_start(out=dbg_d["me"], in_=me_t[0][:]), 16)

                @block.tensor
                def _(t, b=b, sp=sp):
                    t.wait_ge(sv.h, VEme[b])
                    if b >= 2:
                        t.wait_ge(sv.h, VErec[b - 2])
                    for q in range(4):
                        ii = t.matmul(P[2 + sp][:, q:q + 1],
                                      me_t[sp][:, q * 128:(q + 1) * 128],
                                      onesb[:, 0:1], start=(q == 0), stop=(q == 3))
                    st.inc(ii)
                    if b >= 1:
                        t.wait_ge(sv.h, SEvq[b - 1])
                    for q in range(4):
                        t0 = b * 16 + q * 4
                        ii = t.matmul(P[4][:, q * 128:(q + 1) * 128],
                                      VAsb[:, t0 * 32:(t0 + 4) * 32],
                                      identb[:], start=(q == 0), stop=(q == 3))
                    st.inc(ii)
                TZ[b] = st.n - 1
                TVT[b] = st.n

                @block.vector
                def _(v, b=b, sp=sp):
                    v.wait_ge(st.h, TVT[b])
                    sv.inc(v.tensor_copy(vqs[sp][:], P[4][:]))
                SEvq[b] = sv.n

                @block.tensor
                def _(t, b=b, sp=sp):
                    t.wait_ge(sv.h, SEvq[b])
                    if b >= 2:
                        t.wait_ge(sv.h, SEau[b - 2])
                    for q in range(4):
                        ii = t.matmul(P[5 + sp][:, q * 128:(q + 1) * 128],
                                      me_t[sp][:, q * 128:(q + 1) * 128],
                                      vqs[sp][:, q * 128:(q + 1) * 128],
                                      start=(q == 0), stop=(q == 3))
                    st.inc(ii)
                TAV[b] = st.n

                @block.vector
                def _(v, b=b, sp=sp):
                    v.wait_ge(st.h, TZ[b])
                    sv.inc(v.reciprocal(rec_t[sp][:], P[2 + sp][:, 0:4]))
                VErec[b] = sv.n

                @block.vector
                def _(v, b=b, sp=sp):
                    v.wait_ge(sv.h, VErec[b])
                    if b >= 2:
                        v.wait_ge(st.h, TAVT[b - 2])
                    for q in range(4):
                        ii = v.tensor_scalar(drec[sp][:, q * 128:(q + 1) * 128],
                                             identb[:], rec_t[sp][:, q:q + 1],
                                             None, ALU.mult)
                    sv.inc(ii)
                VEdrec[b] = sv.n

                @block.vector
                def _(v, b=b, sp=sp):
                    v.wait_ge(st.h, TAV[b])
                    sv.inc(v.tensor_copy(avus[sp][:], P[5 + sp][:]))
                SEau[b] = sv.n

                @block.tensor
                def _(t, b=b, sp=sp):
                    t.wait_ge(sv.h, SEau[b])
                    t.wait_ge(sv.h, VEdrec[b])
                    if b >= 1:
                        t.wait_ge(sv.h, VEao[b - 1])
                    for q in range(4):
                        ii = t.matmul(P[7][:, q * 128:(q + 1) * 128],
                                      avus[sp][:, q * 128:(q + 1) * 128],
                                      drec[sp][:, q * 128:(q + 1) * 128],
                                      start=(q == 0), stop=(q == 3))
                    st.inc(ii)
                TAVT[b] = st.n

                @block.vector
                def _(v, b=b, sp=sp):
                    v.wait_ge(st.h, TAVT[b])
                    sv.inc(v.tensor_copy(VAsb[:, b * 512:(b + 1) * 512], P[7][:]))
                VEao[b] = sv.n

            attn_ve = sv.n
            attn_se = ss.n

            if dbg and first_tile:
                @block.gpsimd
                def _(gp):
                    gp.wait_ge(sv.h, attn_ve)
                    sd.inc(gp.dma_start(out=dbg_d["ao"], in_=VAsb[:]), 16)

            @block.gpsimd
            def _(gp):
                ring_wait(gp, ("gwr",))
                s_gw.inc(gp.dma_start(out=gwr[:], in_=gw_d), 16)
            w_gw = s_gw.n

            @block.tensor
            def _(t, w_gw=w_gw):
                t.wait_ge(s_gw.h, w_gw)
                t.wait_ge(sv.h, attn_ve)
                t.wait_ge(ss.h, attn_se)
                for ki in range(NKC):
                    for m in range(NM):
                        ii = t.matmul(P[0][:, m * NE:(m + 1) * NE],
                                      Xsb.ap()[:, ki, m * 128:(m + 1) * 128],
                                      gwr.ap()[:, ki, :],
                                      start=(ki == 0 and m == 0),
                                      stop=(ki == NKC - 1))
                st.inc(ii)
            ring_set(("gwr",), st)
            t_logits = st.n

            @block.vector
            def _(v):
                v.wait_ge(st.h, t_logits)
                sv.inc(v.tensor_copy(Lsb[:], P[0][:, 0:4 * NE]))
            v_logits = sv.n

            if dbg and it == 0:
                @block.gpsimd
                def _(gp):
                    gp.wait_ge(sv.h, v_logits)
                    sd.inc(gp.dma_start(out=dbg_d["lg"], in_=Lsb[:]), 16)
                    sd.inc(gp.dma_start(
                        out=dbg_d["xn2"].rearrange("p (c t) -> p c t", c=NKC),
                        in_=Xsb[:]), 16)
                    sd.inc(gp.dma_start(out=dbg_d["gw0"][:, 0:NE], in_=gwr.ap()[:, 30, :]), 16)
                    sd.inc(gp.dma_start(out=dbg_d["gw0"][:, NE:2*NE], in_=gwr.ap()[:, 31, :]), 16)

            s_exp_prev = [0]
            for m in range(NM):
                L = Lsb[:, m * NE:(m + 1) * NE]

                @block.vector
                def _(v, m=m, L=L):
                    if m == 0:
                        v.wait_ge(sv.h, v_logits)
                    else:
                        v.wait_ge(ss.h, s_exp_prev[0])
                    sv.inc(v.reduce_max(m1p[:], L, axis=AX.X))
                    v.wait_ge(sv.h, sv.n)
                    sv.inc(v.tensor_scalar(nm1[:], m1p[:], -1.0, None, ALU.mult))
                    v.wait_ge(sv.h, sv.n)
                    sv.inc(v.tensor_scalar(t1s[:], L, nm1[:, 0:1], None, ALU.add))
                    v.wait_ge(sv.h, sv.n)
                    sv.inc(v.tensor_single_scalar(c1s[:], t1s[:], 0.0, ALU.is_ge))
                    v.wait_ge(sv.h, sv.n)
                    sv.inc(v.scalar_tensor_tensor(maskd[:], c1s[:], BIGNEG, L,
                                                  ALU.mult, ALU.add))
                    v.wait_ge(sv.h, sv.n)
                    sv.inc(v.reduce_max(m2p[:], maskd[:], axis=AX.X))
                    v.wait_ge(sv.h, sv.n)
                    sv.inc(v.tensor_scalar(nm2[:], m2p[:], -1.0, None, ALU.mult))
                    sv.inc(v.tensor_sub(d21[:], m2p[:], m1p[:]))
                    v.wait_ge(sv.h, sv.n)
                v_m2 = sv.n

                @block.scalar
                def _(s, m=m, L=L, v_m2=v_m2):
                    s.wait_ge(sv.h, v_m2)
                    ss.inc(s.activation(els[:], L, AF.Exp, bias=nm1[:, 0:1]))
                    ss.inc(s.activation(em2[:], d21[:], AF.Exp))
                s_exp = ss.n
                s_exp_prev[0] = s_exp

                if dbg and it == 0 and m == 0:
                    @block.vector
                    def _(v):
                        v.wait_ge(ss.h, s_exp)
                        v.tensor_copy(t2ds[:, 0:1], m1p[:])
                        v.tensor_copy(t2ds[:, 1:2], m2p[:])
                        v.tensor_copy(t2ds[:, 2:3], d21[:])
                        v.tensor_copy(t2ds[:, 3:4], em2[:])
                        v.tensor_copy(t2ds[:, 4:5], els[:, 0:1])
                        sv.inc(v.tensor_copy(t2ds[:, 5:6], c1s[:, 0:1]))

                    @block.gpsimd
                    def _(gp):
                        gp.wait_ge(sv.h, sv.n)
                        sd.inc(gp.dma_start(out=dbg_d["t2d"], in_=t2ds[:]), 16)

                @block.vector
                def _(v, m=m, L=L, s_exp=s_exp):
                    v.wait_ge(ss.h, s_exp)
                    sv.inc(v.tensor_scalar(den[:], em2[:], 1.0, None, ALU.add))
                    v.wait_ge(sv.h, sv.n)
                    sv.inc(v.reciprocal(rec2[:], den[:]))
                    v.wait_ge(sv.h, sv.n)
                    sv.inc(v.tensor_scalar(c2s[:], L, nm2[:, 0:1], 0.0,
                                           ALU.add, ALU.is_ge))
                    v.wait_ge(sv.h, sv.n)
                    sv.inc(v.tensor_mul(g1s[:], c2s[:], els[:]))
                    v.wait_ge(sv.h, sv.n)
                    sv.inc(v.tensor_scalar(gf[:, m * NE:(m + 1) * NE], g1s[:],
                                           rec2[:, 0:1], None, ALU.mult))

            @block.vector
            def _(v):
                v.wait_ge(sv.h, sv.n)
                sv.inc(v.tensor_copy(gfb[:], gf[:]))
            v_gfb = sv.n

            if dbg and first_tile:
                @block.gpsimd
                def _(gp):
                    gp.wait_ge(sv.h, v_gfb)
                    sd.inc(gp.dma_start(out=dbg_d["gf"], in_=gf[:]), 16)

            @block.tensor
            def _(t):
                t.wait_ge(sv.h, v_gfb)
                for m in range(NM):
                    ii = t.matmul(P[1][0:NE, m * 128:(m + 1) * 128],
                                  gfb[:, m * NE:(m + 1) * NE], identb[:],
                                  start=(m == 0), stop=(m == NM - 1))
                st.inc(ii)
            t_gft = st.n

            @block.vector
            def _(v):
                v.wait_ge(st.h, t_gft)
                sv.inc(v.tensor_copy(gfTs[:], P[1][0:NE, :]))
            v_gft = sv.n

            @block.tensor
            def _(t):
                t.wait_ge(sv.h, v_gft)
                st.inc(t.matmul(P[2][:], e2b[:, 0:128], gfTs[:], start=True, stop=True))
                st.inc(t.matmul(P[3][0:96, :], e2b[:, 128:224], gfTs[:],
                                start=True, stop=True))
            t_ge = st.n

            @block.vector
            def _(v):
                v.wait_ge(st.h, t_ge)
                sv.inc(v.tensor_copy(ge0[:], P[2][:]))
                sv.inc(v.tensor_copy(ge1[:], P[3][0:96, :]))
            v_ge = sv.n

            for ki in range(NKC):
                aslot = ki % 6

                @block.gpsimd
                def _(gp, ki=ki, aslot=aslot):
                    ring_wait(gp, ("a2b", aslot))
                    s_a2[aslot].inc(gp.dma_start(out=a2b[aslot][:], in_=a2_d[ki]), 16)
                s_cnt = s_a2[aslot].n

                @block.tensor
                def _(t, ki=ki, aslot=aslot, s_cnt=s_cnt):
                    t.wait_ge(s_a2[aslot].h, s_cnt)
                    if ki == 0:
                        t.wait_ge(sv.h, attn_ve)
                        t.wait_ge(ss.h, attn_se)
                    t.matmul(P[4][:], a2b[aslot][:, 0:128], Xsb.ap()[:, ki, :],
                             start=(ki == 0), stop=(ki == NKC - 1))
                    ii = t.matmul(P[5][0:96, :], a2b[aslot][:, 128:224],
                                  Xsb.ap()[:, ki, :],
                                  start=(ki == 0), stop=(ki == NKC - 1))
                    st.inc(ii)
                ring_set(("a2b", aslot), st)
            t_h = st.n

            @block.vector
            def _(v):
                v.wait_ge(st.h, t_h)
                v.wait_ge(sv.h, v_ge)
                sv.inc(v.tensor_tensor(hs0[:], P[4][:], ge0[:], ALU.mult))
                sv.inc(v.tensor_tensor(hs1[:], P[5][0:96, :], ge1[:], ALU.mult))
            v_hs = sv.n

            if dbg and first_tile:
                @block.gpsimd
                def _(gp):
                    gp.wait_ge(sv.h, v_hs)
                    sd.inc(gp.dma_start(out=dbg_d["hs0"], in_=hs0[:]), 16)
                    sd.inc(gp.dma_start(out=dbg_d["hs1"], in_=hs1[:]), 16)

            ev_done = [0] * NOG
            oflat = 0
            for n in range(NOG):
                par = 4 * (n % 2)
                bslot_bo = n % 2

                @block.gpsimd
                def _(gp, n=n, bslot_bo=bslot_bo):
                    ring_wait(gp, ("bo", bslot_bo))
                    s_bo[bslot_bo].inc(gp.dma_start(
                        out=bo_bf[bslot_bo][:],
                        in_=bo_d[:, n * 512:(n + 1) * 512]), 16)
                x_bo = s_bo[bslot_bo].n

                @block.gpsimd
                def _(gp, n=n):
                    ring_wait(gp, ("b2b",))
                    s_b2.inc(gp.dma_start(out=b2b0[:], in_=b20_d[n]), 16)
                    s_b2.inc(gp.dma_start(out=b2b1[:], in_=b21_d[n]), 16)
                s_b2n = s_b2.n

                @block.sync
                def _(sy, n=n):
                    for m in range(NM):
                        sy.wait_ge(s_ost[m].h, s_ost[m].n)
                        s_xr[m].inc(sy.dma_start(
                            out=xr[m][:],
                            in_=x_d[t_base + m * 128: t_base + (m + 1) * 128,
                                    n * 512:(n + 1) * 512]), 16)
                w_xr = [s.n for s in s_xr]

                for kk in range(19):
                    if kk < 8:
                        src_d, kind, base_k = wo_d[n][:, kk * 4:kk * 4 + 4, :], "w", kk * 4
                    elif kk == 8:
                        kind = "bo"
                    elif kk < 17:
                        c = kk - 9
                        src_d, kind, base_k = wb_d[n][:, c * 4:c * 4 + 4, :], "w", 33 + c * 4
                    elif kk == 17:
                        kind = "b20"
                    else:
                        kind = "b21"

                    if kind == "w":
                        bslot = oflat % OB_R

                        @block.gpsimd
                        def _(gp, src_d=src_d, bslot=bslot):
                            ring_wait(gp, ("obf", bslot))
                            s_obf[bslot].inc(gp.dma_start(
                                out=obf[bslot][:], in_=src_d), 16)
                        rdy = (s_obf[bslot], s_obf[bslot].n)

                        @block.tensor
                        def _(t, kk=kk, n=n, par=par, bslot=bslot, rdy=rdy,
                              base_k=base_k):
                            t.wait_ge(rdy[0].h, rdy[1])
                            if kk == 0:
                                if n <= 1:
                                    t.wait_ge(sv.h, v_hs)
                                else:
                                    t.wait_ge(sv.h, ev_done[n - 2])
                            for kl in range(4):
                                kabs = base_k + kl
                                for m in range(NM):
                                    if kabs < 32:
                                        lhsT = VAsb.ap().rearrange(
                                            "p (t h) -> p t h",
                                            h=32)[:, m * 128:(m + 1) * 128, kabs]
                                    else:
                                        lhsT = Xsb.ap()[:, kabs - 33, m * 128:(m + 1) * 128]
                                    ii = t.matmul(P[par + m][:], lhsT,
                                                  obf[bslot].ap()[:, kl, :],
                                                  start=(kabs == 0), stop=False)
                            st.inc(ii)
                        ring_set(("obf", bslot), st)
                        oflat += 1
                    elif kind == "bo":
                        @block.tensor
                        def _(t, n=n, par=par, x_bo=x_bo, bslot_bo=bslot_bo):
                            t.wait_ge(s_bo[bslot_bo].h, x_bo)
                            for m in range(NM):
                                t.matmul(P[par + m][:], onesb[0:1, 0:128],
                                         bo_bf[bslot_bo][:], start=False, stop=False)
                    elif kind == "b20":
                        @block.tensor
                        def _(t, par=par, s_b2n=s_b2n):
                            t.wait_ge(s_b2.h, s_b2n)
                            for m in range(NM):
                                t.matmul(P[par + m][:], hs0[:, m * 128:(m + 1) * 128],
                                         b2b0[:], start=False, stop=False)
                    else:
                        @block.tensor
                        def _(t, par=par):
                            for m in range(NM):
                                ii = t.matmul(P[par + m][:],
                                              hs1[:, m * 128:(m + 1) * 128],
                                              b2b1[:], start=False, stop=True)
                            st.inc(ii)
                t_out_n = st.n
                ring_set(("b2b",), st)
                ring_set(("bo", bslot_bo), st)

                @block.vector
                def _(v, n=n, par=par, t_out_n=t_out_n, w_xr=w_xr):
                    v.wait_ge(st.h, t_out_n)
                    for m in range(NM):
                        v.wait_ge(s_xr[m].h, w_xr[m])
                        sv.inc(v.tensor_add(xr[m][:], P[par + m][:], xr[m][:]))
                        ring_set(("xr", m), sv)
                ev_done[n] = sv.n

                @block.gpsimd
                def _(gp, n=n, t_base=t_base):
                    gp.wait_ge(sv.h, ev_done[n])
                    for m in range(NM):
                        s_ost[m].inc(gp.dma_start(
                            out=y_d[t_base + m * 128: t_base + (m + 1) * 128,
                                    n * 512:(n + 1) * 512],
                            in_=xr[m][:]), 16)

            ring_set(("tile_ve",), sv)
            ring_set(("tile_se",), ss)

        @block.gpsimd
        def _(gp):
            for m in range(4):
                gp.wait_ge(s_ost[m].h, s_ost[m].n)
            if dbg:
                gp.wait_ge(sd.h, sd.n)

    return nc


def prep_inputs(inputs):
    Wq = np.asarray(inputs["L0_attn_Wq"]); Wk = np.asarray(inputs["L0_attn_Wk"])
    Wv = np.asarray(inputs["L0_attn_Wv"]); Wo = np.asarray(inputs["L0_attn_Wo"])
    baseW = np.asarray(inputs["L0_ffn_base_W"])
    gateW = np.asarray(inputs["L0_ffn_gate_W"])
    loraA = np.asarray(inputs["L0_ffn_lora_A"]); loraB = np.asarray(inputs["L0_ffn_lora_B"])

    def qkv_tiles(W):
        WT = np.ascontiguousarray(W.T)
        return np.ascontiguousarray(WT.reshape(NKC, 128, NKC, 128).transpose(2, 1, 0, 3))

    def out_tiles(W):
        WT = np.ascontiguousarray(W.T)
        return np.ascontiguousarray(WT.reshape(NKC, 128, NOG, 512).transpose(2, 1, 0, 3))

    A2 = loraA.transpose(1, 0, 2).reshape(224, HID)
    a2t = np.ascontiguousarray(A2.T.reshape(NKC, 128, 224))
    B2 = loraB.transpose(2, 0, 1).reshape(224, HID)
    b20 = np.ascontiguousarray(B2[:128].reshape(128, NOG, 512).transpose(1, 0, 2))
    b21 = np.ascontiguousarray(B2[128:].reshape(96, NOG, 512).transpose(1, 0, 2))
    gw = np.ascontiguousarray(gateW.T.reshape(NKC, 128, NE).transpose(1, 0, 2))

    def bias_cols(b):
        return np.ascontiguousarray(np.asarray(b).reshape(NKC, 128).T)

    e2 = np.zeros((NE, 224), dtype=BF)
    for r in range(RK):
        for e in range(NE):
            e2[e, r * NE + e] = BF(2.0)
    ident = np.eye(128, dtype=BF)
    onesc = np.ones((128, 128), dtype=BF)
    pat = np.full((128, 128), 1.0e30, dtype=BF)
    for tt in range(4):
        pat[tt * 32:(tt + 1) * 32, tt * 32:(tt + 1) * 32] = BF(TH_RAW)
    thb = np.ascontiguousarray(np.tile(pat, (1, 4)))

    return {
        "wq": qkv_tiles(Wq), "wk": qkv_tiles(Wk), "wv": qkv_tiles(Wv),
        "wo": out_tiles(Wo), "wb": out_tiles(baseW),
        "a2": a2t, "b20": b20, "b21": b21, "gw": gw,
        "bq": bias_cols(inputs["L0_attn_bq"]),
        "bk": bias_cols(inputs["L0_attn_bk"]),
        "bv": bias_cols(inputs["L0_attn_bv"]),
        "bo": np.ascontiguousarray(np.asarray(inputs["L0_attn_bo"])[None, :]),
        "idq": ident, "onesc": onesc, "e2": e2, "thb": thb,
    }


def kernel(**inputs):
    ncores = 8
    ntok = 2048
    T = 512
    x_full = np.ascontiguousarray(
        np.asarray(inputs["L0_attn_x"], dtype=np.float32).reshape(16384, HID))
    wmap = prep_inputs(inputs)
    nc = build_program(ntok, T, dbg=False)
    in_maps = []
    for c in range(ncores):
        m = dict(wmap)
        m["x"] = x_full[c * ntok:(c + 1) * ntok]
        in_maps.append(m)
    res = run_bass_kernel_spmd(nc, in_maps, core_ids=list(range(ncores)))
    out = np.concatenate([np.asarray(r["y"]) for r in res.results], axis=0)
    return out.reshape(4, 4096, HID)


# version 18
# speedup vs baseline: 1.0053x; 1.0053x over previous
import numpy as np
import ml_dtypes
from contextlib import ExitStack

import concourse.bass as bass
import concourse.mybir as mybir
from concourse.bass_utils import run_bass_kernel_spmd

F32 = mybir.dt.float32
BF16 = mybir.dt.bfloat16
AF = mybir.ActivationFunctionType
ALU = mybir.AluOpType
AX = mybir.AxisListType
BF = ml_dtypes.bfloat16

HID = 4096
NH = 32
NE = 28
RK = 8
NKC = 32
NOG = 8
TH_RAW = float(np.float32(0.001) * np.float32(np.sqrt(np.float32(128.0))))
INV_SQRT_D = float(np.float32(1.0) / np.float32(np.sqrt(np.float32(128.0))))
EPS = 1e-5
BIGNEG = -1.0e30


class Sem:
    def __init__(self, es, nc, name):
        self.h = es.enter_context(nc.semaphore(name))
        self.n = 0

    def inc(self, instr, k=1):
        instr.then_inc(self.h, k)
        self.n += k
        return self.n


def build_program(ntok, T, dbg=False):
    nc = bass.Bass("TRN2", target_bir_lowering=False)
    NT = ntok // T
    NM = T // 128
    NB = T // 16

    x_d = nc.dram_tensor("x", [ntok, HID], F32, kind="ExternalInput").ap()
    y_d = nc.dram_tensor("y", [ntok, HID], F32, kind="ExternalOutput").ap()
    wq_d = nc.dram_tensor("wq", [NKC, 128, NKC, 128], F32, kind="ExternalInput").ap()
    wk_d = nc.dram_tensor("wk", [NKC, 128, NKC, 128], F32, kind="ExternalInput").ap()
    wv_d = nc.dram_tensor("wv", [NKC, 128, NKC, 128], F32, kind="ExternalInput").ap()
    wo_d = nc.dram_tensor("wo", [NOG, 128, NKC, 512], F32, kind="ExternalInput").ap()
    wb_d = nc.dram_tensor("wb", [NOG, 128, NKC, 512], F32, kind="ExternalInput").ap()
    a2_d = nc.dram_tensor("a2", [NKC, 128, 224], F32, kind="ExternalInput").ap()
    b20_d = nc.dram_tensor("b20", [NOG, 128, 512], F32, kind="ExternalInput").ap()
    b21_d = nc.dram_tensor("b21", [NOG, 96, 512], F32, kind="ExternalInput").ap()
    gw_d = nc.dram_tensor("gw", [128, NKC, NE], F32, kind="ExternalInput").ap()
    bq_d = nc.dram_tensor("bq", [128, NKC], F32, kind="ExternalInput").ap()
    bk_d = nc.dram_tensor("bk", [128, NKC], F32, kind="ExternalInput").ap()
    bv_d = nc.dram_tensor("bv", [128, NKC], F32, kind="ExternalInput").ap()
    bo_d = nc.dram_tensor("bo", [1, HID], F32, kind="ExternalInput").ap()
    idq_d = nc.dram_tensor("idq", [128, 128], BF16, kind="ExternalInput").ap()
    ones_d = nc.dram_tensor("onesc", [128, 128], BF16, kind="ExternalInput").ap()
    e2_d = nc.dram_tensor("e2", [NE, 224], BF16, kind="ExternalInput").ap()
    thb_d = nc.dram_tensor("thb", [128, 512], BF16, kind="ExternalInput").ap()
    qkv_w_d = [wq_d, wk_d, wv_d]
    dbg_d = {}
    if dbg:
        for nm, shp in [("xn", [128, NKC * T]), ("q", [128, T * 32]),
                        ("k", [128, T * 32]), ("v", [128, T * 32]),
                        ("ao", [128, T * 32]), ("gf", [128, 4 * NE]),
                        ("hs0", [128, T]), ("hs1", [96, T]), ("me", [128, 512]),
                        ("lnd", [128, 8]), ("mrow", [1, 128]),
                        ("lg", [128, 4 * NE]), ("t2d", [128, 8]),
                        ("xn2", [128, NKC * T]), ("gw0", [128, 2 * NE])]:
            dbg_d[nm] = nc.dram_tensor("d_" + nm, shp, F32, kind="ExternalOutput").ap()

    es = ExitStack()

    def sbt(name, shape, dt):
        return es.enter_context(nc.sbuf_tensor(name, shape, dt))

    Xsb = sbt("Xsb", [128, NKC, T], BF16)
    Qsb = sbt("Qsb", [128, T * 32], BF16)
    Ksb = sbt("Ksb", [128, T * 32], BF16)
    VAsb = sbt("VAsb", [128, T * 32], BF16)
    xbf = [sbt(f"xbf{i}", [128, HID], BF16) for i in range(2)]
    WF_R, OB_R = 3, 3
    wbb = [sbt(f"wbb{i}", [128, 2048], BF16) for i in range(WF_R)]
    obf = [sbt(f"obf{i}", [128, 4, 512], BF16) for i in range(OB_R)]
    a2b = [sbt(f"a2b{i}", [128, 224], BF16) for i in range(6)]
    b2b0 = sbt("b2b0", [128, 512], BF16)
    b2b1 = sbt("b2b1", [96, 512], BF16)
    gwr = sbt("gwr", [128, NKC, NE], BF16)
    bo_bf = [sbt(f"bo_bf{i}", [1, 512], BF16) for i in range(2)]
    identb = sbt("identb", [128, 128], BF16)
    onesb = sbt("onesb", [128, 128], BF16)
    e2b = sbt("e2b", [NE, 224], BF16)
    thb = sbt("thbs", [128, 512], BF16)
    biasv_s = [sbt(f"bias{i}", [128, NKC], F32) for i in range(3)]
    stats = sbt("stats", [128, 48], F32)
    mv = sbt("mv", [128, 2], F32)
    veps = sbt("veps", [128, 1], F32)
    sqv = sbt("sqv", [128, 1], F32)
    rsig = sbt("rsig", [128, 1], F32)
    tmp1 = sbt("tmp1", [128, 1], F32)
    tmpb = sbt("tmpb", [128, 1], BF16)
    Dm = [sbt(f"Dm{i}", [128, 128], BF16) for i in range(2)]
    mrowS = [sbt(f"mrowS{i}", [1, 128], BF16) for i in range(2)]
    e_t = sbt("e_t", [128, 512], BF16)
    ind_t = sbt("ind_t", [128, 512], BF16)
    me_t = [sbt(f"me{i}", [128, 512], BF16) for i in range(2)]
    vqs = [sbt(f"vqs{i}", [128, 512], BF16) for i in range(2)]
    avus = [sbt(f"avus{i}", [128, 512], BF16) for i in range(2)]
    drec = [sbt(f"drec{i}", [128, 512], BF16) for i in range(2)]
    rec_t = [sbt(f"rec{i}", [128, 4], F32) for i in range(2)]
    Lsb = sbt("Lsb", [128, 4 * NE], F32)
    m1p = sbt("m1p", [128, 1], F32)
    nm1 = sbt("nm1", [128, 1], F32)
    m2p = sbt("m2p", [128, 1], F32)
    nm2 = sbt("nm2", [128, 1], F32)
    t1s = sbt("t1s", [128, NE], F32)
    c1s = sbt("c1s", [128, NE], F32)
    maskd = sbt("maskd", [128, NE], F32)
    d21 = sbt("d21", [128, 1], F32)
    em2 = sbt("em2", [128, 1], F32)
    den = sbt("den", [128, 1], F32)
    rec2 = sbt("rec2", [128, 1], F32)
    els = sbt("els", [128, NE], F32)
    c2s = sbt("c2s", [128, NE], F32)
    g1s = sbt("g1s", [128, NE], F32)
    gf = sbt("gf", [128, 4 * NE], F32)
    gfb = sbt("gfb", [128, 4 * NE], BF16)
    gfTs = sbt("gfTs", [NE, T], BF16)
    ge0 = sbt("ge0", [128, T], BF16)
    ge1 = sbt("ge1", [96, T], BF16)
    hs0 = sbt("hs0", [128, T], BF16)
    hs1 = sbt("hs1", [96, T], BF16)
    if dbg:
        lnds = sbt("lnds", [128, 8], F32)
        mrowf = sbt("mrowf", [1, 128], F32)
        t2ds = sbt("t2ds", [128, 8], F32)
    xr = [sbt(f"xr{i}", [128, 512], F32) for i in range(4)]
    P = [es.enter_context(nc.psum_tensor(f"P{i}", [128, 512], F32)) for i in range(8)]

    sx = Sem(es, nc, "sx")
    sxa = Sem(es, nc, "sxa")
    sxs = Sem(es, nc, "sxs")
    sw = Sem(es, nc, "sw")
    st = Sem(es, nc, "st")
    sv = Sem(es, nc, "sv")
    ss = Sem(es, nc, "ss")
    so = Sem(es, nc, "so")
    sd = Sem(es, nc, "sd")
    s_xbf = [Sem(es, nc, f"s_xbf{i}") for i in range(2)]
    s_wbb = [Sem(es, nc, f"s_wbb{i}") for i in range(4)]
    s_obf = [Sem(es, nc, f"s_obf{i}") for i in range(8)]
    s_a2 = [Sem(es, nc, f"s_a2{i}") for i in range(6)]
    s_gw = Sem(es, nc, "s_gw")
    s_bo = [Sem(es, nc, f"s_bo{i}") for i in range(2)]
    s_b2 = Sem(es, nc, "s_b2")
    s_ost = [Sem(es, nc, f"s_ost{i}") for i in range(4)]
    swc = Sem(es, nc, "swc")
    s_xr = [Sem(es, nc, f"s_xr{i}") for i in range(4)]

    rings = {}

    def ring_wait(eng, key):
        v = rings.get(key)
        if v is not None:
            eng.wait_ge(v[0].h, v[1])

    def ring_set(key, sem):
        rings[key] = (sem, sem.n)

    block = es.enter_context(nc.Block())

    with es:
        @block.sync
        def _(sy):
            swc.inc(sy.dma_start(out=identb[:], in_=idq_d), 16)
            swc.inc(sy.dma_start(out=onesb[:], in_=ones_d), 16)
            swc.inc(sy.dma_start(out=e2b[:], in_=e2_d), 16)
            swc.inc(sy.dma_start(out=thb[:], in_=thb_d), 16)
            swc.inc(sy.dma_start(out=biasv_s[0][:], in_=bq_d), 16)
            swc.inc(sy.dma_start(out=biasv_s[1][:], in_=bk_d), 16)
            swc.inc(sy.dma_start(out=biasv_s[2][:], in_=bv_d), 16)
        const_w = swc.n

        for it in range(NT):
            t_base = it * T
            first_tile = (it == 0)

            for m in range(NM):
                slot = m % 2

                @block.gpsimd
                def _(gp, m=m, slot=slot):
                    ring_wait(gp, ("xbf", slot))
                    s_xbf[slot].inc(gp.dma_start(
                        out=xbf[slot][:],
                        in_=x_d[t_base + m * 128: t_base + (m + 1) * 128, :]), 16)
                x_cnt = s_xbf[slot].n

                @block.vector
                def _(v, slot=slot, x_cnt=x_cnt):
                    v.wait_ge(s_xbf[slot].h, x_cnt)
                    for j in range(8):
                        ii = v.bn_stats(stats[:, j * 6:(j + 1) * 6],
                                        xbf[slot][:, j * 512:(j + 1) * 512])
                    sv.inc(ii)
                    v.wait_ge(sv.h, sv.n)
                    sv.inc(v.bn_aggr(mv[:], stats[:]))
                    v.wait_ge(sv.h, sv.n)
                    sv.inc(v.tensor_scalar(veps[:], mv[:, 1:2], EPS, None, ALU.add))
                v_veps = sv.n

                @block.scalar
                def _(s, v_veps=v_veps):
                    s.wait_ge(sv.h, v_veps)
                    ss.inc(s.activation(sqv[:], veps[:], AF.Ln))
                    s.wait_ge(ss.h, ss.n)
                    ss.inc(s.activation(rsig[:], sqv[:], AF.Exp, scale=-0.5))
                s_sq = ss.n

                @block.vector
                def _(v, slot=slot, s_sq=s_sq, it=it, m=m):
                    v.wait_ge(ss.h, s_sq)
                    if it == 0 and m == 0:
                        v.wait_ge(swc.h, const_w)
                    sv.inc(v.tensor_scalar(Dm[slot][:], identb[:], rsig[:, 0:1],
                                           None, ALU.mult))
                    sv.inc(v.tensor_scalar(tmp1[:], mv[:, 0:1], rsig[:, 0:1], -1.0,
                                           ALU.mult, ALU.mult))
                    v.wait_ge(sv.h, sv.n)
                    sv.inc(v.tensor_copy(tmpb[:], tmp1[:]))
                v_dm = sv.n

                @block.tensor
                def _(t, slot=slot, v_dm=v_dm, m=m, it=it):
                    if it == 0 and m == 0:
                        t.wait_ge(swc.h, const_w)
                    if it > 0 and m == 0:
                        t.wait_ge(sv.h, rings[("tile_ve",)][1])
                        t.wait_ge(ss.h, rings[("tile_se",)][1])
                    t.wait_ge(sv.h, v_dm)
                    st.inc(t.matmul(P[2][0:1, 0:128], tmpb[:], identb[:],
                                    start=True, stop=True))
                t_row = st.n

                @block.vector
                def _(v, slot=slot, t_row=t_row):
                    v.wait_ge(st.h, t_row)
                    sv.inc(v.tensor_copy(mrowS[slot][:], P[2][0:1, 0:128]))
                v_mrow = sv.n

                if dbg and it == 0 and m == 0:
                    @block.vector
                    def _(v):
                        v.wait_ge(sv.h, v_mrow)
                        v.tensor_copy(lnds[:, 0:2], mv[:])
                        v.tensor_copy(lnds[:, 2:3], veps[:])
                        v.tensor_copy(lnds[:, 3:4], sqv[:])
                        v.tensor_copy(lnds[:, 4:5], rsig[:])
                        v.tensor_copy(lnds[:, 5:6], tmp1[:])
                        sv.inc(v.tensor_copy(mrowf[:], mrowS[0][:]))

                    @block.gpsimd
                    def _(gp):
                        gp.wait_ge(sv.h, sv.n)
                        sd.inc(gp.dma_start(out=dbg_d["lnd"], in_=lnds[:]), 16)
                        sd.inc(gp.dma_start(out=dbg_d["mrow"], in_=mrowf[:]), 16)

                for cg in range(8):
                    bank = cg % 2

                    @block.tensor
                    def _(t, cg=cg, bank=bank, slot=slot, v_mrow=v_mrow, m=m):
                        t.wait_ge(sv.h, v_mrow)
                        ring_wait(t, ("pln", bank))
                        for cs in range(4):
                            c = cg * 4 + cs
                            t.matmul(P[bank][:, cs * 128:(cs + 1) * 128],
                                     xbf[slot][:, c * 128:(c + 1) * 128],
                                     Dm[slot][:], start=(cs == 0), stop=False)
                            ii = t.matmul(P[bank][:, cs * 128:(cs + 1) * 128],
                                          onesb[0:1, :], mrowS[slot][:],
                                          start=False, stop=(cs == 3))
                        st.inc(ii)
                    t_dmm = st.n

                    @block.vector
                    def _(v, cg=cg, bank=bank, t_dmm=t_dmm, m=m):
                        v.wait_ge(st.h, t_dmm)
                        dst = Xsb.ap()[:, cg * 4:(cg + 1) * 4, m * 128:(m + 1) * 128]
                        sv.inc(v.tensor_copy(dst, P[bank][:].rearrange("p (c t) -> p c t", c=4)))
                    ring_set(("pln", bank), sv)
                ring_set(("xbf", slot), st)
            ln_ve_last = sv.n

            flat = 0
            for w in range(3):
                dstt = [Qsb, Ksb, VAsb][w]
                for ko in range(NKC):
                    bank = 3 + (ko % 2)
                    for subq in range(2):
                        fslot = flat % WF_R

                        @block.gpsimd
                        def _(gp, w=w, ko=ko, subq=subq, fslot=fslot):
                            ring_wait(gp, ("wbb", fslot))
                            s_wbb[fslot].inc(gp.dma_start(
                                out=wbb[fslot].ap().rearrange("p (a c) -> p a c", a=16),
                                in_=qkv_w_d[w][ko][:, subq * 16:(subq + 1) * 16, :]), 16)
                        w_cnt = s_wbb[fslot].n

                        @block.tensor
                        def _(t, w=w, ko=ko, subq=subq, bank=bank, fslot=fslot,
                              w_cnt=w_cnt):
                            t.wait_ge(s_wbb[fslot].h, w_cnt)
                            if w == 0 and ko == 0 and subq == 0:
                                t.wait_ge(sv.h, ln_ve_last)
                            if subq == 0:
                                ring_wait(t, ("pqkv", bank))
                            for kl in range(16):
                                ki = subq * 16 + kl
                                ii = t.matmul(P[bank][:],
                                              wbb[fslot][:, kl * 128:(kl + 1) * 128],
                                              Xsb.ap()[:, ki, :],
                                              start=(ki == 0), stop=(ki == NKC - 1))
                            st.inc(ii)
                        ring_set(("wbb", fslot), st)
                        flat += 1
                    t_proj = st.n

                    @block.vector
                    def _(v, w=w, ko=ko, bank=bank, t_proj=t_proj, dstt=dstt, it=it):
                        v.wait_ge(st.h, t_proj)
                        if it == 0 and w == 0 and ko == 0:
                            v.wait_ge(swc.h, const_w)
                        dst = dstt.ap().rearrange("p (t h) -> p t h", h=32)[:, :, ko]
                        sv.inc(v.tensor_scalar(dst, P[bank][:],
                                               biasv_s[w][:, ko:ko + 1], None, ALU.add))
                    ring_set(("pqkv", bank), sv)
            qkv_ve = sv.n

            if dbg and first_tile:
                @block.gpsimd
                def _(gp):
                    gp.wait_ge(sv.h, qkv_ve)
                    sd.inc(gp.dma_start(out=dbg_d["q"], in_=Qsb[:]), 16)
                    sd.inc(gp.dma_start(out=dbg_d["k"], in_=Ksb[:]), 16)
                    sd.inc(gp.dma_start(out=dbg_d["v"], in_=VAsb[:]), 16)
                    sd.inc(gp.dma_start(
                        out=dbg_d["xn"].rearrange("p (c t) -> p c t", c=NKC),
                        in_=Xsb[:]), 16)

            TS = [0] * NB; TZ = [0] * NB; TVT = [0] * NB; TAV = [0] * NB
            TAVT = [0] * NB
            VEind = [0] * NB; VEme = [0] * NB; VErec = [0] * NB
            VEdrec = [0] * NB; VEao = [0] * NB
            SEexp = [0] * NB; SEvq = [0] * NB; SEau = [0] * NB

            for b in range(NB):
                sp = b % 2

                @block.tensor
                def _(t, b=b, sp=sp):
                    if b >= 2:
                        t.wait_ge(ss.h, SEexp[b - 2])
                        t.wait_ge(sv.h, VEind[b - 2])
                    else:
                        t.wait_ge(sv.h, qkv_ve)
                    for q in range(4):
                        t0 = b * 16 + q * 4
                        ii = t.matmul(P[sp][:, q * 128:(q + 1) * 128],
                                      Ksb[:, t0 * 32:(t0 + 4) * 32],
                                      Qsb[:, t0 * 32:(t0 + 4) * 32],
                                      start=(q == 0), stop=(q == 3))
                    st.inc(ii)
                TS[b] = st.n

                @block.scalar
                def _(s, b=b, sp=sp):
                    s.wait_ge(st.h, TS[b])
                    if b >= 1:
                        s.wait_ge(sv.h, VEme[b - 1])
                    ss.inc(s.activation(e_t[:], P[sp][:], AF.Exp, scale=INV_SQRT_D))
                SEexp[b] = ss.n

                @block.vector
                def _(v, b=b, sp=sp, it=it):
                    v.wait_ge(st.h, TS[b])
                    if it == 0 and b == 0:
                        v.wait_ge(swc.h, const_w)
                    sv.inc(v.tensor_tensor(ind_t[:], P[sp][:], thb[:], ALU.is_gt))
                VEind[b] = sv.n

                @block.vector
                def _(v, b=b, sp=sp):
                    v.wait_ge(ss.h, SEexp[b])
                    v.wait_ge(sv.h, VEind[b])
                    if b >= 2:
                        v.wait_ge(st.h, TAV[b - 2])
                    sv.inc(v.tensor_tensor(me_t[sp][:], e_t[:], ind_t[:], ALU.mult))
                VEme[b] = sv.n

                if dbg and first_tile and b == 0:
                    @block.gpsimd
                    def _(gp):
                        gp.wait_ge(sv.h, VEme[0])
                        sd.inc(gp.dma_start(out=dbg_d["me"], in_=me_t[0][:]), 16)

                @block.tensor
                def _(t, b=b, sp=sp):
                    t.wait_ge(sv.h, VEme[b])
                    if b >= 2:
                        t.wait_ge(sv.h, VErec[b - 2])
                    for q in range(4):
                        ii = t.matmul(P[2 + sp][:, q:q + 1],
                                      me_t[sp][:, q * 128:(q + 1) * 128],
                                      onesb[:, 0:1], start=(q == 0), stop=(q == 3))
                    st.inc(ii)
                    if b >= 1:
                        t.wait_ge(sv.h, SEvq[b - 1])
                    for q in range(4):
                        t0 = b * 16 + q * 4
                        ii = t.matmul(P[4][:, q * 128:(q + 1) * 128],
                                      VAsb[:, t0 * 32:(t0 + 4) * 32],
                                      identb[:], start=(q == 0), stop=(q == 3))
                    st.inc(ii)
                TZ[b] = st.n - 1
                TVT[b] = st.n

                @block.vector
                def _(v, b=b, sp=sp):
                    v.wait_ge(st.h, TVT[b])
                    sv.inc(v.tensor_copy(vqs[sp][:], P[4][:]))
                SEvq[b] = sv.n

                @block.tensor
                def _(t, b=b, sp=sp):
                    t.wait_ge(sv.h, SEvq[b])
                    if b >= 2:
                        t.wait_ge(sv.h, SEau[b - 2])
                    for q in range(4):
                        ii = t.matmul(P[5 + sp][:, q * 128:(q + 1) * 128],
                                      me_t[sp][:, q * 128:(q + 1) * 128],
                                      vqs[sp][:, q * 128:(q + 1) * 128],
                                      start=(q == 0), stop=(q == 3))
                    st.inc(ii)
                TAV[b] = st.n

                @block.vector
                def _(v, b=b, sp=sp):
                    v.wait_ge(st.h, TZ[b])
                    sv.inc(v.reciprocal(rec_t[sp][:], P[2 + sp][:, 0:4]))
                VErec[b] = sv.n

                @block.vector
                def _(v, b=b, sp=sp):
                    v.wait_ge(sv.h, VErec[b])
                    if b >= 2:
                        v.wait_ge(st.h, TAVT[b - 2])
                    for q in range(4):
                        ii = v.tensor_scalar(drec[sp][:, q * 128:(q + 1) * 128],
                                             identb[:], rec_t[sp][:, q:q + 1],
                                             None, ALU.mult)
                    sv.inc(ii)
                VEdrec[b] = sv.n

                @block.vector
                def _(v, b=b, sp=sp):
                    v.wait_ge(st.h, TAV[b])
                    sv.inc(v.tensor_copy(avus[sp][:], P[5 + sp][:]))
                SEau[b] = sv.n

                @block.tensor
                def _(t, b=b, sp=sp):
                    t.wait_ge(sv.h, SEau[b])
                    t.wait_ge(sv.h, VEdrec[b])
                    if b >= 1:
                        t.wait_ge(sv.h, VEao[b - 1])
                    for q in range(4):
                        ii = t.matmul(P[7][:, q * 128:(q + 1) * 128],
                                      avus[sp][:, q * 128:(q + 1) * 128],
                                      drec[sp][:, q * 128:(q + 1) * 128],
                                      start=(q == 0), stop=(q == 3))
                    st.inc(ii)
                TAVT[b] = st.n

                @block.vector
                def _(v, b=b, sp=sp):
                    v.wait_ge(st.h, TAVT[b])
                    sv.inc(v.tensor_copy(VAsb[:, b * 512:(b + 1) * 512], P[7][:]))
                VEao[b] = sv.n

            attn_ve = sv.n
            attn_se = ss.n

            if dbg and first_tile:
                @block.gpsimd
                def _(gp):
                    gp.wait_ge(sv.h, attn_ve)
                    sd.inc(gp.dma_start(out=dbg_d["ao"], in_=VAsb[:]), 16)

            @block.gpsimd
            def _(gp):
                ring_wait(gp, ("gwr",))
                s_gw.inc(gp.dma_start(out=gwr[:], in_=gw_d), 16)
            w_gw = s_gw.n

            @block.tensor
            def _(t, w_gw=w_gw):
                t.wait_ge(s_gw.h, w_gw)
                t.wait_ge(sv.h, attn_ve)
                t.wait_ge(ss.h, attn_se)
                for ki in range(NKC):
                    for m in range(NM):
                        ii = t.matmul(P[0][:, m * NE:(m + 1) * NE],
                                      Xsb.ap()[:, ki, m * 128:(m + 1) * 128],
                                      gwr.ap()[:, ki, :],
                                      start=(ki == 0 and m == 0),
                                      stop=(ki == NKC - 1))
                st.inc(ii)
            ring_set(("gwr",), st)
            t_logits = st.n

            @block.vector
            def _(v):
                v.wait_ge(st.h, t_logits)
                sv.inc(v.tensor_copy(Lsb[:], P[0][:, 0:4 * NE]))
            v_logits = sv.n

            if dbg and it == 0:
                @block.gpsimd
                def _(gp):
                    gp.wait_ge(sv.h, v_logits)
                    sd.inc(gp.dma_start(out=dbg_d["lg"], in_=Lsb[:]), 16)
                    sd.inc(gp.dma_start(
                        out=dbg_d["xn2"].rearrange("p (c t) -> p c t", c=NKC),
                        in_=Xsb[:]), 16)
                    sd.inc(gp.dma_start(out=dbg_d["gw0"][:, 0:NE], in_=gwr.ap()[:, 30, :]), 16)
                    sd.inc(gp.dma_start(out=dbg_d["gw0"][:, NE:2*NE], in_=gwr.ap()[:, 31, :]), 16)

            s_exp_prev = [0]
            for m in range(NM):
                L = Lsb[:, m * NE:(m + 1) * NE]

                @block.vector
                def _(v, m=m, L=L):
                    if m == 0:
                        v.wait_ge(sv.h, v_logits)
                    else:
                        v.wait_ge(ss.h, s_exp_prev[0])
                    sv.inc(v.reduce_max(m1p[:], L, axis=AX.X))
                    v.wait_ge(sv.h, sv.n)
                    sv.inc(v.tensor_scalar(nm1[:], m1p[:], -1.0, None, ALU.mult))
                    v.wait_ge(sv.h, sv.n)
                    sv.inc(v.tensor_scalar(t1s[:], L, nm1[:, 0:1], None, ALU.add))
                    v.wait_ge(sv.h, sv.n)
                    sv.inc(v.tensor_single_scalar(c1s[:], t1s[:], 0.0, ALU.is_ge))
                    v.wait_ge(sv.h, sv.n)
                    sv.inc(v.scalar_tensor_tensor(maskd[:], c1s[:], BIGNEG, L,
                                                  ALU.mult, ALU.add))
                    v.wait_ge(sv.h, sv.n)
                    sv.inc(v.reduce_max(m2p[:], maskd[:], axis=AX.X))
                    v.wait_ge(sv.h, sv.n)
                    sv.inc(v.tensor_scalar(nm2[:], m2p[:], -1.0, None, ALU.mult))
                    sv.inc(v.tensor_sub(d21[:], m2p[:], m1p[:]))
                    v.wait_ge(sv.h, sv.n)
                v_m2 = sv.n

                @block.scalar
                def _(s, m=m, L=L, v_m2=v_m2):
                    s.wait_ge(sv.h, v_m2)
                    ss.inc(s.activation(els[:], L, AF.Exp, bias=nm1[:, 0:1]))
                    ss.inc(s.activation(em2[:], d21[:], AF.Exp))
                s_exp = ss.n
                s_exp_prev[0] = s_exp

                if dbg and it == 0 and m == 0:
                    @block.vector
                    def _(v):
                        v.wait_ge(ss.h, s_exp)
                        v.tensor_copy(t2ds[:, 0:1], m1p[:])
                        v.tensor_copy(t2ds[:, 1:2], m2p[:])
                        v.tensor_copy(t2ds[:, 2:3], d21[:])
                        v.tensor_copy(t2ds[:, 3:4], em2[:])
                        v.tensor_copy(t2ds[:, 4:5], els[:, 0:1])
                        sv.inc(v.tensor_copy(t2ds[:, 5:6], c1s[:, 0:1]))

                    @block.gpsimd
                    def _(gp):
                        gp.wait_ge(sv.h, sv.n)
                        sd.inc(gp.dma_start(out=dbg_d["t2d"], in_=t2ds[:]), 16)

                @block.vector
                def _(v, m=m, L=L, s_exp=s_exp):
                    v.wait_ge(ss.h, s_exp)
                    sv.inc(v.tensor_scalar(den[:], em2[:], 1.0, None, ALU.add))
                    v.wait_ge(sv.h, sv.n)
                    sv.inc(v.reciprocal(rec2[:], den[:]))
                    v.wait_ge(sv.h, sv.n)
                    sv.inc(v.tensor_scalar(c2s[:], L, nm2[:, 0:1], 0.0,
                                           ALU.add, ALU.is_ge))
                    v.wait_ge(sv.h, sv.n)
                    sv.inc(v.tensor_mul(g1s[:], c2s[:], els[:]))
                    v.wait_ge(sv.h, sv.n)
                    sv.inc(v.tensor_scalar(gf[:, m * NE:(m + 1) * NE], g1s[:],
                                           rec2[:, 0:1], None, ALU.mult))

            @block.vector
            def _(v):
                v.wait_ge(sv.h, sv.n)
                sv.inc(v.tensor_copy(gfb[:], gf[:]))
            v_gfb = sv.n

            if dbg and first_tile:
                @block.gpsimd
                def _(gp):
                    gp.wait_ge(sv.h, v_gfb)
                    sd.inc(gp.dma_start(out=dbg_d["gf"], in_=gf[:]), 16)

            @block.tensor
            def _(t):
                t.wait_ge(sv.h, v_gfb)
                for m in range(NM):
                    ii = t.matmul(P[1][0:NE, m * 128:(m + 1) * 128],
                                  gfb[:, m * NE:(m + 1) * NE], identb[:],
                                  start=(m == 0), stop=(m == NM - 1))
                st.inc(ii)
            t_gft = st.n

            @block.vector
            def _(v):
                v.wait_ge(st.h, t_gft)
                sv.inc(v.tensor_copy(gfTs[:], P[1][0:NE, :]))
            v_gft = sv.n

            @block.tensor
            def _(t):
                t.wait_ge(sv.h, v_gft)
                st.inc(t.matmul(P[2][:], e2b[:, 0:128], gfTs[:], start=True, stop=True))
                st.inc(t.matmul(P[3][0:96, :], e2b[:, 128:224], gfTs[:],
                                start=True, stop=True))
            t_ge = st.n

            @block.vector
            def _(v):
                v.wait_ge(st.h, t_ge)
                sv.inc(v.tensor_copy(ge0[:], P[2][:]))
                sv.inc(v.tensor_copy(ge1[:], P[3][0:96, :]))
            v_ge = sv.n

            for ki in range(NKC):
                aslot = ki % 6

                @block.gpsimd
                def _(gp, ki=ki, aslot=aslot):
                    ring_wait(gp, ("a2b", aslot))
                    s_a2[aslot].inc(gp.dma_start(out=a2b[aslot][:], in_=a2_d[ki]), 16)
                s_cnt = s_a2[aslot].n

                @block.tensor
                def _(t, ki=ki, aslot=aslot, s_cnt=s_cnt):
                    t.wait_ge(s_a2[aslot].h, s_cnt)
                    if ki == 0:
                        t.wait_ge(sv.h, attn_ve)
                        t.wait_ge(ss.h, attn_se)
                    t.matmul(P[4][:], a2b[aslot][:, 0:128], Xsb.ap()[:, ki, :],
                             start=(ki == 0), stop=(ki == NKC - 1))
                    ii = t.matmul(P[5][0:96, :], a2b[aslot][:, 128:224],
                                  Xsb.ap()[:, ki, :],
                                  start=(ki == 0), stop=(ki == NKC - 1))
                    st.inc(ii)
                ring_set(("a2b", aslot), st)
            t_h = st.n

            @block.vector
            def _(v):
                v.wait_ge(st.h, t_h)
                v.wait_ge(sv.h, v_ge)
                sv.inc(v.tensor_tensor(hs0[:], P[4][:], ge0[:], ALU.mult))
                sv.inc(v.tensor_tensor(hs1[:], P[5][0:96, :], ge1[:], ALU.mult))
            v_hs = sv.n

            if dbg and first_tile:
                @block.gpsimd
                def _(gp):
                    gp.wait_ge(sv.h, v_hs)
                    sd.inc(gp.dma_start(out=dbg_d["hs0"], in_=hs0[:]), 16)
                    sd.inc(gp.dma_start(out=dbg_d["hs1"], in_=hs1[:]), 16)

            ev_done = [0] * NOG
            oflat = 0
            for n in range(NOG):
                par = 4 * (n % 2)
                bslot_bo = n % 2

                @block.gpsimd
                def _(gp, n=n, bslot_bo=bslot_bo):
                    ring_wait(gp, ("bo", bslot_bo))
                    s_bo[bslot_bo].inc(gp.dma_start(
                        out=bo_bf[bslot_bo][:],
                        in_=bo_d[:, n * 512:(n + 1) * 512]), 16)
                x_bo = s_bo[bslot_bo].n

                @block.gpsimd
                def _(gp, n=n):
                    ring_wait(gp, ("b2b",))
                    s_b2.inc(gp.dma_start(out=b2b0[:], in_=b20_d[n]), 16)
                    s_b2.inc(gp.dma_start(out=b2b1[:], in_=b21_d[n]), 16)
                s_b2n = s_b2.n

                @block.sync
                def _(sy, n=n):
                    for m in range(NM):
                        sy.wait_ge(s_ost[m].h, s_ost[m].n)
                        s_xr[m].inc(sy.dma_start(
                            out=xr[m][:],
                            in_=x_d[t_base + m * 128: t_base + (m + 1) * 128,
                                    n * 512:(n + 1) * 512]), 16)
                w_xr = [s.n for s in s_xr]

                for kk in range(19):
                    if kk < 8:
                        src_d, kind, base_k = wo_d[n][:, kk * 4:kk * 4 + 4, :], "w", kk * 4
                    elif kk == 8:
                        kind = "bo"
                    elif kk < 17:
                        c = kk - 9
                        src_d, kind, base_k = wb_d[n][:, c * 4:c * 4 + 4, :], "w", 33 + c * 4
                    elif kk == 17:
                        kind = "b20"
                    else:
                        kind = "b21"

                    if kind == "w":
                        bslot = oflat % OB_R

                        @block.gpsimd
                        def _(gp, src_d=src_d, bslot=bslot):
                            ring_wait(gp, ("obf", bslot))
                            s_obf[bslot].inc(gp.dma_start(
                                out=obf[bslot][:], in_=src_d), 16)
                        rdy = (s_obf[bslot], s_obf[bslot].n)

                        @block.tensor
                        def _(t, kk=kk, n=n, par=par, bslot=bslot, rdy=rdy,
                              base_k=base_k):
                            t.wait_ge(rdy[0].h, rdy[1])
                            if kk == 0:
                                if n <= 1:
                                    t.wait_ge(sv.h, v_hs)
                                else:
                                    t.wait_ge(sv.h, ev_done[n - 2])
                            for kl in range(4):
                                kabs = base_k + kl
                                for m in range(NM):
                                    if kabs < 32:
                                        lhsT = VAsb.ap().rearrange(
                                            "p (t h) -> p t h",
                                            h=32)[:, m * 128:(m + 1) * 128, kabs]
                                    else:
                                        lhsT = Xsb.ap()[:, kabs - 33, m * 128:(m + 1) * 128]
                                    ii = t.matmul(P[par + m][:], lhsT,
                                                  obf[bslot].ap()[:, kl, :],
                                                  start=(kabs == 0), stop=False)
                            st.inc(ii)
                        ring_set(("obf", bslot), st)
                        oflat += 1
                    elif kind == "bo":
                        @block.tensor
                        def _(t, n=n, par=par, x_bo=x_bo, bslot_bo=bslot_bo):
                            t.wait_ge(s_bo[bslot_bo].h, x_bo)
                            for m in range(NM):
                                t.matmul(P[par + m][:], onesb[0:1, 0:128],
                                         bo_bf[bslot_bo][:], start=False, stop=False)
                    elif kind == "b20":
                        @block.tensor
                        def _(t, par=par, s_b2n=s_b2n):
                            t.wait_ge(s_b2.h, s_b2n)
                            for m in range(NM):
                                t.matmul(P[par + m][:], hs0[:, m * 128:(m + 1) * 128],
                                         b2b0[:], start=False, stop=False)
                    else:
                        @block.tensor
                        def _(t, par=par):
                            for m in range(NM):
                                ii = t.matmul(P[par + m][:],
                                              hs1[:, m * 128:(m + 1) * 128],
                                              b2b1[:], start=False, stop=True)
                            st.inc(ii)
                t_out_n = st.n
                ring_set(("b2b",), st)
                ring_set(("bo", bslot_bo), st)

                @block.vector
                def _(v, n=n, par=par, t_out_n=t_out_n, w_xr=w_xr):
                    v.wait_ge(st.h, t_out_n)
                    for m in range(NM):
                        v.wait_ge(s_xr[m].h, w_xr[m])
                        sv.inc(v.tensor_add(xr[m][:], P[par + m][:], xr[m][:]))
                        ring_set(("xr", m), sv)
                ev_done[n] = sv.n

                @block.gpsimd
                def _(gp, n=n, t_base=t_base):
                    gp.wait_ge(sv.h, ev_done[n])
                    for m in range(NM):
                        s_ost[m].inc(gp.dma_start(
                            out=y_d[t_base + m * 128: t_base + (m + 1) * 128,
                                    n * 512:(n + 1) * 512],
                            in_=xr[m][:]), 16)

            ring_set(("tile_ve",), sv)
            ring_set(("tile_se",), ss)

        @block.gpsimd
        def _(gp):
            for m in range(4):
                gp.wait_ge(s_ost[m].h, s_ost[m].n)
            if dbg:
                gp.wait_ge(sd.h, sd.n)

    return nc


def prep_inputs(inputs):
    Wq = np.asarray(inputs["L0_attn_Wq"]); Wk = np.asarray(inputs["L0_attn_Wk"])
    Wv = np.asarray(inputs["L0_attn_Wv"]); Wo = np.asarray(inputs["L0_attn_Wo"])
    baseW = np.asarray(inputs["L0_ffn_base_W"])
    gateW = np.asarray(inputs["L0_ffn_gate_W"])
    loraA = np.asarray(inputs["L0_ffn_lora_A"]); loraB = np.asarray(inputs["L0_ffn_lora_B"])

    def qkv_tiles(W):
        WT = np.ascontiguousarray(W.T)
        return np.ascontiguousarray(WT.reshape(NKC, 128, NKC, 128).transpose(2, 1, 0, 3))

    def out_tiles(W):
        WT = np.ascontiguousarray(W.T)
        return np.ascontiguousarray(WT.reshape(NKC, 128, NOG, 512).transpose(2, 1, 0, 3))

    A2 = loraA.transpose(1, 0, 2).reshape(224, HID)
    a2t = np.ascontiguousarray(A2.T.reshape(NKC, 128, 224))
    B2 = loraB.transpose(2, 0, 1).reshape(224, HID)
    b20 = np.ascontiguousarray(B2[:128].reshape(128, NOG, 512).transpose(1, 0, 2))
    b21 = np.ascontiguousarray(B2[128:].reshape(96, NOG, 512).transpose(1, 0, 2))
    gw = np.ascontiguousarray(gateW.T.reshape(NKC, 128, NE).transpose(1, 0, 2))

    def bias_cols(b):
        return np.ascontiguousarray(np.asarray(b).reshape(NKC, 128).T)

    e2 = np.zeros((NE, 224), dtype=BF)
    for r in range(RK):
        for e in range(NE):
            e2[e, r * NE + e] = BF(2.0)
    ident = np.eye(128, dtype=BF)
    onesc = np.ones((128, 128), dtype=BF)
    pat = np.full((128, 128), 1.0e30, dtype=BF)
    for tt in range(4):
        pat[tt * 32:(tt + 1) * 32, tt * 32:(tt + 1) * 32] = BF(TH_RAW)
    thb = np.ascontiguousarray(np.tile(pat, (1, 4)))

    return {
        "wq": qkv_tiles(Wq), "wk": qkv_tiles(Wk), "wv": qkv_tiles(Wv),
        "wo": out_tiles(Wo), "wb": out_tiles(baseW),
        "a2": a2t, "b20": b20, "b21": b21, "gw": gw,
        "bq": bias_cols(inputs["L0_attn_bq"]),
        "bk": bias_cols(inputs["L0_attn_bk"]),
        "bv": bias_cols(inputs["L0_attn_bv"]),
        "bo": np.ascontiguousarray(np.asarray(inputs["L0_attn_bo"])[None, :]),
        "idq": ident, "onesc": onesc, "e2": e2, "thb": thb,
    }


def kernel(**inputs):
    ncores = 8
    ntok = 2048
    T = 512
    x_full = np.ascontiguousarray(
        np.asarray(inputs["L0_attn_x"], dtype=np.float32).reshape(16384, HID))
    wmap = prep_inputs(inputs)
    nc = build_program(ntok, T, dbg=False)
    in_maps = []
    for c in range(ncores):
        m = dict(wmap)
        m["x"] = x_full[c * ntok:(c + 1) * ntok]
        in_maps.append(m)
    res = run_bass_kernel_spmd(nc, in_maps, core_ids=list(range(ncores)))
    out = np.concatenate([np.asarray(r["y"]) for r in res.results], axis=0)
    return out.reshape(4, 4096, HID)


# version 19
# speedup vs baseline: 1.0319x; 1.0265x over previous
import numpy as np
import ml_dtypes
from contextlib import ExitStack

import concourse.bass as bass
import concourse.mybir as mybir
from concourse.bass_utils import run_bass_kernel_spmd

F32 = mybir.dt.float32
BF16 = mybir.dt.bfloat16
AF = mybir.ActivationFunctionType
ALU = mybir.AluOpType
AX = mybir.AxisListType
BF = ml_dtypes.bfloat16

HID = 4096
NH = 32
NE = 28
RK = 8
NKC = 32
NOG = 8
TH_RAW = float(np.float32(0.001) * np.float32(np.sqrt(np.float32(128.0))))
INV_SQRT_D = float(np.float32(1.0) / np.float32(np.sqrt(np.float32(128.0))))
EPS = 1e-5
BIGNEG = -1.0e30




class Sem:
    def __init__(self, es, nc, name):
        self.h = es.enter_context(nc.semaphore(name))
        self.n = 0

    def inc(self, instr, k=1):
        instr.then_inc(self.h, k)
        self.n += k
        return self.n


def build_program(ntok, T, dbg=False):
    nc = bass.Bass("TRN2", target_bir_lowering=False)
    NT = ntok // T
    NM = T // 128
    NB = T // 16

    x_d = nc.dram_tensor("x", [ntok, HID], F32, kind="ExternalInput").ap()
    y_d = nc.dram_tensor("y", [ntok, HID], F32, kind="ExternalOutput").ap()
    wq_d = nc.dram_tensor("wq", [NKC, 128, NKC, 128], F32, kind="ExternalInput").ap()
    wk_d = nc.dram_tensor("wk", [NKC, 128, NKC, 128], F32, kind="ExternalInput").ap()
    wv_d = nc.dram_tensor("wv", [NKC, 128, NKC, 128], F32, kind="ExternalInput").ap()
    wo_d = nc.dram_tensor("wo", [NOG, 128, NKC, 512], F32, kind="ExternalInput").ap()
    wb_d = nc.dram_tensor("wb", [NOG, 128, NKC, 512], F32, kind="ExternalInput").ap()
    a2_d = nc.dram_tensor("a2", [NKC, 128, 224], F32, kind="ExternalInput").ap()
    b20_d = nc.dram_tensor("b20", [NOG, 128, 512], F32, kind="ExternalInput").ap()
    b21_d = nc.dram_tensor("b21", [NOG, 96, 512], F32, kind="ExternalInput").ap()
    gw_d = nc.dram_tensor("gw", [128, NKC, NE], F32, kind="ExternalInput").ap()
    bq_d = nc.dram_tensor("bq", [128, NKC], F32, kind="ExternalInput").ap()
    bk_d = nc.dram_tensor("bk", [128, NKC], F32, kind="ExternalInput").ap()
    bv_d = nc.dram_tensor("bv", [128, NKC], F32, kind="ExternalInput").ap()
    bo_d = nc.dram_tensor("bo", [1, HID], F32, kind="ExternalInput").ap()
    idq_d = nc.dram_tensor("idq", [128, 128], BF16, kind="ExternalInput").ap()
    ones_d = nc.dram_tensor("onesc", [128, 128], BF16, kind="ExternalInput").ap()
    e2_d = nc.dram_tensor("e2", [NE, 224], BF16, kind="ExternalInput").ap()
    thb_d = nc.dram_tensor("thb", [128, 512], BF16, kind="ExternalInput").ap()
    qkv_w_d = [wq_d, wk_d, wv_d]
    dbg_d = {}
    if dbg:
        for nm, shp in [("xn", [128, NKC * T]), ("q", [128, T * 32]),
                        ("k", [128, T * 32]), ("v", [128, T * 32]),
                        ("ao", [128, T * 32]), ("gf", [128, 4 * NE]),
                        ("hs0", [128, T]), ("hs1", [96, T]), ("me", [128, 512]),
                        ("lnd", [128, 8]), ("mrow", [1, 128]),
                        ("lg", [128, 4 * NE]), ("t2d", [128, 8]),
                        ("xn2", [128, NKC * T]), ("gw0", [128, 2 * NE])]:
            dbg_d[nm] = nc.dram_tensor("d_" + nm, shp, F32, kind="ExternalOutput").ap()

    es = ExitStack()

    def sbt(name, shape, dt):
        return es.enter_context(nc.sbuf_tensor(name, shape, dt))

    Xsb = sbt("Xsb", [128, NKC, T], BF16)
    Qsb = sbt("Qsb", [128, T * 32], BF16)
    Ksb = sbt("Ksb", [128, T * 32], BF16)
    VAsb = sbt("VAsb", [128, T * 32], BF16)
    xbf = [sbt(f"xbf{i}", [128, HID], BF16) for i in range(2)]
    WF_R, OB_R = 3, 3
    wbb = [sbt(f"wbb{i}", [128, 2048], BF16) for i in range(WF_R)]
    obf = [sbt(f"obf{i}", [128, 4, 512], BF16) for i in range(OB_R)]
    a2b = [sbt(f"a2b{i}", [128, 224], BF16) for i in range(6)]
    b2b0 = sbt("b2b0", [128, 512], BF16)
    b2b1 = sbt("b2b1", [96, 512], BF16)
    gwr = sbt("gwr", [128, NKC, NE], BF16)
    bo_bf = [sbt(f"bo_bf{i}", [1, 512], BF16) for i in range(2)]
    identb = sbt("identb", [128, 128], BF16)
    onesb = sbt("onesb", [128, 128], BF16)
    e2b = sbt("e2b", [NE, 224], BF16)
    thb = sbt("thbs", [128, 512], BF16)
    biasv_s = [sbt(f"bias{i}", [128, NKC], F32) for i in range(3)]
    stats = sbt("stats", [128, 48], F32)
    mv = sbt("mv", [128, 2], F32)
    veps = sbt("veps", [128, 1], F32)
    sqv = sbt("sqv", [128, 1], F32)
    rsig = sbt("rsig", [128, 1], F32)
    tmp1 = sbt("tmp1", [128, 1], F32)
    tmpb = sbt("tmpb", [128, 1], BF16)
    Dm = [sbt(f"Dm{i}", [128, 128], BF16) for i in range(2)]
    mrowS = [sbt(f"mrowS{i}", [1, 128], BF16) for i in range(2)]
    e_t = sbt("e_t", [128, 512], BF16)
    ind_t = sbt("ind_t", [128, 512], BF16)
    me_t = [sbt(f"me{i}", [128, 512], BF16) for i in range(2)]
    vqs = [sbt(f"vqs{i}", [128, 512], BF16) for i in range(2)]
    avus = [sbt(f"avus{i}", [128, 512], BF16) for i in range(2)]
    drec = [sbt(f"drec{i}", [128, 512], BF16) for i in range(2)]
    rec_t = [sbt(f"rec{i}", [128, 4], F32) for i in range(2)]
    Lsb = sbt("Lsb", [128, 4 * NE], F32)
    m1p = sbt("m1p", [128, 1], F32)
    nm1 = sbt("nm1", [128, 1], F32)
    m2p = sbt("m2p", [128, 1], F32)
    nm2 = sbt("nm2", [128, 1], F32)
    t1s = sbt("t1s", [128, NE], F32)
    c1s = sbt("c1s", [128, NE], F32)
    maskd = sbt("maskd", [128, NE], F32)
    d21 = sbt("d21", [128, 1], F32)
    em2 = sbt("em2", [128, 1], F32)
    den = sbt("den", [128, 1], F32)
    rec2 = sbt("rec2", [128, 1], F32)
    els = sbt("els", [128, NE], F32)
    c2s = sbt("c2s", [128, NE], F32)
    g1s = sbt("g1s", [128, NE], F32)
    gf = sbt("gf", [128, 4 * NE], F32)
    gfb = sbt("gfb", [128, 4 * NE], BF16)
    gfTs = sbt("gfTs", [NE, T], BF16)
    ge0 = sbt("ge0", [128, T], BF16)
    ge1 = sbt("ge1", [96, T], BF16)
    hs0 = sbt("hs0", [128, T], BF16)
    hs1 = sbt("hs1", [96, T], BF16)
    if dbg:
        lnds = sbt("lnds", [128, 8], F32)
        mrowf = sbt("mrowf", [1, 128], F32)
        t2ds = sbt("t2ds", [128, 8], F32)
    xr = [sbt(f"xr{i}", [128, 512], F32) for i in range(4)]
    P = [es.enter_context(nc.psum_tensor(f"P{i}", [128, 512], F32)) for i in range(8)]

    sx = Sem(es, nc, "sx")
    sxa = Sem(es, nc, "sxa")
    sxs = Sem(es, nc, "sxs")
    sw = Sem(es, nc, "sw")
    st = Sem(es, nc, "st")
    sv = Sem(es, nc, "sv")
    ss = Sem(es, nc, "ss")
    so = Sem(es, nc, "so")
    sd = Sem(es, nc, "sd")
    s_xbf = [Sem(es, nc, f"s_xbf{i}") for i in range(2)]
    s_wbb = [Sem(es, nc, f"s_wbb{i}") for i in range(4)]
    s_obf = [Sem(es, nc, f"s_obf{i}") for i in range(8)]
    s_a2 = [Sem(es, nc, f"s_a2{i}") for i in range(6)]
    s_gw = Sem(es, nc, "s_gw")
    s_bo = [Sem(es, nc, f"s_bo{i}") for i in range(2)]
    s_b2 = Sem(es, nc, "s_b2")
    s_ost = [Sem(es, nc, f"s_ost{i}") for i in range(4)]

    rings = {}

    def ring_wait(eng, key):
        v = rings.get(key)
        if v is not None:
            eng.wait_ge(v[0].h, v[1])

    def ring_set(key, sem):
        rings[key] = (sem, sem.n)

    block = es.enter_context(nc.Block())

    with es:
        @block.sync
        def _(sy):
            sw.inc(sy.dma_start(out=identb[:], in_=idq_d), 16)
            sw.inc(sy.dma_start(out=onesb[:], in_=ones_d), 16)
            sw.inc(sy.dma_start(out=e2b[:], in_=e2_d), 16)
            sw.inc(sy.dma_start(out=thb[:], in_=thb_d), 16)
            sw.inc(sy.dma_start(out=biasv_s[0][:], in_=bq_d), 16)
            sw.inc(sy.dma_start(out=biasv_s[1][:], in_=bk_d), 16)
            sw.inc(sy.dma_start(out=biasv_s[2][:], in_=bv_d), 16)
        const_w = sw.n

        for it in range(NT):
            t_base = it * T
            first_tile = (it == 0)

            for m in range(NM):
                slot = m % 2

                @block.gpsimd
                def _(gp, m=m, slot=slot):
                    ring_wait(gp, ("xbf", slot))
                    s_xbf[slot].inc(gp.dma_start(
                        out=xbf[slot][:],
                        in_=x_d[t_base + m * 128: t_base + (m + 1) * 128, :]), 16)
                x_cnt = s_xbf[slot].n

                @block.vector
                def _(v, slot=slot, x_cnt=x_cnt):
                    v.wait_ge(s_xbf[slot].h, x_cnt)
                    for j in range(8):
                        ii = v.bn_stats(stats[:, j * 6:(j + 1) * 6],
                                        xbf[slot][:, j * 512:(j + 1) * 512])
                    sv.inc(ii)
                    v.wait_ge(sv.h, sv.n)
                    sv.inc(v.bn_aggr(mv[:], stats[:]))
                    v.wait_ge(sv.h, sv.n)
                    sv.inc(v.tensor_scalar(veps[:], mv[:, 1:2], EPS, None, ALU.add))
                v_veps = sv.n

                @block.scalar
                def _(s, v_veps=v_veps):
                    s.wait_ge(sv.h, v_veps)
                    ss.inc(s.activation(sqv[:], veps[:], AF.Ln))
                    s.wait_ge(ss.h, ss.n)
                    ss.inc(s.activation(rsig[:], sqv[:], AF.Exp, scale=-0.5))
                s_sq = ss.n

                @block.vector
                def _(v, slot=slot, s_sq=s_sq, it=it, m=m):
                    v.wait_ge(ss.h, s_sq)
                    if it == 0 and m == 0:
                        v.wait_ge(sw.h, const_w)
                    sv.inc(v.tensor_scalar(Dm[slot][:], identb[:], rsig[:, 0:1],
                                           None, ALU.mult))
                    sv.inc(v.tensor_scalar(tmp1[:], mv[:, 0:1], rsig[:, 0:1], -1.0,
                                           ALU.mult, ALU.mult))
                    v.wait_ge(sv.h, sv.n)
                    sv.inc(v.tensor_copy(tmpb[:], tmp1[:]))
                v_dm = sv.n

                @block.tensor
                def _(t, slot=slot, v_dm=v_dm, m=m, it=it):
                    if it == 0 and m == 0:
                        t.wait_ge(sw.h, const_w)
                    if it > 0 and m == 0:
                        t.wait_ge(sv.h, rings[("tile_ve",)][1])
                        t.wait_ge(ss.h, rings[("tile_se",)][1])
                    t.wait_ge(sv.h, v_dm)
                    st.inc(t.matmul(P[2][0:1, 0:128], tmpb[:], identb[:],
                                    start=True, stop=True))
                t_row = st.n

                @block.vector
                def _(v, slot=slot, t_row=t_row):
                    v.wait_ge(st.h, t_row)
                    sv.inc(v.tensor_copy(mrowS[slot][:], P[2][0:1, 0:128]))
                v_mrow = sv.n

                if dbg and it == 0 and m == 0:
                    @block.vector
                    def _(v):
                        v.wait_ge(sv.h, v_mrow)
                        v.tensor_copy(lnds[:, 0:2], mv[:])
                        v.tensor_copy(lnds[:, 2:3], veps[:])
                        v.tensor_copy(lnds[:, 3:4], sqv[:])
                        v.tensor_copy(lnds[:, 4:5], rsig[:])
                        v.tensor_copy(lnds[:, 5:6], tmp1[:])
                        sv.inc(v.tensor_copy(mrowf[:], mrowS[0][:]))

                    @block.gpsimd
                    def _(gp):
                        gp.wait_ge(sv.h, sv.n)
                        sd.inc(gp.dma_start(out=dbg_d["lnd"], in_=lnds[:]), 16)
                        sd.inc(gp.dma_start(out=dbg_d["mrow"], in_=mrowf[:]), 16)

                for cg in range(8):
                    bank = cg % 2

                    @block.tensor
                    def _(t, cg=cg, bank=bank, slot=slot, v_mrow=v_mrow, m=m):
                        t.wait_ge(sv.h, v_mrow)
                        ring_wait(t, ("pln", bank))
                        for cs in range(4):
                            c = cg * 4 + cs
                            t.matmul(P[bank][:, cs * 128:(cs + 1) * 128],
                                     xbf[slot][:, c * 128:(c + 1) * 128],
                                     Dm[slot][:], start=(cs == 0), stop=False)
                            ii = t.matmul(P[bank][:, cs * 128:(cs + 1) * 128],
                                          onesb[0:1, :], mrowS[slot][:],
                                          start=False, stop=(cs == 3))
                        st.inc(ii)
                    t_dmm = st.n

                    @block.vector
                    def _(v, cg=cg, bank=bank, t_dmm=t_dmm, m=m):
                        v.wait_ge(st.h, t_dmm)
                        dst = Xsb.ap()[:, cg * 4:(cg + 1) * 4, m * 128:(m + 1) * 128]
                        sv.inc(v.tensor_copy(dst, P[bank][:].rearrange("p (c t) -> p c t", c=4)))
                    ring_set(("pln", bank), sv)
                ring_set(("xbf", slot), st)
            ln_ve_last = sv.n

            flat = 0
            for w in range(3):
                dstt = [Qsb, Ksb, VAsb][w]
                for ko in range(NKC):
                    bank = 3 + (ko % 2)
                    for subq in range(2):
                        fslot = flat % WF_R

                        @block.gpsimd
                        def _(gp, w=w, ko=ko, subq=subq, fslot=fslot):
                            ring_wait(gp, ("wbb", fslot))
                            s_wbb[fslot].inc(gp.dma_start(
                                out=wbb[fslot].ap().rearrange("p (a c) -> p a c", a=16),
                                in_=qkv_w_d[w][ko][:, subq * 16:(subq + 1) * 16, :]), 16)
                        w_cnt = s_wbb[fslot].n

                        @block.tensor
                        def _(t, w=w, ko=ko, subq=subq, bank=bank, fslot=fslot,
                              w_cnt=w_cnt):
                            t.wait_ge(s_wbb[fslot].h, w_cnt)
                            if w == 0 and ko == 0 and subq == 0:
                                t.wait_ge(sv.h, ln_ve_last)
                            if subq == 0:
                                ring_wait(t, ("pqkv", bank))
                            for kl in range(16):
                                ki = subq * 16 + kl
                                ii = t.matmul(P[bank][:],
                                              wbb[fslot][:, kl * 128:(kl + 1) * 128],
                                              Xsb.ap()[:, ki, :],
                                              start=(ki == 0), stop=(ki == NKC - 1))
                            st.inc(ii)
                        ring_set(("wbb", fslot), st)
                        flat += 1
                    t_proj = st.n

                    @block.vector
                    def _(v, w=w, ko=ko, bank=bank, t_proj=t_proj, dstt=dstt, it=it):
                        v.wait_ge(st.h, t_proj)
                        if it == 0 and w == 0 and ko == 0:
                            v.wait_ge(sw.h, const_w)
                        dst = dstt.ap().rearrange("p (t h) -> p t h", h=32)[:, :, ko]
                        sv.inc(v.tensor_scalar(dst, P[bank][:],
                                               biasv_s[w][:, ko:ko + 1], None, ALU.add))
                    ring_set(("pqkv", bank), sv)
            qkv_ve = sv.n

            if dbg and first_tile:
                @block.gpsimd
                def _(gp):
                    gp.wait_ge(sv.h, qkv_ve)
                    sd.inc(gp.dma_start(out=dbg_d["q"], in_=Qsb[:]), 16)
                    sd.inc(gp.dma_start(out=dbg_d["k"], in_=Ksb[:]), 16)
                    sd.inc(gp.dma_start(out=dbg_d["v"], in_=VAsb[:]), 16)
                    sd.inc(gp.dma_start(
                        out=dbg_d["xn"].rearrange("p (c t) -> p c t", c=NKC),
                        in_=Xsb[:]), 16)

            TS = [0] * NB; TZ = [0] * NB; TVT = [0] * NB; TAV = [0] * NB
            TAVT = [0] * NB
            VEind = [0] * NB; VEme = [0] * NB; VErec = [0] * NB
            VEdrec = [0] * NB; VEao = [0] * NB
            SEexp = [0] * NB; SEvq = [0] * NB; SEau = [0] * NB

            for b in range(NB):
                sp = b % 2

                @block.tensor
                def _(t, b=b, sp=sp):
                    if b >= 2:
                        t.wait_ge(ss.h, SEexp[b - 2])
                        t.wait_ge(sv.h, VEind[b - 2])
                    else:
                        t.wait_ge(sv.h, qkv_ve)
                    for q in range(4):
                        t0 = b * 16 + q * 4
                        ii = t.matmul(P[sp][:, q * 128:(q + 1) * 128],
                                      Ksb[:, t0 * 32:(t0 + 4) * 32],
                                      Qsb[:, t0 * 32:(t0 + 4) * 32],
                                      start=(q == 0), stop=(q == 3))
                    st.inc(ii)
                TS[b] = st.n

                @block.scalar
                def _(s, b=b, sp=sp):
                    s.wait_ge(st.h, TS[b])
                    if b >= 1:
                        s.wait_ge(sv.h, VEme[b - 1])
                    ss.inc(s.activation(e_t[:], P[sp][:], AF.Exp, scale=INV_SQRT_D))
                SEexp[b] = ss.n

                @block.vector
                def _(v, b=b, sp=sp, it=it):
                    v.wait_ge(st.h, TS[b])
                    if it == 0 and b == 0:
                        v.wait_ge(sw.h, const_w)
                    sv.inc(v.tensor_tensor(ind_t[:], P[sp][:], thb[:], ALU.is_gt))
                VEind[b] = sv.n

                @block.vector
                def _(v, b=b, sp=sp):
                    v.wait_ge(ss.h, SEexp[b])
                    if b >= 2:
                        v.wait_ge(st.h, TAV[b - 2])
                    sv.inc(v.tensor_tensor(me_t[sp][:], e_t[:], ind_t[:], ALU.mult))
                VEme[b] = sv.n

                if dbg and first_tile and b == 0:
                    @block.gpsimd
                    def _(gp):
                        gp.wait_ge(sv.h, VEme[0])
                        sd.inc(gp.dma_start(out=dbg_d["me"], in_=me_t[0][:]), 16)

                @block.tensor
                def _(t, b=b, sp=sp):
                    t.wait_ge(sv.h, VEme[b])
                    if b >= 2:
                        t.wait_ge(sv.h, VErec[b - 2])
                    for q in range(4):
                        ii = t.matmul(P[2 + sp][:, q:q + 1],
                                      me_t[sp][:, q * 128:(q + 1) * 128],
                                      onesb[:, 0:1], start=(q == 0), stop=(q == 3))
                    st.inc(ii)
                    if b >= 1:
                        t.wait_ge(sv.h, SEvq[b - 1])
                    for q in range(4):
                        t0 = b * 16 + q * 4
                        ii = t.matmul(P[4][:, q * 128:(q + 1) * 128],
                                      VAsb[:, t0 * 32:(t0 + 4) * 32],
                                      identb[:], start=(q == 0), stop=(q == 3))
                    st.inc(ii)
                TZ[b] = st.n - 1
                TVT[b] = st.n

                @block.vector
                def _(v, b=b, sp=sp):
                    v.wait_ge(st.h, TVT[b])
                    sv.inc(v.tensor_copy(vqs[sp][:], P[4][:]))
                SEvq[b] = sv.n

                @block.tensor
                def _(t, b=b, sp=sp):
                    t.wait_ge(sv.h, SEvq[b])
                    if b >= 2:
                        t.wait_ge(sv.h, SEau[b - 2])
                    for q in range(4):
                        ii = t.matmul(P[5 + sp][:, q * 128:(q + 1) * 128],
                                      me_t[sp][:, q * 128:(q + 1) * 128],
                                      vqs[sp][:, q * 128:(q + 1) * 128],
                                      start=(q == 0), stop=(q == 3))
                    st.inc(ii)
                TAV[b] = st.n

                @block.vector
                def _(v, b=b, sp=sp):
                    v.wait_ge(st.h, TZ[b])
                    sv.inc(v.reciprocal(rec_t[sp][:], P[2 + sp][:, 0:4]))
                VErec[b] = sv.n

                @block.vector
                def _(v, b=b, sp=sp):
                    v.wait_ge(sv.h, VErec[b])
                    if b >= 2:
                        v.wait_ge(st.h, TAVT[b - 2])
                    for q in range(4):
                        ii = v.tensor_scalar(drec[sp][:, q * 128:(q + 1) * 128],
                                             identb[:], rec_t[sp][:, q:q + 1],
                                             None, ALU.mult)
                    sv.inc(ii)
                VEdrec[b] = sv.n

                @block.vector
                def _(v, b=b, sp=sp):
                    v.wait_ge(st.h, TAV[b])
                    sv.inc(v.tensor_copy(avus[sp][:], P[5 + sp][:]))
                SEau[b] = sv.n

                @block.tensor
                def _(t, b=b, sp=sp):
                    t.wait_ge(sv.h, SEau[b])
                    t.wait_ge(sv.h, VEdrec[b])
                    if b >= 1:
                        t.wait_ge(sv.h, VEao[b - 1])
                    for q in range(4):
                        ii = t.matmul(P[7][:, q * 128:(q + 1) * 128],
                                      avus[sp][:, q * 128:(q + 1) * 128],
                                      drec[sp][:, q * 128:(q + 1) * 128],
                                      start=(q == 0), stop=(q == 3))
                    st.inc(ii)
                TAVT[b] = st.n

                @block.vector
                def _(v, b=b, sp=sp):
                    v.wait_ge(st.h, TAVT[b])
                    sv.inc(v.tensor_copy(VAsb[:, b * 512:(b + 1) * 512], P[7][:]))
                VEao[b] = sv.n

            attn_ve = sv.n
            attn_se = ss.n

            if dbg and first_tile:
                @block.gpsimd
                def _(gp):
                    gp.wait_ge(sv.h, attn_ve)
                    sd.inc(gp.dma_start(out=dbg_d["ao"], in_=VAsb[:]), 16)

            @block.gpsimd
            def _(gp):
                ring_wait(gp, ("gwr",))
                s_gw.inc(gp.dma_start(out=gwr[:], in_=gw_d), 16)
            w_gw = s_gw.n

            @block.tensor
            def _(t, w_gw=w_gw):
                t.wait_ge(s_gw.h, w_gw)
                t.wait_ge(sv.h, attn_ve)
                t.wait_ge(ss.h, attn_se)
                for ki in range(NKC):
                    for m in range(NM):
                        ii = t.matmul(P[0][:, m * NE:(m + 1) * NE],
                                      Xsb.ap()[:, ki, m * 128:(m + 1) * 128],
                                      gwr.ap()[:, ki, :],
                                      start=(ki == 0 and m == 0),
                                      stop=(ki == NKC - 1))
                st.inc(ii)
            ring_set(("gwr",), st)
            t_logits = st.n

            @block.vector
            def _(v):
                v.wait_ge(st.h, t_logits)
                sv.inc(v.tensor_copy(Lsb[:], P[0][:, 0:4 * NE]))
            v_logits = sv.n

            if dbg and it == 0:
                @block.gpsimd
                def _(gp):
                    gp.wait_ge(sv.h, v_logits)
                    sd.inc(gp.dma_start(out=dbg_d["lg"], in_=Lsb[:]), 16)
                    sd.inc(gp.dma_start(
                        out=dbg_d["xn2"].rearrange("p (c t) -> p c t", c=NKC),
                        in_=Xsb[:]), 16)
                    sd.inc(gp.dma_start(out=dbg_d["gw0"][:, 0:NE], in_=gwr.ap()[:, 30, :]), 16)
                    sd.inc(gp.dma_start(out=dbg_d["gw0"][:, NE:2*NE], in_=gwr.ap()[:, 31, :]), 16)

            s_exp_prev = [0]
            for m in range(NM):
                L = Lsb[:, m * NE:(m + 1) * NE]

                @block.vector
                def _(v, m=m, L=L):
                    if m == 0:
                        v.wait_ge(sv.h, v_logits)
                    else:
                        v.wait_ge(ss.h, s_exp_prev[0])
                    sv.inc(v.reduce_max(m1p[:], L, axis=AX.X))
                    v.wait_ge(sv.h, sv.n)
                    sv.inc(v.tensor_scalar(nm1[:], m1p[:], -1.0, None, ALU.mult))
                    v.wait_ge(sv.h, sv.n)
                    sv.inc(v.tensor_scalar(t1s[:], L, nm1[:, 0:1], None, ALU.add))
                    v.wait_ge(sv.h, sv.n)
                    sv.inc(v.tensor_single_scalar(c1s[:], t1s[:], 0.0, ALU.is_ge))
                    v.wait_ge(sv.h, sv.n)
                    sv.inc(v.scalar_tensor_tensor(maskd[:], c1s[:], BIGNEG, L,
                                                  ALU.mult, ALU.add))
                    v.wait_ge(sv.h, sv.n)
                    sv.inc(v.reduce_max(m2p[:], maskd[:], axis=AX.X))
                    v.wait_ge(sv.h, sv.n)
                    sv.inc(v.tensor_scalar(nm2[:], m2p[:], -1.0, None, ALU.mult))
                    sv.inc(v.tensor_sub(d21[:], m2p[:], m1p[:]))
                    v.wait_ge(sv.h, sv.n)
                v_m2 = sv.n

                @block.scalar
                def _(s, m=m, L=L, v_m2=v_m2):
                    s.wait_ge(sv.h, v_m2)
                    ss.inc(s.activation(els[:], L, AF.Exp, bias=nm1[:, 0:1]))
                    ss.inc(s.activation(em2[:], d21[:], AF.Exp))
                s_exp = ss.n
                s_exp_prev[0] = s_exp

                if dbg and it == 0 and m == 0:
                    @block.vector
                    def _(v):
                        v.wait_ge(ss.h, s_exp)
                        v.tensor_copy(t2ds[:, 0:1], m1p[:])
                        v.tensor_copy(t2ds[:, 1:2], m2p[:])
                        v.tensor_copy(t2ds[:, 2:3], d21[:])
                        v.tensor_copy(t2ds[:, 3:4], em2[:])
                        v.tensor_copy(t2ds[:, 4:5], els[:, 0:1])
                        sv.inc(v.tensor_copy(t2ds[:, 5:6], c1s[:, 0:1]))

                    @block.gpsimd
                    def _(gp):
                        gp.wait_ge(sv.h, sv.n)
                        sd.inc(gp.dma_start(out=dbg_d["t2d"], in_=t2ds[:]), 16)

                @block.vector
                def _(v, m=m, L=L, s_exp=s_exp):
                    v.wait_ge(ss.h, s_exp)
                    sv.inc(v.tensor_scalar(den[:], em2[:], 1.0, None, ALU.add))
                    v.wait_ge(sv.h, sv.n)
                    sv.inc(v.reciprocal(rec2[:], den[:]))
                    v.wait_ge(sv.h, sv.n)
                    sv.inc(v.tensor_scalar(c2s[:], L, nm2[:, 0:1], 0.0,
                                           ALU.add, ALU.is_ge))
                    v.wait_ge(sv.h, sv.n)
                    sv.inc(v.tensor_mul(g1s[:], c2s[:], els[:]))
                    v.wait_ge(sv.h, sv.n)
                    sv.inc(v.tensor_scalar(gf[:, m * NE:(m + 1) * NE], g1s[:],
                                           rec2[:, 0:1], None, ALU.mult))

            @block.vector
            def _(v):
                v.wait_ge(sv.h, sv.n)
                sv.inc(v.tensor_copy(gfb[:], gf[:]))
            v_gfb = sv.n

            if dbg and first_tile:
                @block.gpsimd
                def _(gp):
                    gp.wait_ge(sv.h, v_gfb)
                    sd.inc(gp.dma_start(out=dbg_d["gf"], in_=gf[:]), 16)

            @block.tensor
            def _(t):
                t.wait_ge(sv.h, v_gfb)
                for m in range(NM):
                    ii = t.matmul(P[1][0:NE, m * 128:(m + 1) * 128],
                                  gfb[:, m * NE:(m + 1) * NE], identb[:],
                                  start=(m == 0), stop=(m == NM - 1))
                st.inc(ii)
            t_gft = st.n

            @block.vector
            def _(v):
                v.wait_ge(st.h, t_gft)
                sv.inc(v.tensor_copy(gfTs[:], P[1][0:NE, :]))
            v_gft = sv.n

            @block.tensor
            def _(t):
                t.wait_ge(sv.h, v_gft)
                st.inc(t.matmul(P[2][:], e2b[:, 0:128], gfTs[:], start=True, stop=True))
                st.inc(t.matmul(P[3][0:96, :], e2b[:, 128:224], gfTs[:],
                                start=True, stop=True))
            t_ge = st.n

            @block.vector
            def _(v):
                v.wait_ge(st.h, t_ge)
                sv.inc(v.tensor_copy(ge0[:], P[2][:]))
                sv.inc(v.tensor_copy(ge1[:], P[3][0:96, :]))
            v_ge = sv.n

            for ki in range(NKC):
                aslot = ki % 6

                @block.gpsimd
                def _(gp, ki=ki, aslot=aslot):
                    ring_wait(gp, ("a2b", aslot))
                    s_a2[aslot].inc(gp.dma_start(out=a2b[aslot][:], in_=a2_d[ki]), 16)
                s_cnt = s_a2[aslot].n

                @block.tensor
                def _(t, ki=ki, aslot=aslot, s_cnt=s_cnt):
                    t.wait_ge(s_a2[aslot].h, s_cnt)
                    if ki == 0:
                        t.wait_ge(sv.h, attn_ve)
                        t.wait_ge(ss.h, attn_se)
                    t.matmul(P[4][:], a2b[aslot][:, 0:128], Xsb.ap()[:, ki, :],
                             start=(ki == 0), stop=(ki == NKC - 1))
                    ii = t.matmul(P[5][0:96, :], a2b[aslot][:, 128:224],
                                  Xsb.ap()[:, ki, :],
                                  start=(ki == 0), stop=(ki == NKC - 1))
                    st.inc(ii)
                ring_set(("a2b", aslot), st)
            t_h = st.n

            @block.vector
            def _(v):
                v.wait_ge(st.h, t_h)
                v.wait_ge(sv.h, v_ge)
                sv.inc(v.tensor_tensor(hs0[:], P[4][:], ge0[:], ALU.mult))
                sv.inc(v.tensor_tensor(hs1[:], P[5][0:96, :], ge1[:], ALU.mult))
            v_hs = sv.n

            if dbg and first_tile:
                @block.gpsimd
                def _(gp):
                    gp.wait_ge(sv.h, v_hs)
                    sd.inc(gp.dma_start(out=dbg_d["hs0"], in_=hs0[:]), 16)
                    sd.inc(gp.dma_start(out=dbg_d["hs1"], in_=hs1[:]), 16)

            ev_done = [0] * NOG
            oflat = 0
            for n in range(NOG):
                par = 4 * (n % 2)
                bslot_bo = n % 2

                @block.gpsimd
                def _(gp, n=n, bslot_bo=bslot_bo):
                    ring_wait(gp, ("bo", bslot_bo))
                    s_bo[bslot_bo].inc(gp.dma_start(
                        out=bo_bf[bslot_bo][:],
                        in_=bo_d[:, n * 512:(n + 1) * 512]), 16)
                x_bo = s_bo[bslot_bo].n

                @block.gpsimd
                def _(gp, n=n):
                    ring_wait(gp, ("b2b",))
                    s_b2.inc(gp.dma_start(out=b2b0[:], in_=b20_d[n]), 16)
                    s_b2.inc(gp.dma_start(out=b2b1[:], in_=b21_d[n]), 16)
                s_b2n = s_b2.n

                @block.sync
                def _(sy, n=n):
                    for m in range(NM):
                        sy.wait_ge(s_ost[m].h, s_ost[m].n)
                        sw.inc(sy.dma_start(
                            out=xr[m][:],
                            in_=x_d[t_base + m * 128: t_base + (m + 1) * 128,
                                    n * 512:(n + 1) * 512]), 16)
                w_xr = sw.n

                for kk in range(19):
                    if kk < 8:
                        src_d, kind, base_k = wo_d[n][:, kk * 4:kk * 4 + 4, :], "w", kk * 4
                    elif kk == 8:
                        kind = "bo"
                    elif kk < 17:
                        c = kk - 9
                        src_d, kind, base_k = wb_d[n][:, c * 4:c * 4 + 4, :], "w", 33 + c * 4
                    elif kk == 17:
                        kind = "b20"
                    else:
                        kind = "b21"

                    if kind == "w":
                        bslot = oflat % OB_R

                        @block.gpsimd
                        def _(gp, src_d=src_d, bslot=bslot):
                            ring_wait(gp, ("obf", bslot))
                            s_obf[bslot].inc(gp.dma_start(
                                out=obf[bslot][:], in_=src_d), 16)
                        rdy = (s_obf[bslot], s_obf[bslot].n)

                        @block.tensor
                        def _(t, kk=kk, n=n, par=par, bslot=bslot, rdy=rdy,
                              base_k=base_k):
                            t.wait_ge(rdy[0].h, rdy[1])
                            if kk == 0:
                                if n <= 1:
                                    t.wait_ge(sv.h, v_hs)
                                else:
                                    t.wait_ge(sv.h, ev_done[n - 2])
                            for kl in range(4):
                                kabs = base_k + kl
                                for m in range(NM):
                                    if kabs < 32:
                                        lhsT = VAsb.ap().rearrange(
                                            "p (t h) -> p t h",
                                            h=32)[:, m * 128:(m + 1) * 128, kabs]
                                    else:
                                        lhsT = Xsb.ap()[:, kabs - 33, m * 128:(m + 1) * 128]
                                    ii = t.matmul(P[par + m][:], lhsT,
                                                  obf[bslot].ap()[:, kl, :],
                                                  start=(kabs == 0), stop=False)
                            st.inc(ii)
                        ring_set(("obf", bslot), st)
                        oflat += 1
                    elif kind == "bo":
                        @block.tensor
                        def _(t, n=n, par=par, x_bo=x_bo, bslot_bo=bslot_bo):
                            t.wait_ge(s_bo[bslot_bo].h, x_bo)
                            for m in range(NM):
                                t.matmul(P[par + m][:], onesb[0:1, 0:128],
                                         bo_bf[bslot_bo][:], start=False, stop=False)
                    elif kind == "b20":
                        @block.tensor
                        def _(t, par=par, s_b2n=s_b2n):
                            t.wait_ge(s_b2.h, s_b2n)
                            for m in range(NM):
                                t.matmul(P[par + m][:], hs0[:, m * 128:(m + 1) * 128],
                                         b2b0[:], start=False, stop=False)
                    else:
                        @block.tensor
                        def _(t, par=par):
                            for m in range(NM):
                                ii = t.matmul(P[par + m][:],
                                              hs1[:, m * 128:(m + 1) * 128],
                                              b2b1[:], start=False, stop=True)
                            st.inc(ii)
                t_out_n = st.n
                ring_set(("b2b",), st)
                ring_set(("bo", bslot_bo), st)

                @block.vector
                def _(v, n=n, par=par, t_out_n=t_out_n, w_xr=w_xr):
                    v.wait_ge(st.h, t_out_n)
                    v.wait_ge(sw.h, w_xr)
                    for m in range(NM):
                        sv.inc(v.tensor_add(xr[m][:], P[par + m][:], xr[m][:]))
                        ring_set(("xr", m), sv)
                ev_done[n] = sv.n

                @block.gpsimd
                def _(gp, n=n, t_base=t_base):
                    gp.wait_ge(sv.h, ev_done[n])
                    for m in range(NM):
                        s_ost[m].inc(gp.dma_start(
                            out=y_d[t_base + m * 128: t_base + (m + 1) * 128,
                                    n * 512:(n + 1) * 512],
                            in_=xr[m][:]), 16)

            ring_set(("tile_ve",), sv)
            ring_set(("tile_se",), ss)

        @block.gpsimd
        def _(gp):
            for m in range(4):
                gp.wait_ge(s_ost[m].h, s_ost[m].n)
            if dbg:
                gp.wait_ge(sd.h, sd.n)

    return nc


def prep_inputs(inputs):
    Wq = np.asarray(inputs["L0_attn_Wq"]); Wk = np.asarray(inputs["L0_attn_Wk"])
    Wv = np.asarray(inputs["L0_attn_Wv"]); Wo = np.asarray(inputs["L0_attn_Wo"])
    baseW = np.asarray(inputs["L0_ffn_base_W"])
    gateW = np.asarray(inputs["L0_ffn_gate_W"])
    loraA = np.asarray(inputs["L0_ffn_lora_A"]); loraB = np.asarray(inputs["L0_ffn_lora_B"])

    def qkv_tiles(W):
        WT = np.ascontiguousarray(W.T)
        return np.ascontiguousarray(WT.reshape(NKC, 128, NKC, 128).transpose(2, 1, 0, 3))

    def out_tiles(W):
        WT = np.ascontiguousarray(W.T)
        return np.ascontiguousarray(WT.reshape(NKC, 128, NOG, 512).transpose(2, 1, 0, 3))

    A2 = loraA.transpose(1, 0, 2).reshape(224, HID)
    a2t = np.ascontiguousarray(A2.T.reshape(NKC, 128, 224))
    B2 = loraB.transpose(2, 0, 1).reshape(224, HID)
    b20 = np.ascontiguousarray(B2[:128].reshape(128, NOG, 512).transpose(1, 0, 2))
    b21 = np.ascontiguousarray(B2[128:].reshape(96, NOG, 512).transpose(1, 0, 2))
    gw = np.ascontiguousarray(gateW.T.reshape(NKC, 128, NE).transpose(1, 0, 2))

    def bias_cols(b):
        return np.ascontiguousarray(np.asarray(b).reshape(NKC, 128).T)

    e2 = np.zeros((NE, 224), dtype=BF)
    for r in range(RK):
        for e in range(NE):
            e2[e, r * NE + e] = BF(2.0)
    ident = np.eye(128, dtype=BF)
    onesc = np.ones((128, 128), dtype=BF)
    pat = np.full((128, 128), 1.0e30, dtype=BF)
    for tt in range(4):
        pat[tt * 32:(tt + 1) * 32, tt * 32:(tt + 1) * 32] = BF(TH_RAW)
    thb = np.ascontiguousarray(np.tile(pat, (1, 4)))

    return {
        "wq": qkv_tiles(Wq), "wk": qkv_tiles(Wk), "wv": qkv_tiles(Wv),
        "wo": out_tiles(Wo), "wb": out_tiles(baseW),
        "a2": a2t, "b20": b20, "b21": b21, "gw": gw,
        "bq": bias_cols(inputs["L0_attn_bq"]),
        "bk": bias_cols(inputs["L0_attn_bk"]),
        "bv": bias_cols(inputs["L0_attn_bv"]),
        "bo": np.ascontiguousarray(np.asarray(inputs["L0_attn_bo"])[None, :]),
        "idq": ident, "onesc": onesc, "e2": e2, "thb": thb,
    }


def kernel(**inputs):
    ncores = 8
    ntok = 2048
    T = 512
    x_full = np.ascontiguousarray(
        np.asarray(inputs["L0_attn_x"], dtype=np.float32).reshape(16384, HID))
    wmap = prep_inputs(inputs)
    nc = build_program(ntok, T, dbg=False)
    in_maps = []
    for c in range(ncores):
        m = dict(wmap)
        m["x"] = x_full[c * ntok:(c + 1) * ntok]
        in_maps.append(m)
    res = run_bass_kernel_spmd(nc, in_maps, core_ids=list(range(ncores)))
    out = np.concatenate([np.asarray(r["y"]) for r in res.results], axis=0)
    return out.reshape(4, 4096, HID)


# version 20
# speedup vs baseline: 1.0491x; 1.0166x over previous
import numpy as np
import ml_dtypes
from contextlib import ExitStack

import concourse.bass as bass
import concourse.mybir as mybir
from concourse.bass_utils import run_bass_kernel_spmd

F32 = mybir.dt.float32
BF16 = mybir.dt.bfloat16
AF = mybir.ActivationFunctionType
ALU = mybir.AluOpType
AX = mybir.AxisListType
BF = ml_dtypes.bfloat16

HID = 4096
NH = 32
NE = 28
RK = 8
NKC = 32
NOG = 8
TH_RAW = float(np.float32(0.001) * np.float32(np.sqrt(np.float32(128.0))))
INV_SQRT_D = float(np.float32(1.0) / np.float32(np.sqrt(np.float32(128.0))))
EPS = 1e-5
BIGNEG = -1.0e30




class Sem:
    def __init__(self, es, nc, name):
        self.h = es.enter_context(nc.semaphore(name))
        self.n = 0

    def inc(self, instr, k=1):
        instr.then_inc(self.h, k)
        self.n += k
        return self.n


def build_program(ntok, T, dbg=False):
    nc = bass.Bass("TRN2", target_bir_lowering=False)
    NT = ntok // T
    NM = T // 128
    NB = T // 16

    x_d = nc.dram_tensor("x", [ntok, HID], F32, kind="ExternalInput").ap()
    y_d = nc.dram_tensor("y", [ntok, HID], F32, kind="ExternalOutput").ap()
    wq_d = nc.dram_tensor("wq", [NKC, 128, NKC, 128], F32, kind="ExternalInput").ap()
    wk_d = nc.dram_tensor("wk", [NKC, 128, NKC, 128], F32, kind="ExternalInput").ap()
    wv_d = nc.dram_tensor("wv", [NKC, 128, NKC, 128], F32, kind="ExternalInput").ap()
    wo_d = nc.dram_tensor("wo", [NOG, 128, NKC, 512], F32, kind="ExternalInput").ap()
    wb_d = nc.dram_tensor("wb", [NOG, 128, NKC, 512], F32, kind="ExternalInput").ap()
    a2_d = nc.dram_tensor("a2", [NKC, 128, 224], F32, kind="ExternalInput").ap()
    b20_d = nc.dram_tensor("b20", [NOG, 128, 512], F32, kind="ExternalInput").ap()
    b21_d = nc.dram_tensor("b21", [NOG, 96, 512], F32, kind="ExternalInput").ap()
    gw_d = nc.dram_tensor("gw", [128, NKC, NE], F32, kind="ExternalInput").ap()
    bq_d = nc.dram_tensor("bq", [128, NKC], F32, kind="ExternalInput").ap()
    bk_d = nc.dram_tensor("bk", [128, NKC], F32, kind="ExternalInput").ap()
    bv_d = nc.dram_tensor("bv", [128, NKC], F32, kind="ExternalInput").ap()
    bo_d = nc.dram_tensor("bo", [1, HID], F32, kind="ExternalInput").ap()
    idq_d = nc.dram_tensor("idq", [128, 128], BF16, kind="ExternalInput").ap()
    ones_d = nc.dram_tensor("onesc", [128, 128], BF16, kind="ExternalInput").ap()
    e2_d = nc.dram_tensor("e2", [NE, 224], BF16, kind="ExternalInput").ap()
    thb_d = nc.dram_tensor("thb", [128, 512], BF16, kind="ExternalInput").ap()
    qkv_w_d = [wq_d, wk_d, wv_d]
    dbg_d = {}
    if dbg:
        for nm, shp in [("xn", [128, NKC * T]), ("q", [128, T * 32]),
                        ("k", [128, T * 32]), ("v", [128, T * 32]),
                        ("ao", [128, T * 32]), ("gf", [128, 4 * NE]),
                        ("hs0", [128, T]), ("hs1", [96, T]), ("me", [128, 512]),
                        ("lnd", [128, 8]), ("mrow", [1, 128]),
                        ("lg", [128, 4 * NE]), ("t2d", [128, 8]),
                        ("xn2", [128, NKC * T]), ("gw0", [128, 2 * NE])]:
            dbg_d[nm] = nc.dram_tensor("d_" + nm, shp, F32, kind="ExternalOutput").ap()

    es = ExitStack()

    def sbt(name, shape, dt):
        return es.enter_context(nc.sbuf_tensor(name, shape, dt))

    Xsb = sbt("Xsb", [128, NKC, T], BF16)
    Qsb = sbt("Qsb", [128, T * 32], BF16)
    Ksb = sbt("Ksb", [128, T * 32], BF16)
    VAsb = sbt("VAsb", [128, T * 32], BF16)
    xbf = [sbt(f"xbf{i}", [128, HID], BF16) for i in range(2)]
    WF_R, OB_R = 3, 3
    wbb = [sbt(f"wbb{i}", [128, 2048], BF16) for i in range(WF_R)]
    obf = [sbt(f"obf{i}", [128, 4, 512], BF16) for i in range(OB_R)]
    a2b = [sbt(f"a2b{i}", [128, 224], BF16) for i in range(6)]
    b2b0 = sbt("b2b0", [128, 512], BF16)
    b2b1 = sbt("b2b1", [96, 512], BF16)
    gwr = sbt("gwr", [128, NKC, NE], BF16)
    bo_bf = [sbt(f"bo_bf{i}", [1, 512], BF16) for i in range(2)]
    identb = sbt("identb", [128, 128], BF16)
    onesb = sbt("onesb", [128, 128], BF16)
    e2b = sbt("e2b", [NE, 224], BF16)
    thb = sbt("thbs", [128, 512], BF16)
    biasv_s = [sbt(f"bias{i}", [128, NKC], F32) for i in range(3)]
    stats = sbt("stats", [128, 48], F32)
    mv = sbt("mv", [128, 2], F32)
    veps = sbt("veps", [128, 1], F32)
    sqv = sbt("sqv", [128, 1], F32)
    rsig = sbt("rsig", [128, 1], F32)
    tmp1 = sbt("tmp1", [128, 1], F32)
    tmpb = sbt("tmpb", [128, 1], BF16)
    Dm = [sbt(f"Dm{i}", [128, 128], BF16) for i in range(2)]
    mrowS = [sbt(f"mrowS{i}", [1, 128], BF16) for i in range(2)]
    e_t = sbt("e_t", [128, 512], BF16)
    ind_t = sbt("ind_t", [128, 512], BF16)
    me_t = [sbt(f"me{i}", [128, 512], BF16) for i in range(2)]
    vqs = [sbt(f"vqs{i}", [128, 512], BF16) for i in range(2)]
    avus = [sbt(f"avus{i}", [128, 512], BF16) for i in range(2)]
    drec = [sbt(f"drec{i}", [128, 512], BF16) for i in range(2)]
    rec_t = [sbt(f"rec{i}", [128, 4], F32) for i in range(2)]
    Lsb = sbt("Lsb", [128, 4 * NE], F32)
    m1p = sbt("m1p", [128, 1], F32)
    nm1 = sbt("nm1", [128, 1], F32)
    m2p = sbt("m2p", [128, 1], F32)
    nm2 = sbt("nm2", [128, 1], F32)
    t1s = sbt("t1s", [128, NE], F32)
    c1s = sbt("c1s", [128, NE], F32)
    maskd = sbt("maskd", [128, NE], F32)
    d21 = sbt("d21", [128, 1], F32)
    em2 = sbt("em2", [128, 1], F32)
    den = sbt("den", [128, 1], F32)
    rec2 = sbt("rec2", [128, 1], F32)
    els = sbt("els", [128, NE], F32)
    c2s = sbt("c2s", [128, NE], F32)
    g1s = sbt("g1s", [128, NE], F32)
    gf = sbt("gf", [128, 4 * NE], F32)
    gfb = sbt("gfb", [128, 4 * NE], BF16)
    gfTs = sbt("gfTs", [NE, T], BF16)
    ge0 = sbt("ge0", [128, T], BF16)
    ge1 = sbt("ge1", [96, T], BF16)
    hs0 = sbt("hs0", [128, T], BF16)
    hs1 = sbt("hs1", [96, T], BF16)
    if dbg:
        lnds = sbt("lnds", [128, 8], F32)
        mrowf = sbt("mrowf", [1, 128], F32)
        t2ds = sbt("t2ds", [128, 8], F32)
    xr = [sbt(f"xr{i}", [128, 512], F32) for i in range(4)]
    P = [es.enter_context(nc.psum_tensor(f"P{i}", [128, 512], F32)) for i in range(8)]

    sx = Sem(es, nc, "sx")
    sxa = Sem(es, nc, "sxa")
    sxs = Sem(es, nc, "sxs")
    sw = Sem(es, nc, "sw")
    st = Sem(es, nc, "st")
    sv = Sem(es, nc, "sv")
    ss = Sem(es, nc, "ss")
    so = Sem(es, nc, "so")
    sd = Sem(es, nc, "sd")
    s_xbf = [Sem(es, nc, f"s_xbf{i}") for i in range(2)]
    s_wbb = [Sem(es, nc, f"s_wbb{i}") for i in range(4)]
    s_obf = [Sem(es, nc, f"s_obf{i}") for i in range(8)]
    s_a2 = [Sem(es, nc, f"s_a2{i}") for i in range(6)]
    s_gw = Sem(es, nc, "s_gw")
    s_bo = [Sem(es, nc, f"s_bo{i}") for i in range(2)]
    s_b2 = Sem(es, nc, "s_b2")
    s_ost = [Sem(es, nc, f"s_ost{i}") for i in range(4)]

    rings = {}

    def ring_wait(eng, key):
        v = rings.get(key)
        if v is not None:
            eng.wait_ge(v[0].h, v[1])

    def ring_set(key, sem):
        rings[key] = (sem, sem.n)

    block = es.enter_context(nc.Block())

    with es:
        @block.sync
        def _(sy):
            sw.inc(sy.dma_start(out=identb[:], in_=idq_d), 16)
            sw.inc(sy.dma_start(out=onesb[:], in_=ones_d), 16)
            sw.inc(sy.dma_start(out=e2b[:], in_=e2_d), 16)
            sw.inc(sy.dma_start(out=thb[:], in_=thb_d), 16)
            sw.inc(sy.dma_start(out=biasv_s[0][:], in_=bq_d), 16)
            sw.inc(sy.dma_start(out=biasv_s[1][:], in_=bk_d), 16)
            sw.inc(sy.dma_start(out=biasv_s[2][:], in_=bv_d), 16)
        const_w = sw.n

        for it in range(NT):
            t_base = it * T
            first_tile = (it == 0)

            for m in range(NM):
                slot = m % 2

                @block.gpsimd
                def _(gp, m=m, slot=slot):
                    ring_wait(gp, ("xbf", slot))
                    s_xbf[slot].inc(gp.dma_start(
                        out=xbf[slot][:],
                        in_=x_d[t_base + m * 128: t_base + (m + 1) * 128, :]), 16)
                x_cnt = s_xbf[slot].n

                @block.vector
                def _(v, slot=slot, x_cnt=x_cnt):
                    v.wait_ge(s_xbf[slot].h, x_cnt)
                    for j in range(8):
                        ii = v.bn_stats(stats[:, j * 6:(j + 1) * 6],
                                        xbf[slot][:, j * 512:(j + 1) * 512])
                    sv.inc(ii)
                    v.wait_ge(sv.h, sv.n)
                    sv.inc(v.bn_aggr(mv[:], stats[:]))
                    v.wait_ge(sv.h, sv.n)
                    sv.inc(v.tensor_scalar(veps[:], mv[:, 1:2], EPS, None, ALU.add))
                v_veps = sv.n

                @block.scalar
                def _(s, v_veps=v_veps):
                    s.wait_ge(sv.h, v_veps)
                    ss.inc(s.activation(sqv[:], veps[:], AF.Ln))
                    s.wait_ge(ss.h, ss.n)
                    ss.inc(s.activation(rsig[:], sqv[:], AF.Exp, scale=-0.5))
                s_sq = ss.n

                @block.vector
                def _(v, slot=slot, s_sq=s_sq, it=it, m=m):
                    v.wait_ge(ss.h, s_sq)
                    if it == 0 and m == 0:
                        v.wait_ge(sw.h, const_w)
                    sv.inc(v.tensor_scalar(Dm[slot][:], identb[:], rsig[:, 0:1],
                                           None, ALU.mult))
                    sv.inc(v.tensor_scalar(tmp1[:], mv[:, 0:1], rsig[:, 0:1], -1.0,
                                           ALU.mult, ALU.mult))
                    v.wait_ge(sv.h, sv.n)
                    sv.inc(v.tensor_copy(tmpb[:], tmp1[:]))
                v_dm = sv.n

                @block.tensor
                def _(t, slot=slot, v_dm=v_dm, m=m, it=it):
                    if it == 0 and m == 0:
                        t.wait_ge(sw.h, const_w)
                    if it > 0 and m == 0:
                        t.wait_ge(sv.h, rings[("tile_ve",)][1])
                        t.wait_ge(ss.h, rings[("tile_se",)][1])
                    t.wait_ge(sv.h, v_dm)
                    st.inc(t.matmul(P[2][0:1, 0:128], tmpb[:], identb[:],
                                    start=True, stop=True))
                t_row = st.n

                @block.vector
                def _(v, slot=slot, t_row=t_row):
                    v.wait_ge(st.h, t_row)
                    sv.inc(v.tensor_copy(mrowS[slot][:], P[2][0:1, 0:128]))
                v_mrow = sv.n

                if dbg and it == 0 and m == 0:
                    @block.vector
                    def _(v):
                        v.wait_ge(sv.h, v_mrow)
                        v.tensor_copy(lnds[:, 0:2], mv[:])
                        v.tensor_copy(lnds[:, 2:3], veps[:])
                        v.tensor_copy(lnds[:, 3:4], sqv[:])
                        v.tensor_copy(lnds[:, 4:5], rsig[:])
                        v.tensor_copy(lnds[:, 5:6], tmp1[:])
                        sv.inc(v.tensor_copy(mrowf[:], mrowS[0][:]))

                    @block.gpsimd
                    def _(gp):
                        gp.wait_ge(sv.h, sv.n)
                        sd.inc(gp.dma_start(out=dbg_d["lnd"], in_=lnds[:]), 16)
                        sd.inc(gp.dma_start(out=dbg_d["mrow"], in_=mrowf[:]), 16)

                for cg in range(8):
                    bank = cg % 2

                    @block.tensor
                    def _(t, cg=cg, bank=bank, slot=slot, v_mrow=v_mrow, m=m):
                        t.wait_ge(sv.h, v_mrow)
                        ring_wait(t, ("pln", bank))
                        for cs in range(4):
                            c = cg * 4 + cs
                            t.matmul(P[bank][:, cs * 128:(cs + 1) * 128],
                                     xbf[slot][:, c * 128:(c + 1) * 128],
                                     Dm[slot][:], start=(cs == 0), stop=False)
                            ii = t.matmul(P[bank][:, cs * 128:(cs + 1) * 128],
                                          onesb[0:1, :], mrowS[slot][:],
                                          start=False, stop=(cs == 3))
                        st.inc(ii)
                    t_dmm = st.n

                    @block.vector
                    def _(v, cg=cg, bank=bank, t_dmm=t_dmm, m=m):
                        v.wait_ge(st.h, t_dmm)
                        dst = Xsb.ap()[:, cg * 4:(cg + 1) * 4, m * 128:(m + 1) * 128]
                        sv.inc(v.tensor_copy(dst, P[bank][:].rearrange("p (c t) -> p c t", c=4)))
                    ring_set(("pln", bank), sv)
                ring_set(("xbf", slot), st)
            ln_ve_last = sv.n

            flat = 0
            for w in range(3):
                dstt = [Qsb, Ksb, VAsb][w]
                for ko in range(NKC):
                    bank = 3 + (ko % 2)
                    for subq in range(2):
                        fslot = flat % WF_R

                        @block.gpsimd
                        def _(gp, w=w, ko=ko, subq=subq, fslot=fslot):
                            ring_wait(gp, ("wbb", fslot))
                            s_wbb[fslot].inc(gp.dma_start(
                                out=wbb[fslot].ap().rearrange("p (a c) -> p a c", a=16),
                                in_=qkv_w_d[w][ko][:, subq * 16:(subq + 1) * 16, :]), 16)
                        w_cnt = s_wbb[fslot].n

                        @block.tensor
                        def _(t, w=w, ko=ko, subq=subq, bank=bank, fslot=fslot,
                              w_cnt=w_cnt):
                            t.wait_ge(s_wbb[fslot].h, w_cnt)
                            if w == 0 and ko == 0 and subq == 0:
                                t.wait_ge(sv.h, ln_ve_last)
                            if subq == 0:
                                ring_wait(t, ("pqkv", bank))
                            for kl in range(16):
                                ki = subq * 16 + kl
                                ii = t.matmul(P[bank][:],
                                              wbb[fslot][:, kl * 128:(kl + 1) * 128],
                                              Xsb.ap()[:, ki, :],
                                              start=(ki == 0), stop=(ki == NKC - 1))
                            st.inc(ii)
                        ring_set(("wbb", fslot), st)
                        flat += 1
                    t_proj = st.n

                    @block.vector
                    def _(v, w=w, ko=ko, bank=bank, t_proj=t_proj, dstt=dstt, it=it):
                        v.wait_ge(st.h, t_proj)
                        if it == 0 and w == 0 and ko == 0:
                            v.wait_ge(sw.h, const_w)
                        dst = dstt.ap().rearrange("p (t h) -> p t h", h=32)[:, :, ko]
                        sv.inc(v.tensor_scalar(dst, P[bank][:],
                                               biasv_s[w][:, ko:ko + 1], None, ALU.add))
                    ring_set(("pqkv", bank), sv)
            qkv_ve = sv.n

            if dbg and first_tile:
                @block.gpsimd
                def _(gp):
                    gp.wait_ge(sv.h, qkv_ve)
                    sd.inc(gp.dma_start(out=dbg_d["q"], in_=Qsb[:]), 16)
                    sd.inc(gp.dma_start(out=dbg_d["k"], in_=Ksb[:]), 16)
                    sd.inc(gp.dma_start(out=dbg_d["v"], in_=VAsb[:]), 16)
                    sd.inc(gp.dma_start(
                        out=dbg_d["xn"].rearrange("p (c t) -> p c t", c=NKC),
                        in_=Xsb[:]), 16)

            TS = [0] * NB; TZ = [0] * NB; TVT = [0] * NB; TAV = [0] * NB
            TAVT = [0] * NB
            VEind = [0] * NB; VEme = [0] * NB; VErec = [0] * NB
            VEdrec = [0] * NB; VEao = [0] * NB
            SEexp = [0] * NB; SEvq = [0] * NB; SEau = [0] * NB

            for b in range(NB):
                sp = b % 2

                @block.tensor
                def _(t, b=b, sp=sp):
                    if b >= 2:
                        t.wait_ge(ss.h, SEexp[b - 2])
                        t.wait_ge(sv.h, VEind[b - 2])
                    else:
                        t.wait_ge(sv.h, qkv_ve)
                    for q in range(4):
                        t0 = b * 16 + q * 4
                        ii = t.matmul(P[sp][:, q * 128:(q + 1) * 128],
                                      Ksb[:, t0 * 32:(t0 + 4) * 32],
                                      Qsb[:, t0 * 32:(t0 + 4) * 32],
                                      start=(q == 0), stop=(q == 3))
                    st.inc(ii)
                TS[b] = st.n

                @block.scalar
                def _(s, b=b, sp=sp):
                    s.wait_ge(st.h, TS[b])
                    if b >= 1:
                        s.wait_ge(sv.h, VEme[b - 1])
                    ss.inc(s.activation(e_t[:], P[sp][:], AF.Exp, scale=INV_SQRT_D))
                SEexp[b] = ss.n

                @block.vector
                def _(v, b=b, sp=sp, it=it):
                    v.wait_ge(st.h, TS[b])
                    if it == 0 and b == 0:
                        v.wait_ge(sw.h, const_w)
                    sv.inc(v.tensor_tensor(ind_t[:], P[sp][:], thb[:], ALU.is_gt))
                VEind[b] = sv.n

                @block.vector
                def _(v, b=b, sp=sp):
                    v.wait_ge(ss.h, SEexp[b])
                    if b >= 2:
                        v.wait_ge(st.h, TAV[b - 2])
                    sv.inc(v.tensor_tensor(me_t[sp][:], e_t[:], ind_t[:], ALU.mult))
                VEme[b] = sv.n

                if dbg and first_tile and b == 0:
                    @block.gpsimd
                    def _(gp):
                        gp.wait_ge(sv.h, VEme[0])
                        sd.inc(gp.dma_start(out=dbg_d["me"], in_=me_t[0][:]), 16)

                @block.tensor
                def _(t, b=b, sp=sp):
                    t.wait_ge(sv.h, VEme[b])
                    if b >= 2:
                        t.wait_ge(sv.h, VErec[b - 2])
                    for q in range(4):
                        ii = t.matmul(P[2 + sp][:, q:q + 1],
                                      me_t[sp][:, q * 128:(q + 1) * 128],
                                      onesb[:, 0:1], start=(q == 0), stop=(q == 3))
                    st.inc(ii)
                    if b >= 1:
                        t.wait_ge(sv.h, SEvq[b - 1])
                    for q in range(4):
                        t0 = b * 16 + q * 4
                        ii = t.matmul(P[4][:, q * 128:(q + 1) * 128],
                                      VAsb[:, t0 * 32:(t0 + 4) * 32],
                                      identb[:], start=(q == 0), stop=(q == 3))
                    st.inc(ii)
                TZ[b] = st.n - 1
                TVT[b] = st.n

                @block.vector
                def _(v, b=b, sp=sp):
                    v.wait_ge(st.h, TVT[b])
                    sv.inc(v.tensor_copy(vqs[sp][:], P[4][:]))
                SEvq[b] = sv.n

                @block.tensor
                def _(t, b=b, sp=sp):
                    t.wait_ge(sv.h, SEvq[b])
                    if b >= 2:
                        t.wait_ge(sv.h, SEau[b - 2])
                    for q in range(4):
                        ii = t.matmul(P[5 + sp][:, q * 128:(q + 1) * 128],
                                      me_t[sp][:, q * 128:(q + 1) * 128],
                                      vqs[sp][:, q * 128:(q + 1) * 128],
                                      start=(q == 0), stop=(q == 3))
                    st.inc(ii)
                TAV[b] = st.n

                @block.vector
                def _(v, b=b, sp=sp):
                    v.wait_ge(st.h, TZ[b])
                    sv.inc(v.reciprocal(rec_t[sp][:], P[2 + sp][:, 0:4]))
                VErec[b] = sv.n

                @block.vector
                def _(v, b=b, sp=sp):
                    v.wait_ge(sv.h, VErec[b])
                    if b >= 2:
                        v.wait_ge(st.h, TAVT[b - 2])
                    for q in range(4):
                        ii = v.tensor_scalar(drec[sp][:, q * 128:(q + 1) * 128],
                                             identb[:], rec_t[sp][:, q:q + 1],
                                             None, ALU.mult)
                    sv.inc(ii)
                VEdrec[b] = sv.n

                @block.vector
                def _(v, b=b, sp=sp):
                    v.wait_ge(st.h, TAV[b])
                    sv.inc(v.tensor_copy(avus[sp][:], P[5 + sp][:]))
                SEau[b] = sv.n

                @block.tensor
                def _(t, b=b, sp=sp):
                    t.wait_ge(sv.h, SEau[b])
                    t.wait_ge(sv.h, VEdrec[b])
                    if b >= 1:
                        t.wait_ge(sv.h, VEao[b - 1])
                    for q in range(4):
                        ii = t.matmul(P[7][:, q * 128:(q + 1) * 128],
                                      avus[sp][:, q * 128:(q + 1) * 128],
                                      drec[sp][:, q * 128:(q + 1) * 128],
                                      start=(q == 0), stop=(q == 3))
                    st.inc(ii)
                TAVT[b] = st.n

                @block.vector
                def _(v, b=b, sp=sp):
                    v.wait_ge(st.h, TAVT[b])
                    sv.inc(v.tensor_copy(VAsb[:, b * 512:(b + 1) * 512], P[7][:]))
                VEao[b] = sv.n

            attn_ve = sv.n
            attn_se = ss.n

            if dbg and first_tile:
                @block.gpsimd
                def _(gp):
                    gp.wait_ge(sv.h, attn_ve)
                    sd.inc(gp.dma_start(out=dbg_d["ao"], in_=VAsb[:]), 16)

            @block.gpsimd
            def _(gp):
                ring_wait(gp, ("gwr",))
                s_gw.inc(gp.dma_start(out=gwr[:], in_=gw_d), 16)
            w_gw = s_gw.n

            @block.tensor
            def _(t, w_gw=w_gw):
                t.wait_ge(s_gw.h, w_gw)
                t.wait_ge(sv.h, attn_ve)
                t.wait_ge(ss.h, attn_se)
                for ki in range(NKC):
                    for m in range(NM):
                        ii = t.matmul(P[0][:, m * NE:(m + 1) * NE],
                                      Xsb.ap()[:, ki, m * 128:(m + 1) * 128],
                                      gwr.ap()[:, ki, :],
                                      start=(ki == 0 and m == 0),
                                      stop=(ki == NKC - 1))
                st.inc(ii)
            ring_set(("gwr",), st)
            t_logits = st.n

            @block.vector
            def _(v):
                v.wait_ge(st.h, t_logits)
                sv.inc(v.tensor_copy(Lsb[:], P[0][:, 0:4 * NE]))
            v_logits = sv.n

            if dbg and it == 0:
                @block.gpsimd
                def _(gp):
                    gp.wait_ge(sv.h, v_logits)
                    sd.inc(gp.dma_start(out=dbg_d["lg"], in_=Lsb[:]), 16)
                    sd.inc(gp.dma_start(
                        out=dbg_d["xn2"].rearrange("p (c t) -> p c t", c=NKC),
                        in_=Xsb[:]), 16)
                    sd.inc(gp.dma_start(out=dbg_d["gw0"][:, 0:NE], in_=gwr.ap()[:, 30, :]), 16)
                    sd.inc(gp.dma_start(out=dbg_d["gw0"][:, NE:2*NE], in_=gwr.ap()[:, 31, :]), 16)

            s_exp_prev = [0]
            for m in range(NM):
                L = Lsb[:, m * NE:(m + 1) * NE]

                @block.vector
                def _(v, m=m, L=L):
                    if m == 0:
                        v.wait_ge(sv.h, v_logits)
                    else:
                        v.wait_ge(ss.h, s_exp_prev[0])
                    sv.inc(v.reduce_max(m1p[:], L, axis=AX.X))
                    v.wait_ge(sv.h, sv.n)
                    sv.inc(v.tensor_scalar(nm1[:], m1p[:], -1.0, None, ALU.mult))
                    v.wait_ge(sv.h, sv.n)
                    sv.inc(v.tensor_scalar(t1s[:], L, nm1[:, 0:1], None, ALU.add))
                    v.wait_ge(sv.h, sv.n)
                    sv.inc(v.tensor_single_scalar(c1s[:], t1s[:], 0.0, ALU.is_ge))
                    v.wait_ge(sv.h, sv.n)
                    sv.inc(v.scalar_tensor_tensor(maskd[:], c1s[:], BIGNEG, L,
                                                  ALU.mult, ALU.add))
                    v.wait_ge(sv.h, sv.n)
                    sv.inc(v.reduce_max(m2p[:], maskd[:], axis=AX.X))
                    v.wait_ge(sv.h, sv.n)
                    sv.inc(v.tensor_scalar(nm2[:], m2p[:], -1.0, None, ALU.mult))
                    sv.inc(v.tensor_sub(d21[:], m2p[:], m1p[:]))
                    v.wait_ge(sv.h, sv.n)
                v_m2 = sv.n

                @block.scalar
                def _(s, m=m, L=L, v_m2=v_m2):
                    s.wait_ge(sv.h, v_m2)
                    ss.inc(s.activation(els[:], L, AF.Exp, bias=nm1[:, 0:1]))
                    ss.inc(s.activation(em2[:], d21[:], AF.Exp))
                s_exp = ss.n
                s_exp_prev[0] = s_exp

                if dbg and it == 0 and m == 0:
                    @block.vector
                    def _(v):
                        v.wait_ge(ss.h, s_exp)
                        v.tensor_copy(t2ds[:, 0:1], m1p[:])
                        v.tensor_copy(t2ds[:, 1:2], m2p[:])
                        v.tensor_copy(t2ds[:, 2:3], d21[:])
                        v.tensor_copy(t2ds[:, 3:4], em2[:])
                        v.tensor_copy(t2ds[:, 4:5], els[:, 0:1])
                        sv.inc(v.tensor_copy(t2ds[:, 5:6], c1s[:, 0:1]))

                    @block.gpsimd
                    def _(gp):
                        gp.wait_ge(sv.h, sv.n)
                        sd.inc(gp.dma_start(out=dbg_d["t2d"], in_=t2ds[:]), 16)

                @block.vector
                def _(v, m=m, L=L, s_exp=s_exp):
                    v.wait_ge(ss.h, s_exp)
                    sv.inc(v.tensor_scalar(den[:], em2[:], 1.0, None, ALU.add))
                    v.wait_ge(sv.h, sv.n)
                    sv.inc(v.reciprocal(rec2[:], den[:]))
                    v.wait_ge(sv.h, sv.n)
                    sv.inc(v.tensor_scalar(c2s[:], L, nm2[:, 0:1], 0.0,
                                           ALU.add, ALU.is_ge))
                    v.wait_ge(sv.h, sv.n)
                    sv.inc(v.tensor_mul(g1s[:], c2s[:], els[:]))
                    v.wait_ge(sv.h, sv.n)
                    sv.inc(v.tensor_scalar(gf[:, m * NE:(m + 1) * NE], g1s[:],
                                           rec2[:, 0:1], None, ALU.mult))

            @block.vector
            def _(v):
                v.wait_ge(sv.h, sv.n)
                sv.inc(v.tensor_copy(gfb[:], gf[:]))
            v_gfb = sv.n

            if dbg and first_tile:
                @block.gpsimd
                def _(gp):
                    gp.wait_ge(sv.h, v_gfb)
                    sd.inc(gp.dma_start(out=dbg_d["gf"], in_=gf[:]), 16)

            @block.tensor
            def _(t):
                t.wait_ge(sv.h, v_gfb)
                for m in range(NM):
                    ii = t.matmul(P[1][0:NE, m * 128:(m + 1) * 128],
                                  gfb[:, m * NE:(m + 1) * NE], identb[:],
                                  start=(m == 0), stop=(m == NM - 1))
                st.inc(ii)
            t_gft = st.n

            @block.vector
            def _(v):
                v.wait_ge(st.h, t_gft)
                sv.inc(v.tensor_copy(gfTs[:], P[1][0:NE, :]))
            v_gft = sv.n

            @block.tensor
            def _(t):
                t.wait_ge(sv.h, v_gft)
                st.inc(t.matmul(P[2][:], e2b[:, 0:128], gfTs[:], start=True, stop=True))
                st.inc(t.matmul(P[3][0:96, :], e2b[:, 128:224], gfTs[:],
                                start=True, stop=True))
            t_ge = st.n

            @block.vector
            def _(v):
                v.wait_ge(st.h, t_ge)
                sv.inc(v.tensor_copy(ge0[:], P[2][:]))
                sv.inc(v.tensor_copy(ge1[:], P[3][0:96, :]))
            v_ge = sv.n

            for ki in range(NKC):
                aslot = ki % 6

                @block.gpsimd
                def _(gp, ki=ki, aslot=aslot):
                    ring_wait(gp, ("a2b", aslot))
                    s_a2[aslot].inc(gp.dma_start(out=a2b[aslot][:], in_=a2_d[ki]), 16)
                s_cnt = s_a2[aslot].n

                @block.tensor
                def _(t, ki=ki, aslot=aslot, s_cnt=s_cnt):
                    t.wait_ge(s_a2[aslot].h, s_cnt)
                    if ki == 0:
                        t.wait_ge(sv.h, attn_ve)
                        t.wait_ge(ss.h, attn_se)
                    t.matmul(P[4][:], a2b[aslot][:, 0:128], Xsb.ap()[:, ki, :],
                             start=(ki == 0), stop=(ki == NKC - 1))
                    ii = t.matmul(P[5][0:96, :], a2b[aslot][:, 128:224],
                                  Xsb.ap()[:, ki, :],
                                  start=(ki == 0), stop=(ki == NKC - 1))
                    st.inc(ii)
                ring_set(("a2b", aslot), st)
            t_h = st.n

            @block.vector
            def _(v):
                v.wait_ge(st.h, t_h)
                v.wait_ge(sv.h, v_ge)
                sv.inc(v.tensor_tensor(hs0[:], P[4][:], ge0[:], ALU.mult))
                sv.inc(v.tensor_tensor(hs1[:], P[5][0:96, :], ge1[:], ALU.mult))
            v_hs = sv.n

            if dbg and first_tile:
                @block.gpsimd
                def _(gp):
                    gp.wait_ge(sv.h, v_hs)
                    sd.inc(gp.dma_start(out=dbg_d["hs0"], in_=hs0[:]), 16)
                    sd.inc(gp.dma_start(out=dbg_d["hs1"], in_=hs1[:]), 16)

            ev_done = [0] * NOG
            oflat = 0
            for n in range(NOG):
                par = 4 * (n % 2)
                bslot_bo = n % 2

                @block.gpsimd
                def _(gp, n=n, bslot_bo=bslot_bo):
                    ring_wait(gp, ("bo", bslot_bo))
                    s_bo[bslot_bo].inc(gp.dma_start(
                        out=bo_bf[bslot_bo][:],
                        in_=bo_d[:, n * 512:(n + 1) * 512]), 16)
                x_bo = s_bo[bslot_bo].n

                @block.gpsimd
                def _(gp, n=n):
                    ring_wait(gp, ("b2b",))
                    s_b2.inc(gp.dma_start(out=b2b0[:], in_=b20_d[n]), 16)
                    s_b2.inc(gp.dma_start(out=b2b1[:], in_=b21_d[n]), 16)
                s_b2n = s_b2.n

                @block.sync
                def _(sy, n=n):
                    for m in range(NM):
                        sy.wait_ge(s_ost[m].h, s_ost[m].n)
                        sw.inc(sy.dma_start(
                            out=xr[m][:],
                            in_=x_d[t_base + m * 128: t_base + (m + 1) * 128,
                                    n * 512:(n + 1) * 512]), 16)
                w_xr = sw.n

                for kk in range(19):
                    if kk < 8:
                        src_d, kind, base_k = wo_d[n][:, kk * 4:kk * 4 + 4, :], "w", kk * 4
                    elif kk == 8:
                        kind = "bo"
                    elif kk < 17:
                        c = kk - 9
                        src_d, kind, base_k = wb_d[n][:, c * 4:c * 4 + 4, :], "w", 33 + c * 4
                    elif kk == 17:
                        kind = "b20"
                    else:
                        kind = "b21"

                    if kind == "w":
                        bslot = oflat % OB_R

                        @block.gpsimd
                        def _(gp, src_d=src_d, bslot=bslot):
                            ring_wait(gp, ("obf", bslot))
                            s_obf[bslot].inc(gp.dma_start(
                                out=obf[bslot][:], in_=src_d), 16)
                        rdy = (s_obf[bslot], s_obf[bslot].n)

                        @block.tensor
                        def _(t, kk=kk, n=n, par=par, bslot=bslot, rdy=rdy,
                              base_k=base_k):
                            t.wait_ge(rdy[0].h, rdy[1])
                            if kk == 0:
                                if n <= 1:
                                    t.wait_ge(sv.h, v_hs)
                                else:
                                    t.wait_ge(sv.h, ev_done[n - 2])
                            for kl in range(4):
                                kabs = base_k + kl
                                for m in range(NM):
                                    if kabs < 32:
                                        lhsT = VAsb.ap().rearrange(
                                            "p (t h) -> p t h",
                                            h=32)[:, m * 128:(m + 1) * 128, kabs]
                                    else:
                                        lhsT = Xsb.ap()[:, kabs - 33, m * 128:(m + 1) * 128]
                                    ii = t.matmul(P[par + m][:], lhsT,
                                                  obf[bslot].ap()[:, kl, :],
                                                  start=(kabs == 0), stop=False)
                            st.inc(ii)
                        ring_set(("obf", bslot), st)
                        oflat += 1
                    elif kind == "bo":
                        @block.tensor
                        def _(t, n=n, par=par, x_bo=x_bo, bslot_bo=bslot_bo):
                            t.wait_ge(s_bo[bslot_bo].h, x_bo)
                            for m in range(NM):
                                t.matmul(P[par + m][:], onesb[0:1, 0:128],
                                         bo_bf[bslot_bo][:], start=False, stop=False)
                    elif kind == "b20":
                        @block.tensor
                        def _(t, par=par, s_b2n=s_b2n):
                            t.wait_ge(s_b2.h, s_b2n)
                            for m in range(NM):
                                t.matmul(P[par + m][:], hs0[:, m * 128:(m + 1) * 128],
                                         b2b0[:], start=False, stop=False)
                    else:
                        @block.tensor
                        def _(t, par=par):
                            for m in range(NM):
                                ii = t.matmul(P[par + m][:],
                                              hs1[:, m * 128:(m + 1) * 128],
                                              b2b1[:], start=False, stop=True)
                            st.inc(ii)
                t_out_n = st.n
                ring_set(("b2b",), st)
                ring_set(("bo", bslot_bo), st)

                @block.vector
                def _(v, n=n, par=par, t_out_n=t_out_n, w_xr=w_xr):
                    v.wait_ge(st.h, t_out_n)
                    v.wait_ge(sw.h, w_xr)
                    for m in range(NM):
                        sv.inc(v.tensor_add(xr[m][:], P[par + m][:], xr[m][:]))
                        ring_set(("xr", m), sv)
                ev_done[n] = sv.n

                @block.scalar
                def _(s, n=n, t_base=t_base):
                    s.wait_ge(sv.h, ev_done[n])
                    for m in range(NM):
                        s_ost[m].inc(s.dma_start(
                            out=y_d[t_base + m * 128: t_base + (m + 1) * 128,
                                    n * 512:(n + 1) * 512],
                            in_=xr[m][:]), 16)

            ring_set(("tile_ve",), sv)
            ring_set(("tile_se",), ss)

        @block.gpsimd
        def _(gp):
            for m in range(4):
                gp.wait_ge(s_ost[m].h, s_ost[m].n)
            if dbg:
                gp.wait_ge(sd.h, sd.n)

    return nc


def prep_inputs(inputs):
    Wq = np.asarray(inputs["L0_attn_Wq"]); Wk = np.asarray(inputs["L0_attn_Wk"])
    Wv = np.asarray(inputs["L0_attn_Wv"]); Wo = np.asarray(inputs["L0_attn_Wo"])
    baseW = np.asarray(inputs["L0_ffn_base_W"])
    gateW = np.asarray(inputs["L0_ffn_gate_W"])
    loraA = np.asarray(inputs["L0_ffn_lora_A"]); loraB = np.asarray(inputs["L0_ffn_lora_B"])

    def qkv_tiles(W):
        WT = np.ascontiguousarray(W.T)
        return np.ascontiguousarray(WT.reshape(NKC, 128, NKC, 128).transpose(2, 1, 0, 3))

    def out_tiles(W):
        WT = np.ascontiguousarray(W.T)
        return np.ascontiguousarray(WT.reshape(NKC, 128, NOG, 512).transpose(2, 1, 0, 3))

    A2 = loraA.transpose(1, 0, 2).reshape(224, HID)
    a2t = np.ascontiguousarray(A2.T.reshape(NKC, 128, 224))
    B2 = loraB.transpose(2, 0, 1).reshape(224, HID)
    b20 = np.ascontiguousarray(B2[:128].reshape(128, NOG, 512).transpose(1, 0, 2))
    b21 = np.ascontiguousarray(B2[128:].reshape(96, NOG, 512).transpose(1, 0, 2))
    gw = np.ascontiguousarray(gateW.T.reshape(NKC, 128, NE).transpose(1, 0, 2))

    def bias_cols(b):
        return np.ascontiguousarray(np.asarray(b).reshape(NKC, 128).T)

    e2 = np.zeros((NE, 224), dtype=BF)
    for r in range(RK):
        for e in range(NE):
            e2[e, r * NE + e] = BF(2.0)
    ident = np.eye(128, dtype=BF)
    onesc = np.ones((128, 128), dtype=BF)
    pat = np.full((128, 128), 1.0e30, dtype=BF)
    for tt in range(4):
        pat[tt * 32:(tt + 1) * 32, tt * 32:(tt + 1) * 32] = BF(TH_RAW)
    thb = np.ascontiguousarray(np.tile(pat, (1, 4)))

    return {
        "wq": qkv_tiles(Wq), "wk": qkv_tiles(Wk), "wv": qkv_tiles(Wv),
        "wo": out_tiles(Wo), "wb": out_tiles(baseW),
        "a2": a2t, "b20": b20, "b21": b21, "gw": gw,
        "bq": bias_cols(inputs["L0_attn_bq"]),
        "bk": bias_cols(inputs["L0_attn_bk"]),
        "bv": bias_cols(inputs["L0_attn_bv"]),
        "bo": np.ascontiguousarray(np.asarray(inputs["L0_attn_bo"])[None, :]),
        "idq": ident, "onesc": onesc, "e2": e2, "thb": thb,
    }


def kernel(**inputs):
    ncores = 8
    ntok = 2048
    T = 512
    x_full = np.ascontiguousarray(
        np.asarray(inputs["L0_attn_x"], dtype=np.float32).reshape(16384, HID))
    wmap = prep_inputs(inputs)
    nc = build_program(ntok, T, dbg=False)
    in_maps = []
    for c in range(ncores):
        m = dict(wmap)
        m["x"] = x_full[c * ntok:(c + 1) * ntok]
        in_maps.append(m)
    res = run_bass_kernel_spmd(nc, in_maps, core_ids=list(range(ncores)))
    out = np.concatenate([np.asarray(r["y"]) for r in res.results], axis=0)
    return out.reshape(4, 4096, HID)


# version 21 (final)
# speedup vs baseline: 1.1161x; 1.0639x over previous
import numpy as np
import ml_dtypes
from contextlib import ExitStack

import concourse.bass as bass
import concourse.mybir as mybir
from concourse.bass_utils import run_bass_kernel_spmd

F32 = mybir.dt.float32
BF16 = mybir.dt.bfloat16
AF = mybir.ActivationFunctionType
ALU = mybir.AluOpType
AX = mybir.AxisListType
BF = ml_dtypes.bfloat16

HID = 4096
NH = 32
NE = 28
RK = 8
NKC = 32
NOG = 8
TH_RAW = float(np.float32(0.001) * np.float32(np.sqrt(np.float32(128.0))))
INV_SQRT_D = float(np.float32(1.0) / np.float32(np.sqrt(np.float32(128.0))))
EPS = 1e-5
BIGNEG = -1.0e30




class Sem:
    def __init__(self, es, nc, name):
        self.h = es.enter_context(nc.semaphore(name))
        self.n = 0

    def inc(self, instr, k=1):
        instr.then_inc(self.h, k)
        self.n += k
        return self.n


def build_program(ntok, T, dbg=False):
    nc = bass.Bass("TRN2", target_bir_lowering=False)
    NT = ntok // T
    NM = T // 128
    NB = T // 16

    x_d = nc.dram_tensor("x", [ntok, HID], F32, kind="ExternalInput").ap()
    y_d = nc.dram_tensor("y", [ntok, HID], F32, kind="ExternalOutput").ap()
    wq_d = nc.dram_tensor("wq", [NKC, 128, NKC, 128], F32, kind="ExternalInput").ap()
    wk_d = nc.dram_tensor("wk", [NKC, 128, NKC, 128], F32, kind="ExternalInput").ap()
    wv_d = nc.dram_tensor("wv", [NKC, 128, NKC, 128], F32, kind="ExternalInput").ap()
    wo_d = nc.dram_tensor("wo", [NOG, 128, NKC, 512], F32, kind="ExternalInput").ap()
    wb_d = nc.dram_tensor("wb", [NOG, 128, NKC, 512], F32, kind="ExternalInput").ap()
    a2_d = nc.dram_tensor("a2", [NKC, 128, 224], F32, kind="ExternalInput").ap()
    b20_d = nc.dram_tensor("b20", [NOG, 128, 512], F32, kind="ExternalInput").ap()
    b21_d = nc.dram_tensor("b21", [NOG, 96, 512], F32, kind="ExternalInput").ap()
    gw_d = nc.dram_tensor("gw", [128, NKC, NE], F32, kind="ExternalInput").ap()
    bq_d = nc.dram_tensor("bq", [128, NKC], F32, kind="ExternalInput").ap()
    bk_d = nc.dram_tensor("bk", [128, NKC], F32, kind="ExternalInput").ap()
    bv_d = nc.dram_tensor("bv", [128, NKC], F32, kind="ExternalInput").ap()
    bo_d = nc.dram_tensor("bo", [1, HID], F32, kind="ExternalInput").ap()
    idq_d = nc.dram_tensor("idq", [128, 128], BF16, kind="ExternalInput").ap()
    ones_d = nc.dram_tensor("onesc", [128, 128], BF16, kind="ExternalInput").ap()
    e2_d = nc.dram_tensor("e2", [NE, 224], BF16, kind="ExternalInput").ap()
    thb_d = nc.dram_tensor("thb", [128, 512], BF16, kind="ExternalInput").ap()
    qkv_w_d = [wq_d, wk_d, wv_d]
    dbg_d = {}
    if dbg:
        for nm, shp in [("xn", [128, NKC * T]), ("q", [128, T * 32]),
                        ("k", [128, T * 32]), ("v", [128, T * 32]),
                        ("ao", [128, T * 32]), ("gf", [128, 4 * NE]),
                        ("hs0", [128, T]), ("hs1", [96, T]), ("me", [128, 512]),
                        ("lnd", [128, 8]), ("mrow", [1, 128]),
                        ("lg", [128, 4 * NE]), ("t2d", [128, 8]),
                        ("xn2", [128, NKC * T]), ("gw0", [128, 2 * NE])]:
            dbg_d[nm] = nc.dram_tensor("d_" + nm, shp, F32, kind="ExternalOutput").ap()

    es = ExitStack()

    def sbt(name, shape, dt):
        return es.enter_context(nc.sbuf_tensor(name, shape, dt))

    Xsb = sbt("Xsb", [128, NKC, T], BF16)
    Qsb = sbt("Qsb", [128, T * 32], BF16)
    Ksb = sbt("Ksb", [128, T * 32], BF16)
    VAsb = sbt("VAsb", [128, T * 32], BF16)
    xbf = [sbt(f"xbf{i}", [128, HID], BF16) for i in range(2)]
    WF_R, OB_R = 3, 3
    wbb = [sbt(f"wbb{i}", [128, 2048], BF16) for i in range(WF_R)]
    obf = [sbt(f"obf{i}", [128, 4, 512], BF16) for i in range(OB_R)]
    a2b = [sbt(f"a2b{i}", [128, 224], BF16) for i in range(6)]
    b2b0 = sbt("b2b0", [128, 512], BF16)
    b2b1 = sbt("b2b1", [96, 512], BF16)
    gwr = sbt("gwr", [128, NKC, NE], BF16)
    bo_bf = [sbt(f"bo_bf{i}", [1, 512], BF16) for i in range(2)]
    identb = sbt("identb", [128, 128], BF16)
    onesb = sbt("onesb", [128, 128], BF16)
    e2b = sbt("e2b", [NE, 224], BF16)
    thb = sbt("thbs", [128, 512], BF16)
    biasv_s = [sbt(f"bias{i}", [128, NKC], F32) for i in range(3)]
    stats = sbt("stats", [128, 48], F32)
    mv = sbt("mv", [128, 2], F32)
    veps = sbt("veps", [128, 1], F32)
    sqv = sbt("sqv", [128, 1], F32)
    rsig = sbt("rsig", [128, 1], F32)
    tmp1 = sbt("tmp1", [128, 1], F32)
    tmpb = sbt("tmpb", [128, 1], BF16)
    Dm = [sbt(f"Dm{i}", [128, 128], BF16) for i in range(2)]
    mrowS = [sbt(f"mrowS{i}", [1, 128], BF16) for i in range(2)]
    e_t = sbt("e_t", [128, 512], BF16)
    ind_t = sbt("ind_t", [128, 512], BF16)
    me_t = [sbt(f"me{i}", [128, 512], BF16) for i in range(2)]
    vqs = [sbt(f"vqs{i}", [128, 512], BF16) for i in range(2)]
    avus = [sbt(f"avus{i}", [128, 512], BF16) for i in range(2)]
    drec = [sbt(f"drec{i}", [128, 512], BF16) for i in range(2)]
    rec_t = [sbt(f"rec{i}", [128, 4], F32) for i in range(2)]
    Lsb = sbt("Lsb", [128, 4 * NE], F32)
    m1p = sbt("m1p", [128, 1], F32)
    nm1 = sbt("nm1", [128, 1], F32)
    m2p = sbt("m2p", [128, 1], F32)
    nm2 = sbt("nm2", [128, 1], F32)
    t1s = sbt("t1s", [128, NE], F32)
    c1s = sbt("c1s", [128, NE], F32)
    maskd = sbt("maskd", [128, NE], F32)
    d21 = sbt("d21", [128, 1], F32)
    em2 = sbt("em2", [128, 1], F32)
    den = sbt("den", [128, 1], F32)
    rec2 = sbt("rec2", [128, 1], F32)
    els = sbt("els", [128, NE], F32)
    c2s = sbt("c2s", [128, NE], F32)
    g1s = sbt("g1s", [128, NE], F32)
    gf = sbt("gf", [128, 4 * NE], F32)
    gfb = sbt("gfb", [128, 4 * NE], BF16)
    gfTs = sbt("gfTs", [NE, T], BF16)
    ge0 = sbt("ge0", [128, T], BF16)
    ge1 = sbt("ge1", [96, T], BF16)
    hs0 = sbt("hs0", [128, T], BF16)
    hs1 = sbt("hs1", [96, T], BF16)
    if dbg:
        lnds = sbt("lnds", [128, 8], F32)
        mrowf = sbt("mrowf", [1, 128], F32)
        t2ds = sbt("t2ds", [128, 8], F32)
    xr = [sbt(f"xr{i}", [128, 512], F32) for i in range(4)]
    P = [es.enter_context(nc.psum_tensor(f"P{i}", [128, 512], F32)) for i in range(8)]

    sx = Sem(es, nc, "sx")
    sxa = Sem(es, nc, "sxa")
    sxs = Sem(es, nc, "sxs")
    sw = Sem(es, nc, "sw")
    st = Sem(es, nc, "st")
    sv = Sem(es, nc, "sv")
    ss = Sem(es, nc, "ss")
    so = Sem(es, nc, "so")
    sd = Sem(es, nc, "sd")
    s_xbf = [Sem(es, nc, f"s_xbf{i}") for i in range(2)]
    s_wbb = [Sem(es, nc, f"s_wbb{i}") for i in range(4)]
    s_obf = [Sem(es, nc, f"s_obf{i}") for i in range(8)]
    s_a2 = [Sem(es, nc, f"s_a2{i}") for i in range(6)]
    s_gw = Sem(es, nc, "s_gw")
    s_bo = [Sem(es, nc, f"s_bo{i}") for i in range(2)]
    s_b2 = Sem(es, nc, "s_b2")
    s_ost = [Sem(es, nc, f"s_ost{i}") for i in range(4)]

    rings = {}

    def ring_wait(eng, key):
        v = rings.get(key)
        if v is not None:
            eng.wait_ge(v[0].h, v[1])

    def ring_set(key, sem):
        rings[key] = (sem, sem.n)

    block = es.enter_context(nc.Block())

    with es:
        @block.sync
        def _(sy):
            sw.inc(sy.dma_start(out=identb[:], in_=idq_d), 16)
            sw.inc(sy.dma_start(out=onesb[:], in_=ones_d), 16)
            sw.inc(sy.dma_start(out=e2b[:], in_=e2_d), 16)
            sw.inc(sy.dma_start(out=thb[:], in_=thb_d), 16)
            sw.inc(sy.dma_start(out=biasv_s[0][:], in_=bq_d), 16)
            sw.inc(sy.dma_start(out=biasv_s[1][:], in_=bk_d), 16)
            sw.inc(sy.dma_start(out=biasv_s[2][:], in_=bv_d), 16)
        const_w = sw.n

        for it in range(NT):
            t_base = it * T
            first_tile = (it == 0)

            for m in range(NM):
                slot = m % 2

                @block.gpsimd
                def _(gp, m=m, slot=slot):
                    ring_wait(gp, ("xbf", slot))
                    s_xbf[slot].inc(gp.dma_start(
                        out=xbf[slot][:],
                        in_=x_d[t_base + m * 128: t_base + (m + 1) * 128, :]), 16)
                x_cnt = s_xbf[slot].n

                @block.vector
                def _(v, slot=slot, x_cnt=x_cnt):
                    v.wait_ge(s_xbf[slot].h, x_cnt)
                    for j in range(8):
                        ii = v.bn_stats(stats[:, j * 6:(j + 1) * 6],
                                        xbf[slot][:, j * 512:(j + 1) * 512])
                    sv.inc(ii)
                    v.wait_ge(sv.h, sv.n)
                    sv.inc(v.bn_aggr(mv[:], stats[:]))
                    v.wait_ge(sv.h, sv.n)
                    sv.inc(v.tensor_scalar(veps[:], mv[:, 1:2], EPS, None, ALU.add))
                v_veps = sv.n

                @block.scalar
                def _(s, v_veps=v_veps):
                    s.wait_ge(sv.h, v_veps)
                    ss.inc(s.activation(sqv[:], veps[:], AF.Ln))
                    s.wait_ge(ss.h, ss.n)
                    ss.inc(s.activation(rsig[:], sqv[:], AF.Exp, scale=-0.5))
                s_sq = ss.n

                @block.vector
                def _(v, slot=slot, s_sq=s_sq, it=it, m=m):
                    v.wait_ge(ss.h, s_sq)
                    if it == 0 and m == 0:
                        v.wait_ge(sw.h, const_w)
                    sv.inc(v.tensor_scalar(Dm[slot][:], identb[:], rsig[:, 0:1],
                                           None, ALU.mult))
                    sv.inc(v.tensor_scalar(tmp1[:], mv[:, 0:1], rsig[:, 0:1], -1.0,
                                           ALU.mult, ALU.mult))
                    v.wait_ge(sv.h, sv.n)
                    sv.inc(v.tensor_copy(tmpb[:], tmp1[:]))
                v_dm = sv.n

                @block.tensor
                def _(t, slot=slot, v_dm=v_dm, m=m, it=it):
                    if it == 0 and m == 0:
                        t.wait_ge(sw.h, const_w)
                    if it > 0 and m == 0:
                        t.wait_ge(sv.h, rings[("tile_ve",)][1])
                        t.wait_ge(ss.h, rings[("tile_se",)][1])
                    t.wait_ge(sv.h, v_dm)
                    st.inc(t.matmul(P[2][0:1, 0:128], tmpb[:], identb[:],
                                    start=True, stop=True))
                t_row = st.n

                @block.vector
                def _(v, slot=slot, t_row=t_row):
                    v.wait_ge(st.h, t_row)
                    sv.inc(v.tensor_copy(mrowS[slot][:], P[2][0:1, 0:128]))
                v_mrow = sv.n

                if dbg and it == 0 and m == 0:
                    @block.vector
                    def _(v):
                        v.wait_ge(sv.h, v_mrow)
                        v.tensor_copy(lnds[:, 0:2], mv[:])
                        v.tensor_copy(lnds[:, 2:3], veps[:])
                        v.tensor_copy(lnds[:, 3:4], sqv[:])
                        v.tensor_copy(lnds[:, 4:5], rsig[:])
                        v.tensor_copy(lnds[:, 5:6], tmp1[:])
                        sv.inc(v.tensor_copy(mrowf[:], mrowS[0][:]))

                    @block.gpsimd
                    def _(gp):
                        gp.wait_ge(sv.h, sv.n)
                        sd.inc(gp.dma_start(out=dbg_d["lnd"], in_=lnds[:]), 16)
                        sd.inc(gp.dma_start(out=dbg_d["mrow"], in_=mrowf[:]), 16)

                for cg in range(8):
                    bank = cg % 2

                    @block.tensor
                    def _(t, cg=cg, bank=bank, slot=slot, v_mrow=v_mrow, m=m):
                        t.wait_ge(sv.h, v_mrow)
                        ring_wait(t, ("pln", bank))
                        for cs in range(4):
                            c = cg * 4 + cs
                            t.matmul(P[bank][:, cs * 128:(cs + 1) * 128],
                                     xbf[slot][:, c * 128:(c + 1) * 128],
                                     Dm[slot][:], start=(cs == 0), stop=False)
                            ii = t.matmul(P[bank][:, cs * 128:(cs + 1) * 128],
                                          onesb[0:1, :], mrowS[slot][:],
                                          start=False, stop=(cs == 3))
                        st.inc(ii)
                    t_dmm = st.n

                    @block.vector
                    def _(v, cg=cg, bank=bank, t_dmm=t_dmm, m=m):
                        v.wait_ge(st.h, t_dmm)
                        dst = Xsb.ap()[:, cg * 4:(cg + 1) * 4, m * 128:(m + 1) * 128]
                        sv.inc(v.tensor_copy(dst, P[bank][:].rearrange("p (c t) -> p c t", c=4)))
                    ring_set(("pln", bank), sv)
                ring_set(("xbf", slot), st)
            ln_ve_last = sv.n

            flat = 0
            for w in range(3):
                dstt = [Qsb, Ksb, VAsb][w]
                for ko in range(NKC):
                    bank = 3 + (ko % 2)
                    for subq in range(2):
                        fslot = flat % WF_R

                        @block.gpsimd
                        def _(gp, w=w, ko=ko, subq=subq, fslot=fslot):
                            ring_wait(gp, ("wbb", fslot))
                            s_wbb[fslot].inc(gp.dma_start(
                                out=wbb[fslot].ap().rearrange("p (a c) -> p a c", a=16),
                                in_=qkv_w_d[w][ko][:, subq * 16:(subq + 1) * 16, :]), 16)
                        w_cnt = s_wbb[fslot].n

                        @block.tensor
                        def _(t, w=w, ko=ko, subq=subq, bank=bank, fslot=fslot,
                              w_cnt=w_cnt):
                            t.wait_ge(s_wbb[fslot].h, w_cnt)
                            if w == 0 and ko == 0 and subq == 0:
                                t.wait_ge(sv.h, ln_ve_last)
                            if subq == 0:
                                ring_wait(t, ("pqkv", bank))
                            for kl in range(16):
                                ki = subq * 16 + kl
                                ii = t.matmul(P[bank][:],
                                              wbb[fslot][:, kl * 128:(kl + 1) * 128],
                                              Xsb.ap()[:, ki, :],
                                              start=(ki == 0), stop=(ki == NKC - 1))
                            st.inc(ii)
                        ring_set(("wbb", fslot), st)
                        flat += 1
                    t_proj = st.n

                    @block.vector
                    def _(v, w=w, ko=ko, bank=bank, t_proj=t_proj, dstt=dstt, it=it):
                        v.wait_ge(st.h, t_proj)
                        if it == 0 and w == 0 and ko == 0:
                            v.wait_ge(sw.h, const_w)
                        dst = dstt.ap().rearrange("p (t h) -> p t h", h=32)[:, :, ko]
                        sv.inc(v.tensor_scalar(dst, P[bank][:],
                                               biasv_s[w][:, ko:ko + 1], None, ALU.add))
                    ring_set(("pqkv", bank), sv)
            qkv_ve = sv.n

            if dbg and first_tile:
                @block.gpsimd
                def _(gp):
                    gp.wait_ge(sv.h, qkv_ve)
                    sd.inc(gp.dma_start(out=dbg_d["q"], in_=Qsb[:]), 16)
                    sd.inc(gp.dma_start(out=dbg_d["k"], in_=Ksb[:]), 16)
                    sd.inc(gp.dma_start(out=dbg_d["v"], in_=VAsb[:]), 16)
                    sd.inc(gp.dma_start(
                        out=dbg_d["xn"].rearrange("p (c t) -> p c t", c=NKC),
                        in_=Xsb[:]), 16)

            TS = [0] * NB; TZ = [0] * NB; TVT = [0] * NB; TAV = [0] * NB
            TAVT = [0] * NB
            VEind = [0] * NB; VEme = [0] * NB; VErec = [0] * NB
            VEdrec = [0] * NB; VEao = [0] * NB
            SEexp = [0] * NB; SEvq = [0] * NB; SEau = [0] * NB

            for b in range(NB):
                sp = b % 2

                @block.tensor
                def _(t, b=b, sp=sp):
                    if b >= 2:
                        t.wait_ge(ss.h, SEexp[b - 2])
                        t.wait_ge(sv.h, VEind[b - 2])
                    else:
                        t.wait_ge(sv.h, qkv_ve)
                    for q in range(4):
                        t0 = b * 16 + q * 4
                        ii = t.matmul(P[sp][:, q * 128:(q + 1) * 128],
                                      Ksb[:, t0 * 32:(t0 + 4) * 32],
                                      Qsb[:, t0 * 32:(t0 + 4) * 32],
                                      start=(q == 0), stop=(q == 3))
                    st.inc(ii)
                TS[b] = st.n

                @block.scalar
                def _(s, b=b, sp=sp):
                    s.wait_ge(st.h, TS[b])
                    if b >= 1:
                        s.wait_ge(sv.h, VEme[b - 1])
                    ss.inc(s.activation(e_t[:], P[sp][:], AF.Exp, scale=INV_SQRT_D))
                SEexp[b] = ss.n

                @block.vector
                def _(v, b=b, sp=sp, it=it):
                    v.wait_ge(st.h, TS[b])
                    if it == 0 and b == 0:
                        v.wait_ge(sw.h, const_w)
                    sv.inc(v.tensor_tensor(ind_t[:], P[sp][:], thb[:], ALU.is_gt))
                VEind[b] = sv.n

                @block.vector
                def _(v, b=b, sp=sp):
                    v.wait_ge(ss.h, SEexp[b])
                    if b >= 2:
                        v.wait_ge(st.h, TAV[b - 2])
                    sv.inc(v.tensor_tensor(me_t[sp][:], e_t[:], ind_t[:], ALU.mult))
                VEme[b] = sv.n

                if dbg and first_tile and b == 0:
                    @block.gpsimd
                    def _(gp):
                        gp.wait_ge(sv.h, VEme[0])
                        sd.inc(gp.dma_start(out=dbg_d["me"], in_=me_t[0][:]), 16)

                @block.tensor
                def _(t, b=b, sp=sp):
                    t.wait_ge(sv.h, VEme[b])
                    if b >= 2:
                        t.wait_ge(sv.h, VErec[b - 2])
                    for q in range(4):
                        ii = t.matmul(P[2 + sp][:, q:q + 1],
                                      me_t[sp][:, q * 128:(q + 1) * 128],
                                      onesb[:, 0:1], start=(q == 0), stop=(q == 3))
                    st.inc(ii)
                    if b >= 1:
                        t.wait_ge(sv.h, SEvq[b - 1])
                    for q in range(4):
                        t0 = b * 16 + q * 4
                        ii = t.matmul(P[4][:, q * 128:(q + 1) * 128],
                                      VAsb[:, t0 * 32:(t0 + 4) * 32],
                                      identb[:], start=(q == 0), stop=(q == 3))
                    st.inc(ii)
                TZ[b] = st.n - 1
                TVT[b] = st.n

                @block.vector
                def _(v, b=b, sp=sp):
                    v.wait_ge(st.h, TVT[b])
                    sv.inc(v.tensor_copy(vqs[sp][:], P[4][:]))
                SEvq[b] = sv.n

                @block.tensor
                def _(t, b=b, sp=sp):
                    t.wait_ge(sv.h, SEvq[b])
                    if b >= 2:
                        t.wait_ge(sv.h, SEau[b - 2])
                    for q in range(4):
                        ii = t.matmul(P[5 + sp][:, q * 128:(q + 1) * 128],
                                      me_t[sp][:, q * 128:(q + 1) * 128],
                                      vqs[sp][:, q * 128:(q + 1) * 128],
                                      start=(q == 0), stop=(q == 3))
                    st.inc(ii)
                TAV[b] = st.n

                @block.vector
                def _(v, b=b, sp=sp):
                    v.wait_ge(st.h, TZ[b])
                    sv.inc(v.reciprocal(rec_t[sp][:], P[2 + sp][:, 0:4]))
                VErec[b] = sv.n

                @block.vector
                def _(v, b=b, sp=sp):
                    v.wait_ge(sv.h, VErec[b])
                    if b >= 2:
                        v.wait_ge(st.h, TAVT[b - 2])
                    for q in range(4):
                        ii = v.tensor_scalar(drec[sp][:, q * 128:(q + 1) * 128],
                                             identb[:], rec_t[sp][:, q:q + 1],
                                             None, ALU.mult)
                    sv.inc(ii)
                VEdrec[b] = sv.n

                @block.vector
                def _(v, b=b, sp=sp):
                    v.wait_ge(st.h, TAV[b])
                    sv.inc(v.tensor_copy(avus[sp][:], P[5 + sp][:]))
                SEau[b] = sv.n

                @block.tensor
                def _(t, b=b, sp=sp):
                    t.wait_ge(sv.h, SEau[b])
                    t.wait_ge(sv.h, VEdrec[b])
                    if b >= 1:
                        t.wait_ge(sv.h, VEao[b - 1])
                    for q in range(4):
                        ii = t.matmul(P[7][:, q * 128:(q + 1) * 128],
                                      avus[sp][:, q * 128:(q + 1) * 128],
                                      drec[sp][:, q * 128:(q + 1) * 128],
                                      start=(q == 0), stop=(q == 3))
                    st.inc(ii)
                TAVT[b] = st.n

                @block.vector
                def _(v, b=b, sp=sp):
                    v.wait_ge(st.h, TAVT[b])
                    sv.inc(v.tensor_copy(VAsb[:, b * 512:(b + 1) * 512], P[7][:]))
                VEao[b] = sv.n

            attn_ve = sv.n
            attn_se = ss.n

            if dbg and first_tile:
                @block.gpsimd
                def _(gp):
                    gp.wait_ge(sv.h, attn_ve)
                    sd.inc(gp.dma_start(out=dbg_d["ao"], in_=VAsb[:]), 16)

            @block.gpsimd
            def _(gp):
                ring_wait(gp, ("gwr",))
                s_gw.inc(gp.dma_start(out=gwr[:], in_=gw_d), 16)
            w_gw = s_gw.n

            @block.tensor
            def _(t, w_gw=w_gw):
                t.wait_ge(s_gw.h, w_gw)
                t.wait_ge(sv.h, attn_ve)
                t.wait_ge(ss.h, attn_se)
                for ki in range(NKC):
                    for m in range(NM):
                        ii = t.matmul(P[0][:, m * NE:(m + 1) * NE],
                                      Xsb.ap()[:, ki, m * 128:(m + 1) * 128],
                                      gwr.ap()[:, ki, :],
                                      start=(ki == 0 and m == 0),
                                      stop=(ki == NKC - 1))
                st.inc(ii)
            ring_set(("gwr",), st)
            t_logits = st.n

            @block.vector
            def _(v):
                v.wait_ge(st.h, t_logits)
                sv.inc(v.tensor_copy(Lsb[:], P[0][:, 0:4 * NE]))
            v_logits = sv.n

            if dbg and it == 0:
                @block.gpsimd
                def _(gp):
                    gp.wait_ge(sv.h, v_logits)
                    sd.inc(gp.dma_start(out=dbg_d["lg"], in_=Lsb[:]), 16)
                    sd.inc(gp.dma_start(
                        out=dbg_d["xn2"].rearrange("p (c t) -> p c t", c=NKC),
                        in_=Xsb[:]), 16)
                    sd.inc(gp.dma_start(out=dbg_d["gw0"][:, 0:NE], in_=gwr.ap()[:, 30, :]), 16)
                    sd.inc(gp.dma_start(out=dbg_d["gw0"][:, NE:2*NE], in_=gwr.ap()[:, 31, :]), 16)

            s_exp_prev = [0]
            for m in range(NM):
                L = Lsb[:, m * NE:(m + 1) * NE]

                @block.vector
                def _(v, m=m, L=L):
                    if m == 0:
                        v.wait_ge(sv.h, v_logits)
                    else:
                        v.wait_ge(ss.h, s_exp_prev[0])
                    sv.inc(v.reduce_max(m1p[:], L, axis=AX.X))
                    v.wait_ge(sv.h, sv.n)
                    sv.inc(v.tensor_scalar(nm1[:], m1p[:], -1.0, None, ALU.mult))
                    v.wait_ge(sv.h, sv.n)
                    sv.inc(v.tensor_scalar(t1s[:], L, nm1[:, 0:1], None, ALU.add))
                    v.wait_ge(sv.h, sv.n)
                    sv.inc(v.tensor_single_scalar(c1s[:], t1s[:], 0.0, ALU.is_ge))
                    v.wait_ge(sv.h, sv.n)
                    sv.inc(v.scalar_tensor_tensor(maskd[:], c1s[:], BIGNEG, L,
                                                  ALU.mult, ALU.add))
                    v.wait_ge(sv.h, sv.n)
                    sv.inc(v.reduce_max(m2p[:], maskd[:], axis=AX.X))
                    v.wait_ge(sv.h, sv.n)
                    sv.inc(v.tensor_scalar(nm2[:], m2p[:], -1.0, None, ALU.mult))
                    sv.inc(v.tensor_sub(d21[:], m2p[:], m1p[:]))
                    v.wait_ge(sv.h, sv.n)
                v_m2 = sv.n

                @block.scalar
                def _(s, m=m, L=L, v_m2=v_m2):
                    s.wait_ge(sv.h, v_m2)
                    ss.inc(s.activation(els[:], L, AF.Exp, bias=nm1[:, 0:1]))
                    ss.inc(s.activation(em2[:], d21[:], AF.Exp))
                s_exp = ss.n
                s_exp_prev[0] = s_exp

                if dbg and it == 0 and m == 0:
                    @block.vector
                    def _(v):
                        v.wait_ge(ss.h, s_exp)
                        v.tensor_copy(t2ds[:, 0:1], m1p[:])
                        v.tensor_copy(t2ds[:, 1:2], m2p[:])
                        v.tensor_copy(t2ds[:, 2:3], d21[:])
                        v.tensor_copy(t2ds[:, 3:4], em2[:])
                        v.tensor_copy(t2ds[:, 4:5], els[:, 0:1])
                        sv.inc(v.tensor_copy(t2ds[:, 5:6], c1s[:, 0:1]))

                    @block.gpsimd
                    def _(gp):
                        gp.wait_ge(sv.h, sv.n)
                        sd.inc(gp.dma_start(out=dbg_d["t2d"], in_=t2ds[:]), 16)

                @block.vector
                def _(v, m=m, L=L, s_exp=s_exp):
                    v.wait_ge(ss.h, s_exp)
                    sv.inc(v.tensor_scalar(den[:], em2[:], 1.0, None, ALU.add))
                    v.wait_ge(sv.h, sv.n)
                    sv.inc(v.reciprocal(rec2[:], den[:]))
                    v.wait_ge(sv.h, sv.n)
                    sv.inc(v.tensor_scalar(c2s[:], L, nm2[:, 0:1], 0.0,
                                           ALU.add, ALU.is_ge))
                    v.wait_ge(sv.h, sv.n)
                    sv.inc(v.tensor_mul(g1s[:], c2s[:], els[:]))
                    v.wait_ge(sv.h, sv.n)
                    sv.inc(v.tensor_scalar(gf[:, m * NE:(m + 1) * NE], g1s[:],
                                           rec2[:, 0:1], None, ALU.mult))

            @block.vector
            def _(v):
                v.wait_ge(sv.h, sv.n)
                sv.inc(v.tensor_copy(gfb[:], gf[:]))
            v_gfb = sv.n

            if dbg and first_tile:
                @block.gpsimd
                def _(gp):
                    gp.wait_ge(sv.h, v_gfb)
                    sd.inc(gp.dma_start(out=dbg_d["gf"], in_=gf[:]), 16)

            @block.tensor
            def _(t):
                t.wait_ge(sv.h, v_gfb)
                for m in range(NM):
                    ii = t.matmul(P[1][0:NE, m * 128:(m + 1) * 128],
                                  gfb[:, m * NE:(m + 1) * NE], identb[:],
                                  start=(m == 0), stop=(m == NM - 1))
                st.inc(ii)
            t_gft = st.n

            @block.vector
            def _(v):
                v.wait_ge(st.h, t_gft)
                sv.inc(v.tensor_copy(gfTs[:], P[1][0:NE, :]))
            v_gft = sv.n

            @block.tensor
            def _(t):
                t.wait_ge(sv.h, v_gft)
                st.inc(t.matmul(P[2][:], e2b[:, 0:128], gfTs[:], start=True, stop=True))
                st.inc(t.matmul(P[3][0:96, :], e2b[:, 128:224], gfTs[:],
                                start=True, stop=True))
            t_ge = st.n

            @block.vector
            def _(v):
                v.wait_ge(st.h, t_ge)
                sv.inc(v.tensor_copy(ge0[:], P[2][:]))
                sv.inc(v.tensor_copy(ge1[:], P[3][0:96, :]))
            v_ge = sv.n

            for ki in range(NKC):
                aslot = ki % 6

                @block.gpsimd
                def _(gp, ki=ki, aslot=aslot):
                    ring_wait(gp, ("a2b", aslot))
                    s_a2[aslot].inc(gp.dma_start(out=a2b[aslot][:], in_=a2_d[ki]), 16)
                s_cnt = s_a2[aslot].n

                @block.tensor
                def _(t, ki=ki, aslot=aslot, s_cnt=s_cnt):
                    t.wait_ge(s_a2[aslot].h, s_cnt)
                    if ki == 0:
                        t.wait_ge(sv.h, attn_ve)
                        t.wait_ge(ss.h, attn_se)
                    t.matmul(P[4][:], a2b[aslot][:, 0:128], Xsb.ap()[:, ki, :],
                             start=(ki == 0), stop=(ki == NKC - 1))
                    ii = t.matmul(P[5][0:96, :], a2b[aslot][:, 128:224],
                                  Xsb.ap()[:, ki, :],
                                  start=(ki == 0), stop=(ki == NKC - 1))
                    st.inc(ii)
                ring_set(("a2b", aslot), st)
            t_h = st.n

            @block.vector
            def _(v):
                v.wait_ge(st.h, t_h)
                v.wait_ge(sv.h, v_ge)
                sv.inc(v.tensor_tensor(hs0[:], P[4][:], ge0[:], ALU.mult))
                sv.inc(v.tensor_tensor(hs1[:], P[5][0:96, :], ge1[:], ALU.mult))
            v_hs = sv.n

            if dbg and first_tile:
                @block.gpsimd
                def _(gp):
                    gp.wait_ge(sv.h, v_hs)
                    sd.inc(gp.dma_start(out=dbg_d["hs0"], in_=hs0[:]), 16)
                    sd.inc(gp.dma_start(out=dbg_d["hs1"], in_=hs1[:]), 16)

            ev_done = [0] * NOG
            oflat = 0
            for n in range(NOG):
                par = 4 * (n % 2)
                bslot_bo = n % 2

                x_bo = None
                s_b2n = None

                @block.sync
                def _(sy, n=n):
                    for m in range(NM):
                        sy.wait_ge(s_ost[m].h, s_ost[m].n)
                        sw.inc(sy.dma_start(
                            out=xr[m][:],
                            in_=x_d[t_base + m * 128: t_base + (m + 1) * 128,
                                    n * 512:(n + 1) * 512]), 16)
                w_xr = sw.n

                for kk in range(19):
                    if kk == 6:
                        @block.gpsimd
                        def _(gp, n=n, bslot_bo=bslot_bo):
                            ring_wait(gp, ("bo", bslot_bo))
                            s_bo[bslot_bo].inc(gp.dma_start(
                                out=bo_bf[bslot_bo][:],
                                in_=bo_d[:, n * 512:(n + 1) * 512]), 16)
                        x_bo = s_bo[bslot_bo].n

                        @block.gpsimd
                        def _(gp, n=n):
                            ring_wait(gp, ("b2b",))
                            s_b2.inc(gp.dma_start(out=b2b0[:], in_=b20_d[n]), 16)
                            s_b2.inc(gp.dma_start(out=b2b1[:], in_=b21_d[n]), 16)
                        s_b2n = s_b2.n
                    if kk < 8:
                        src_d, kind, base_k = wo_d[n][:, kk * 4:kk * 4 + 4, :], "w", kk * 4
                    elif kk == 8:
                        kind = "bo"
                    elif kk < 17:
                        c = kk - 9
                        src_d, kind, base_k = wb_d[n][:, c * 4:c * 4 + 4, :], "w", 33 + c * 4
                    elif kk == 17:
                        kind = "b20"
                    else:
                        kind = "b21"

                    if kind == "w":
                        bslot = oflat % OB_R

                        @block.gpsimd
                        def _(gp, src_d=src_d, bslot=bslot):
                            ring_wait(gp, ("obf", bslot))
                            s_obf[bslot].inc(gp.dma_start(
                                out=obf[bslot][:], in_=src_d), 16)
                        rdy = (s_obf[bslot], s_obf[bslot].n)

                        @block.tensor
                        def _(t, kk=kk, n=n, par=par, bslot=bslot, rdy=rdy,
                              base_k=base_k):
                            t.wait_ge(rdy[0].h, rdy[1])
                            if kk == 0:
                                if n <= 1:
                                    t.wait_ge(sv.h, v_hs)
                                else:
                                    t.wait_ge(sv.h, ev_done[n - 2])
                            for kl in range(4):
                                kabs = base_k + kl
                                for m in range(NM):
                                    if kabs < 32:
                                        lhsT = VAsb.ap().rearrange(
                                            "p (t h) -> p t h",
                                            h=32)[:, m * 128:(m + 1) * 128, kabs]
                                    else:
                                        lhsT = Xsb.ap()[:, kabs - 33, m * 128:(m + 1) * 128]
                                    ii = t.matmul(P[par + m][:], lhsT,
                                                  obf[bslot].ap()[:, kl, :],
                                                  start=(kabs == 0), stop=False)
                            st.inc(ii)
                        ring_set(("obf", bslot), st)
                        oflat += 1
                    elif kind == "bo":
                        @block.tensor
                        def _(t, n=n, par=par, x_bo=x_bo, bslot_bo=bslot_bo):
                            t.wait_ge(s_bo[bslot_bo].h, x_bo)
                            for m in range(NM):
                                t.matmul(P[par + m][:], onesb[0:1, 0:128],
                                         bo_bf[bslot_bo][:], start=False, stop=False)
                    elif kind == "b20":
                        @block.tensor
                        def _(t, par=par, s_b2n=s_b2n):
                            t.wait_ge(s_b2.h, s_b2n)
                            for m in range(NM):
                                t.matmul(P[par + m][:], hs0[:, m * 128:(m + 1) * 128],
                                         b2b0[:], start=False, stop=False)
                    else:
                        @block.tensor
                        def _(t, par=par):
                            for m in range(NM):
                                ii = t.matmul(P[par + m][:],
                                              hs1[:, m * 128:(m + 1) * 128],
                                              b2b1[:], start=False, stop=True)
                            st.inc(ii)
                t_out_n = st.n
                ring_set(("b2b",), st)
                ring_set(("bo", bslot_bo), st)

                @block.vector
                def _(v, n=n, par=par, t_out_n=t_out_n, w_xr=w_xr):
                    v.wait_ge(st.h, t_out_n)
                    v.wait_ge(sw.h, w_xr)
                    for m in range(NM):
                        sv.inc(v.tensor_add(xr[m][:], P[par + m][:], xr[m][:]))
                        ring_set(("xr", m), sv)
                ev_done[n] = sv.n

                @block.scalar
                def _(s, n=n, t_base=t_base):
                    s.wait_ge(sv.h, ev_done[n])
                    for m in range(NM):
                        s_ost[m].inc(s.dma_start(
                            out=y_d[t_base + m * 128: t_base + (m + 1) * 128,
                                    n * 512:(n + 1) * 512],
                            in_=xr[m][:]), 16)

            ring_set(("tile_ve",), sv)
            ring_set(("tile_se",), ss)

        @block.gpsimd
        def _(gp):
            for m in range(4):
                gp.wait_ge(s_ost[m].h, s_ost[m].n)
            if dbg:
                gp.wait_ge(sd.h, sd.n)

    return nc


def prep_inputs(inputs):
    Wq = np.asarray(inputs["L0_attn_Wq"]); Wk = np.asarray(inputs["L0_attn_Wk"])
    Wv = np.asarray(inputs["L0_attn_Wv"]); Wo = np.asarray(inputs["L0_attn_Wo"])
    baseW = np.asarray(inputs["L0_ffn_base_W"])
    gateW = np.asarray(inputs["L0_ffn_gate_W"])
    loraA = np.asarray(inputs["L0_ffn_lora_A"]); loraB = np.asarray(inputs["L0_ffn_lora_B"])

    def qkv_tiles(W):
        WT = np.ascontiguousarray(W.T)
        return np.ascontiguousarray(WT.reshape(NKC, 128, NKC, 128).transpose(2, 1, 0, 3))

    def out_tiles(W):
        WT = np.ascontiguousarray(W.T)
        return np.ascontiguousarray(WT.reshape(NKC, 128, NOG, 512).transpose(2, 1, 0, 3))

    A2 = loraA.transpose(1, 0, 2).reshape(224, HID)
    a2t = np.ascontiguousarray(A2.T.reshape(NKC, 128, 224))
    B2 = loraB.transpose(2, 0, 1).reshape(224, HID)
    b20 = np.ascontiguousarray(B2[:128].reshape(128, NOG, 512).transpose(1, 0, 2))
    b21 = np.ascontiguousarray(B2[128:].reshape(96, NOG, 512).transpose(1, 0, 2))
    gw = np.ascontiguousarray(gateW.T.reshape(NKC, 128, NE).transpose(1, 0, 2))

    def bias_cols(b):
        return np.ascontiguousarray(np.asarray(b).reshape(NKC, 128).T)

    e2 = np.zeros((NE, 224), dtype=BF)
    for r in range(RK):
        for e in range(NE):
            e2[e, r * NE + e] = BF(2.0)
    ident = np.eye(128, dtype=BF)
    onesc = np.ones((128, 128), dtype=BF)
    pat = np.full((128, 128), 1.0e30, dtype=BF)
    for tt in range(4):
        pat[tt * 32:(tt + 1) * 32, tt * 32:(tt + 1) * 32] = BF(TH_RAW)
    thb = np.ascontiguousarray(np.tile(pat, (1, 4)))

    return {
        "wq": qkv_tiles(Wq), "wk": qkv_tiles(Wk), "wv": qkv_tiles(Wv),
        "wo": out_tiles(Wo), "wb": out_tiles(baseW),
        "a2": a2t, "b20": b20, "b21": b21, "gw": gw,
        "bq": bias_cols(inputs["L0_attn_bq"]),
        "bk": bias_cols(inputs["L0_attn_bk"]),
        "bv": bias_cols(inputs["L0_attn_bv"]),
        "bo": np.ascontiguousarray(np.asarray(inputs["L0_attn_bo"])[None, :]),
        "idq": ident, "onesc": onesc, "e2": e2, "thb": thb,
    }


def kernel(**inputs):
    ncores = 8
    ntok = 2048
    T = 512
    x_full = np.ascontiguousarray(
        np.asarray(inputs["L0_attn_x"], dtype=np.float32).reshape(16384, HID))
    wmap = prep_inputs(inputs)
    nc = build_program(ntok, T, dbg=False)
    in_maps = []
    for c in range(ncores):
        m = dict(wmap)
        m["x"] = x_full[c * ntok:(c + 1) * ntok]
        in_maps.append(m)
    res = run_bass_kernel_spmd(nc, in_maps, core_ids=list(range(ncores)))
    out = np.concatenate([np.asarray(r["y"]) for r in res.results], axis=0)
    return out.reshape(4, 4096, HID)
